# version 06
# baseline (speedup 1.0000x reference)
import math
import numpy as np
import ml_dtypes
import concourse.bass as bass
import concourse.mybir as mybir
from concourse.bass_utils import run_bass_kernel_spmd

F32 = mybir.dt.float32
BF16 = mybir.dt.bfloat16
I32 = mybir.dt.int32
ALU = mybir.AluOpType
ACT = mybir.ActivationFunctionType
AX = mybir.AxisListType.X

S, D, H, KVH, HD = 2048, 1024, 16, 4, 64
T = S // 128
C8 = D // 128
DI = 2816
C22 = DI // 128
NCORE = 8
LN_EPS = 1e-5
RMS_EPS = 1e-6
ROPE_BASE = 10000.0
NEG = -1.0e30
PI = float(np.float32(math.pi))
TWO_PI = 2.0 * math.pi
PI_CLAMP = 3.1415925
TP_HI = float(np.float32(TWO_PI))
TP_LO = TWO_PI - TP_HI
_ES = -math.log(ROPE_BASE) / 32.0
ES_HI = float(np.float32(_ES))
ES_LO = _ES - ES_HI
_C6 = 1.0 / 6.0
C6_HI = float(np.float32(_C6))
C6_LO = _C6 - C6_HI
_PI2 = math.pi / 2.0
PI2_HI = float(np.float32(_PI2))
PI2_LO = _PI2 - PI2_HI

_DEBUG = False

ENGS = ("T", "V", "S", "G", "DL", "DS")


class Script:
    def __init__(self):
        self.ops = {e: [] for e in ENGS}
        self.gidx = 0

    def op(self, eng, fn, after=(), dma=False):
        deps = []
        for t in after:
            if t is None:
                continue
            if t["eng"] == eng and not t["dma"]:
                continue
            t["inc"] = True
            deps.append(t)
        rec = {"eng": eng, "fn": fn, "after": deps, "dma": dma,
               "inc": dma, "val": None, "gidx": self.gidx}
        self.gidx += 1
        self.ops[eng].append(rec)
        return rec

    def cut(self, maxg):
        kept = set()
        for e in ENGS:
            self.ops[e] = [r for r in self.ops[e] if r["gidx"] < maxg]
            for r in self.ops[e]:
                kept.add(id(r))
        for e in ENGS:
            for r in self.ops[e]:
                r["after"] = [t for t in r["after"] if id(t) in kept]

    def finalize(self):
        for e in ("V", "S", "G"):
            for rec in self.ops[e]:
                rec["inc"] = True
        for e in ENGS:
            run = 0
            for rec in self.ops[e]:
                if rec["inc"]:
                    run += 16 if rec["dma"] else 1
                rec["val"] = run

    def replay_merged(self, eng, enames, sems):
        ops = sorted([r for e in enames for r in self.ops[e]],
                     key=lambda r: r["gidx"])
        self._replay_list(eng, ops, sems)

    def replay(self, eng, ename, sems):
        self._replay_list(eng, self.ops[ename], sems)

    def _replay_list(self, eng, ops, sems):
        selfserial = ops and ops[0]["eng"] in ("V", "S", "G")
        nprev = 0
        for rec in ops:
            seen = {}
            for t in rec["after"]:
                seen[t["eng"]] = max(seen.get(t["eng"], 0), t["val"])
            if selfserial and nprev > 0:
                seen[rec["eng"]] = max(seen.get(rec["eng"], 0), nprev)
            for key, v in seen.items():
                if v > 0:
                    eng.wait_ge(sems[key], v)
            instr = rec["fn"](eng)
            if rec["inc"]:
                instr.then_inc(sems[rec["eng"]], 16 if rec["dma"] else 1)
            if selfserial:
                nprev = rec["val"]


_CACHE = {}


def _mask_host():
    i = np.arange(128)[:, None]
    c = np.arange(384)[None, :]
    valid = (c > i) & (c <= i + 256)
    return np.where(valid, 0.0, NEG).astype(np.float32)


def build(phase="full"):
    import os as _os
    KCUT = int(_os.environ.get("KCUT", "0"))
    key = (phase, KCUT)
    if key in _CACHE:
        return _CACHE[key]
    nc = bass.Bass("TRN2", target_bir_lowering=False,
                   detect_race_conditions=False)
    sc = Script()

    def din(name, shape, dt=F32):
        return nc.dram_tensor(name, shape, dt, kind="ExternalInput").ap()

    x_d = din("x", [S, D])
    wqT_d = din("wqT", [D, D])
    wkT_d = din("wkT", [D, KVH * HD])
    wvT_d = din("wvT", [D, KVH * HD])
    woT_d = din("woT", [D, D])
    woTsw_d = din("woTsw", [D, D])
    wmod_d = din("wmod", [D, 6 * D])
    wg_d = din("wg", [DI, D])
    wu_d = din("wu", [DI, D])
    wd_d = din("wd", [D, DI])
    crow_d = din("crow", [1, D])
    gain_d = din("gain", [1, H])
    pm_d = din("pm", [1, 32])
    maskb_d = din("maskb", [128, 384])
    ident_d = din("ident", [128, 128], BF16)
    out_d = nc.dram_tensor("out", [S, D], F32, kind="ExternalOutput").ap()
    xprime_d = nc.dram_tensor("xprime", [S, D], F32).ap()
    dbg = {}
    if _DEBUG:
        for nm, shp in [("dbg_mod", [1, 6 * D]), ("dbg_h1T", [128, C8 * 128]),
                        ("dbg_q", [128, D]), ("dbg_k", [128, 256]),
                        ("dbg_y", [128, C8 * 128]), ("dbg_hq", [128, D]),
                        ("dbg_xp", [128, D])]:
            dbg[nm] = nc.dram_tensor(nm, shp, F32, kind="ExternalOutput").ap()

    import contextlib
    stack = contextlib.ExitStack()

    ident = stack.enter_context(nc.sbuf_tensor("ident_s", [128, 128], BF16))
    gbb = stack.enter_context(nc.sbuf_tensor("gbb", [128, D], F32))
    hqTb = stack.enter_context(nc.sbuf_tensor("hqTb", [128, C8 * S], BF16))

    ps_a = stack.enter_context(nc.psum_tensor("ps_a", [128, 1536], F32))
    ps_b = stack.enter_context(nc.psum_tensor("ps_b", [128, 1536], F32))
    ps_t = stack.enter_context(nc.psum_tensor("ps_t", [128, 1024], BF16))
    ps_c = stack.enter_context(nc.psum_tensor("ps_c", [128, 512], F32))

    ld_ident = sc.op("DL", lambda e: e.dma_start(out=ident[:], in_=ident_d),
                     dma=True)

    outer = contextlib.ExitStack()

    def sbo(name, shape, dt=F32):
        name = "sbo_" + name
        return outer.enter_context(nc.sbuf_tensor(name, shape, dt))

    maskb = sbo("maskb", [128, 384])
    sab = sbo("sab", [128, D])
    ca1b = sbo("ca1b", [128, D])
    gab = sbo("gab", [128, D])
    cb1b = sbo("cb1b", [128, D])
    sbb = sbo("sbb", [128, D])
    gainrep = sbo("gainrep", [128, H])
    pmA = sbo("pmA", [128, C8])
    pmB = sbo("pmB", [128, C8])
    cos_all = sbo("cos_all", [128, T * 32])
    sin_all = sbo("sin_all", [128, T * 32])
    x_ring = sbo("x_ring", [128, 3 * D])

    def x_t(t):
        return x_ring[:, D * (t % 3):D * (t % 3 + 1)]

    ld_mask = sc.op("DL", lambda e: e.dma_start(out=maskb[:], in_=maskb_d),
                    dma=True)
    ld_x = []
    for t in range(3):
        ld_x.append(sc.op("DL", lambda e, t=t: e.dma_start(
            out=x_t(t), in_=x_d[128 * t:128 * (t + 1), :]), dma=True))

    setup = contextlib.ExitStack()

    def sb0(name, shape, dt=F32):
        name = "sb0_" + name
        return setup.enter_context(nc.sbuf_tensor(name, shape, dt))

    modrow = sb0("modrow", [1, 6 * D])
    c_col = sb0("c_col", [128, C8])
    silu_cb = sb0("silu_cb", [128, C8], BF16)
    ones_f = sb0("ones_f", [1, 128])
    stage = sb0("stage", [128, 512])
    stage_b = sb0("stage_b", [128, 512], BF16)
    garow = sb0("garow", [1, 2 * D])
    carow = sb0("carow", [1, 2 * D])
    tmp1 = sb0("tmp1", [1, 2 * D])
    tmp2 = sb0("tmp2", [1, 2 * D])
    grow = sb0("grow", [1, H])
    pmrow = sb0("pmrow", [1, 32])
    tvalsf = sb0("tvalsf", [128, T])
    io_t = sb0("io_t", [128, T], I32)
    io_f = sb0("io_f", [1, 32], I32)
    invf = sb0("invf", [1, 32])
    invfb = sb0("invfb", [128, 32])
    theta = sb0("theta", [128, T * 32])
    ktmp = sb0("ktmp", [128, T * 32], I32)
    redt = sb0("redt", [128, T * 32])
    bndt = sb0("bndt", [128, T * 32])

    ld_c = sc.op("DL", lambda e: e.dma_start(
        out=c_col[:], in_=crow_d.rearrange("o (j p) -> (o p) j", p=128),
        allow_slow_non_contiguous=True), dma=True)
    ld_g = sc.op("DL", lambda e: e.dma_start(out=grow[:], in_=gain_d),
                 dma=True)
    ld_pm = sc.op("DL", lambda e: e.dma_start(out=pmrow[:], in_=pm_d),
                  dma=True)
    ones1 = sc.op("V", lambda e: e.memset(ones_f[:], 1.0))

    sl = sc.op("S", lambda e: e.activation(out=c_col[:], in_=c_col[:],
                                           func=ACT.Silu), after=[ld_c])
    slb = sc.op("V", lambda e: e.tensor_copy(out=silu_cb[:], in_=c_col[:]),
                after=[sl])

    ev_mod = []
    prev_ev = None
    for j in range(12):
        mm_prev = None
        for k in range(C8):
            dep = ([mm_prev] if mm_prev is not None
                   else ([prev_ev] if prev_ev else []))
            ldst = sc.op("DL", lambda e, k=k, j=j: e.dma_start(
                out=stage[:],
                in_=wmod_d[128 * k:128 * (k + 1), 512 * j:512 * (j + 1)]),
                dma=True, after=dep)
            cst = sc.op("V", lambda e: e.tensor_copy(
                out=stage_b[:], in_=stage[:]), after=[ldst])
            mm_prev = sc.op("T", lambda e, k=k: e.matmul(
                ps_c[0:1, 0:512], silu_cb[:, k:k + 1], stage_b[:],
                start=(k == 0), stop=(k == C8 - 1)),
                after=[cst, slb] + ([prev_ev] if prev_ev else []))
        prev_ev = sc.op("S", lambda e, j=j: e.copy(
            out=modrow[0:1, 512 * j:512 * (j + 1)], in_=ps_c[0:1, 0:512]),
            after=[mm_prev])
        ev_mod.append(prev_ev)
    mA = list(ev_mod)

    g1 = sc.op("V", lambda e: e.tensor_copy(
        out=tmp1[0:1, 0:D], in_=modrow[0:1, 2 * D:3 * D]), after=mA)
    g2 = sc.op("V", lambda e: e.tensor_copy(
        out=tmp1[0:1, D:2 * D], in_=modrow[0:1, 5 * D:6 * D]), after=mA)
    sg1 = sc.op("V", lambda e: e.tensor_scalar(
        out=tmp2[:], in0=tmp1[:], scalar1=0.0, scalar2=None, op0=ALU.is_ge),
        after=[g1, g2])
    sg2 = sc.op("V", lambda e: e.tensor_scalar(
        out=tmp2[:], in0=tmp2[:], scalar1=2.0, scalar2=-1.0, op0=ALU.mult,
        op1=ALU.add), after=[sg1])
    mg1 = sc.op("S", lambda e: e.activation(out=tmp1[:], in_=tmp1[:],
                                            func=ACT.Abs),
                after=[g1, g2, sg1])
    mg2 = sc.op("V", lambda e: e.tensor_scalar(
        out=tmp1[:], in0=tmp1[:], scalar1=0.25, scalar2=None, op0=ALU.max),
        after=[mg1])
    gprod = sc.op("V", lambda e: e.tensor_tensor(
        out=garow[:], in0=tmp2[:], in1=tmp1[:], op=ALU.mult),
        after=[mg2, sg2])
    cc1 = sc.op("V", lambda e: e.tensor_copy(
        out=carow[0:1, 0:D], in_=modrow[0:1, D:2 * D]), after=mA)
    cc2 = sc.op("V", lambda e: e.tensor_copy(
        out=carow[0:1, D:2 * D], in_=modrow[0:1, 4 * D:5 * D]), after=mA)
    ca_p1 = sc.op("V", lambda e: e.tensor_scalar(
        out=carow[:], in0=carow[:], scalar1=1.0, scalar2=None, op0=ALU.add),
        after=[cc1, cc2])



    dbg_mod_tok = None
    if _DEBUG:
        dbg_mod_tok = sc.op("DS", lambda e: e.dma_start(
            out=dbg["dbg_mod"], in_=modrow[:]), dma=True, after=mA)

    psc_last = mA[-1]
    mod_ready = []

    def bcast(dst, row_ap, dep):
        nonlocal psc_last
        toks = []
        for hh in range(2):
            mm = sc.op("T", lambda e, hh=hh: e.matmul(
                ps_c[:, 0:512], ones_f[:], row_ap[0:1, 512 * hh:512 * (hh + 1)],
                start=True, stop=True), after=dep + [ones1, psc_last])
            cp = sc.op("V", lambda e, hh=hh: e.tensor_copy(
                out=dst[:, 512 * hh:512 * (hh + 1)], in_=ps_c[:, 0:512]),
                after=[mm])
            psc_last = cp
            toks.append(cp)
        return toks

    bc_gb = bcast(gbb, garow[0:1, D:2 * D], [gprod])
    mod_ready += bc_gb
    mod_ready += bcast(gab, garow[0:1, 0:D], [gprod])
    mod_ready += bcast(cb1b, carow[0:1, D:2 * D], [ca_p1])
    mod_ready += bcast(sbb, modrow[0:1, 3 * D:4 * D], mA)
    mod_ready += bcast(ca1b, carow[0:1, 0:D], [ca_p1])
    mod_ready += bcast(sab, modrow[0:1, 0:D], mA)

    mmg = sc.op("T", lambda e: e.matmul(
        ps_c[:, 0:H], ones_f[:], grow[:], start=True, stop=True),
        after=[ld_g, ones1, psc_last])
    grep = sc.op("V", lambda e: e.tensor_scalar(
        out=gainrep[:], in0=ps_c[:, 0:H], scalar1=0.125, scalar2=None,
        op0=ALU.mult), after=[mmg])
    psc_last = grep
    mod_ready.append(grep)

    for dst, etop, ebot in ((pmA, 0, 3), (pmB, 2, 1)):
        mt = sc.op("T", lambda e, etop=etop: e.matmul(
            ps_c[0:64, 0:C8], ones_f[0:1, 0:64], pmrow[0:1, etop:etop + 29:4],
            start=True, stop=True), after=[ld_pm, ones1, psc_last])
        mb = sc.op("T", lambda e, ebot=ebot: e.matmul(
            ps_c[64:128, 0:C8], ones_f[0:1, 0:64], pmrow[0:1, ebot:ebot + 29:4],
            start=True, stop=True, tile_position=(0, 64)), after=[mt])
        cp = sc.op("V", lambda e, dst=dst: e.tensor_copy(
            out=dst[:], in_=ps_c[:, 0:C8]), after=[mb])
        psc_last = cp
        mod_ready.append(cp)

    iot = sc.op("G", lambda e: e.iota(io_t[:], pattern=[[128, T]], base=0,
                                      channel_multiplier=1))
    iof = sc.op("G", lambda e: e.iota(io_f[:], pattern=[[1, 32]], base=0,
                                      channel_multiplier=0), after=[iot])
    tvf = sc.op("V", lambda e: e.tensor_copy(out=tvalsf[:], in_=io_t[:]),
                after=[iot])
    invf_f = sc.op("V", lambda e: e.tensor_copy(out=invf[:], in_=io_f[:]),
                   after=[iof])
    inve1 = sc.op("S", lambda e: e.activation(
        out=grow[0:1, 0:H] if False else invfb[0:1, 0:32], in_=invf[:],
        func=ACT.Exp, scale=ES_LO), after=[invf_f])
    inve2 = sc.op("S", lambda e: e.activation(
        out=invf[:], in_=invf[:], func=ACT.Exp, scale=ES_HI), after=[inve1])
    inve = sc.op("V", lambda e: e.tensor_tensor(
        out=invf[:], in0=invf[:], in1=invfb[0:1, 0:32], op=ALU.mult),
        after=[inve2])
    mmi = sc.op("T", lambda e: e.matmul(
        ps_c[:, 0:32], ones_f[:], invf[:], start=True, stop=True),
        after=[inve, ones1, psc_last])
    invb = sc.op("V", lambda e: e.tensor_copy(out=invfb[:], in_=ps_c[:, 0:32]),
                 after=[mmi])
    th_toks = []
    for t in range(T):
        th_toks.append(sc.op("V", lambda e, t=t: e.tensor_scalar(
            out=theta[:, 32 * t:32 * (t + 1)], in0=invfb[:],
            scalar1=tvalsf[:, t:t + 1], scalar2=None, op0=ALU.mult),
            after=[invb, tvf]))

    def build_trig(dst, shift_hi, shift_lo, deps):
        t0 = sc.op("V", lambda e, hi=shift_hi, lo=shift_lo: e.tensor_scalar(
            out=redt[:], in0=theta[:], scalar1=hi, scalar2=lo,
            op0=ALU.add, op1=ALU.add), after=deps)
        t1 = sc.op("V", lambda e: e.tensor_scalar(
            out=ktmp[:], in0=redt[:], scalar1=1.0 / TWO_PI, scalar2=None,
            op0=ALU.mult), after=[t0])
        t1b = sc.op("V", lambda e: e.tensor_copy(out=bndt[:], in_=ktmp[:]),
                    after=[t1])
        t2 = sc.op("V", lambda e: e.scalar_tensor_tensor(
            out=redt[:], in0=bndt[:], scalar=-TP_HI, in1=redt[:],
            op0=ALU.mult, op1=ALU.add), after=[t1b])
        t2b = sc.op("V", lambda e: e.scalar_tensor_tensor(
            out=redt[:], in0=bndt[:], scalar=-TP_LO, in1=redt[:],
            op0=ALU.mult, op1=ALU.add), after=[t2])
        t3 = sc.op("V", lambda e: e.tensor_scalar(
            out=bndt[:], in0=redt[:], scalar1=PI, scalar2=None, op0=ALU.is_gt),
            after=[t2b])
        t4 = sc.op("V", lambda e: e.scalar_tensor_tensor(
            out=redt[:], in0=bndt[:], scalar=-TP_HI, in1=redt[:],
            op0=ALU.mult, op1=ALU.add), after=[t3])
        t4b = sc.op("V", lambda e: e.scalar_tensor_tensor(
            out=redt[:], in0=bndt[:], scalar=-TP_LO, in1=redt[:],
            op0=ALU.mult, op1=ALU.add), after=[t4])
        t5 = sc.op("V", lambda e: e.tensor_scalar(
            out=redt[:], in0=redt[:], scalar1=-PI_CLAMP, scalar2=PI_CLAMP,
            op0=ALU.max, op1=ALU.min), after=[t4b])
        return sc.op("S", lambda e, dst=dst: e.activation(
            out=dst[:], in_=redt[:], func=ACT.Sin), after=[t5])

    trig_sin = build_trig(sin_all, 0.0, 0.0, th_toks)
    trig_cos = build_trig(cos_all, PI2_HI, PI2_LO, th_toks + [trig_sin])
    trig_ready = [trig_sin, trig_cos]
    first_gate = mod_ready + trig_ready + ([dbg_mod_tok] if dbg_mod_tok else [])

    setup.close()

    attn = contextlib.ExitStack()

    def sba(name, shape, dt=F32):
        name = "sba_" + name
        return attn.enter_context(nc.sbuf_tensor(name, shape, dt))

    wqb = sba("wqb", [128, C8 * D], BF16)
    wkb = sba("wkb", [128, C8 * 256], BF16)
    wvb = sba("wvb", [128, C8 * 256], BF16)
    wob = sba("wob", [128, C8 * D], BF16)
    kTr = sba("kTr", [128, KVH * 4 * 128], BF16)
    vtr = sba("vtr", [128, 4 * 256], BF16)
    stg = sba("stg", [128, 1024])
    xn = sba("xn", [128, D], BF16)
    h1T = sba("h1T", [128, C8 * 128], BF16)
    tmq = sba("tmq", [128, D])
    tmq2 = sba("tmq2", [128, D])
    qrope = sba("qrope", [128, D], BF16)
    krope2 = sba("krope2", [128, 512], BF16)
    qT = sba("qT", [128, C8 * 128], BF16)
    red16 = sba("red16", [128, 16])
    red16b = sba("red16b", [128, 16])
    red16c = sba("red16c", [128, 16])
    ln_sc = sba("ln_sc", [128, 8])
    msS = sba("msS", [128, 384])
    Pb = sba("Pb", [128, 384], BF16)
    PTs = sba("PTs", [128, 384], BF16)
    rsum = sba("rsum", [128, 1])
    mrow = sba("mrow", [128, 1])
    yT = sba("yT", [128, C8 * 128], BF16)
    h2f = sba("h2f", [128, D])
    xnf = sba("xnf", [128, D])
    Pf = sba("Pf", [128, 384])
    qb0 = sba("qb0", [128, D], BF16)
    qb1 = sba("qb1", [128, D], BF16)
    sgq = sba("sgq", [128, D], BF16)
    m64 = sba("m64", [128, 64])
    s64 = sba("s64", [128, 64])
    r64 = sba("r64", [128, 64])

    wtoks = []

    def loadcast(dst, src_d, ncol):
        toks = []
        for j in range(C8):
            dep = ([toks[-1]] if toks else
                   ([wtoks[-1]] if wtoks else list(first_gate)))
            l = sc.op("DL", lambda e, j=j, ncol=ncol: e.dma_start(
                out=stg[:, 0:ncol], in_=src_d[128 * j:128 * (j + 1), :]),
                dma=True, after=dep)
            cvt = sc.op("V", lambda e, j=j, ncol=ncol: e.tensor_copy(
                out=dst[:, ncol * j:ncol * (j + 1)], in_=stg[:, 0:ncol]),
                after=[l])
            toks.append(cvt)
        return toks

    wtoks += loadcast(wqb, wqT_d, D)
    wtoks += loadcast(wkb, wkT_d, 256)
    wtoks += loadcast(wvb, wvT_d, 256)
    for j in range(C8):
        l1 = sc.op("DL", lambda e, j=j: e.dma_start(
            out=stg[:, 0:D], in_=woT_d[128 * j:128 * (j + 1), :]), dma=True,
            after=[wtoks[-1]] if j == 0 else [wtoks[-1]])
        l2 = sc.op("DL", lambda e, j=j: e.dma_start(
            out=tmq[:], in_=woTsw_d[128 * j:128 * (j + 1), :]), dma=True,
            after=[l1] + (list(first_gate) if j == 0 else []))
        t1 = sc.op("V", lambda e, j=j: e.tensor_scalar(
            out=tmq[:], in0=tmq[:], scalar1=pmB[:, j:j + 1], scalar2=None,
            op0=ALU.mult), after=[l2] + mod_ready)
        t2 = sc.op("V", lambda e, j=j: e.scalar_tensor_tensor(
            out=wob[:, D * j:D * (j + 1)], in0=stg[:, 0:D],
            scalar=pmA[:, j:j + 1], in1=tmq[:], op0=ALU.mult, op1=ALU.add),
            after=[t1])
        wtoks.append(t2)

    def cview(buf, j, n):
        return buf[:].rearrange("p (j n) -> p j n", n=n)[:, j, :]

    def hqT_v(j, t):
        return hqTb[:].rearrange("p (j s) -> p j s", j=C8)[:, j, 128 * t:128 * (t + 1)]

    def emit_ln(xsrc, dst, col, dep):
        a = ln_sc[:, col:col + 1]
        b = ln_sc[:, col + 1:col + 2]
        c_ = ln_sc[:, col + 2:col + 3]
        d_ = ln_sc[:, col + 3:col + 4]
        t1 = sc.op("V", lambda e: e.tensor_reduce(
            out=a, in_=xsrc, axis=AX, op=ALU.add), after=dep)
        t2 = sc.op("S", lambda e: e.activation(
            out=stg[:, 0:D], in_=xsrc, func=ACT.Square), after=dep)
        t3 = sc.op("V", lambda e: e.tensor_reduce(
            out=b, in_=stg[:, 0:D], axis=AX, op=ALU.add), after=[t2])
        t4 = sc.op("V", lambda e: e.tensor_scalar(
            out=a, in0=a, scalar1=1.0 / D, scalar2=None, op0=ALU.mult),
            after=[t1])
        t5 = sc.op("V", lambda e: e.tensor_scalar(
            out=b, in0=b, scalar1=1.0 / D, scalar2=None, op0=ALU.mult),
            after=[t3])
        t6 = sc.op("V", lambda e: e.tensor_tensor(
            out=c_, in0=a, in1=a, op=ALU.mult), after=[t4])
        t7 = sc.op("V", lambda e: e.scalar_tensor_tensor(
            out=b, in0=c_, scalar=-1.0, in1=b, op0=ALU.mult, op1=ALU.add),
            after=[t5, t6])
        t8 = sc.op("V", lambda e: e.tensor_scalar(
            out=b, in0=b, scalar1=LN_EPS, scalar2=None, op0=ALU.add),
            after=[t7])
        t9 = sc.op("S", lambda e: e.activation(
            out=c_, in_=b, func=ACT.Sqrt), after=[t8])
        tok = sc.op("V", lambda e: e.reciprocal(out=c_, in_=c_), after=[t9])
        for _ in range(2):
            n1 = sc.op("V", lambda e: e.tensor_tensor(
                out=d_, in0=c_, in1=c_, op=ALU.mult), after=[tok])
            n2 = sc.op("V", lambda e: e.tensor_tensor(
                out=d_, in0=d_, in1=b, op=ALU.mult), after=[n1])
            n3 = sc.op("V", lambda e: e.tensor_scalar(
                out=d_, in0=d_, scalar1=-0.5, scalar2=1.5, op0=ALU.mult,
                op1=ALU.add), after=[n2])
            tok = sc.op("V", lambda e: e.tensor_tensor(
                out=c_, in0=c_, in1=d_, op=ALU.mult), after=[n3])
        nb = sc.op("V", lambda e: e.scalar_tensor_tensor(
            out=d_, in0=a, scalar=-1.0, in1=c_, op0=ALU.mult, op1=ALU.mult),
            after=[tok])
        xno = sc.op("V", lambda e: e.tensor_scalar(
            out=dst, in0=xsrc, scalar1=c_, scalar2=d_, op0=ALU.mult,
            op1=ALU.add), after=[nb])
        return xno

    def emit_quant(src_ap, dst_bf, width, dep, avq, aq, b0a, b1a, sga, wsa,
                   mm_, ss_, rr_):
        nb16 = width // 16
        v3 = src_ap.rearrange("p (b k) -> p b k", k=16)
        q1 = sc.op("V", lambda e: e.tensor_reduce(
            out=mm_, in_=v3, axis=AX, op=ALU.max, apply_absolute_value=True),
            after=dep)
        q2a = sc.op("V", lambda e: e.tensor_scalar(
            out=ss_, in0=mm_, scalar1=C6_HI, scalar2=None, op0=ALU.mult),
            after=[q1])
        q2 = sc.op("V", lambda e: e.scalar_tensor_tensor(
            out=ss_, in0=mm_, scalar=C6_LO, in1=ss_, op0=ALU.mult,
            op1=ALU.add), after=[q2a])
        q2 = sc.op("V", lambda e: e.tensor_scalar(
            out=ss_, in0=ss_, scalar1=1e-12, scalar2=None, op0=ALU.max),
            after=[q2])
        q3 = sc.op("V", lambda e: e.reciprocal(out=rr_, in_=ss_), after=[q2])
        q3b = sc.op("V", lambda e: e.tensor_scalar(
            out=ss_, in0=ss_, scalar1=0.5, scalar2=None, op0=ALU.mult),
            after=[q3])
        av = sc.op("S", lambda e: e.activation(
            out=avq, in_=src_ap, func=ACT.Abs), after=dep)
        sg = sc.op("S", lambda e: e.activation(
            out=sga, in_=src_ap, func=ACT.Sign), after=dep + [av])
        rbc = rr_.unsqueeze(2).to_broadcast((128, nb16, 16))
        sbc = ss_.unsqueeze(2).to_broadcast((128, nb16, 16))
        qa = sc.op("V", lambda e: e.tensor_tensor(
            out=aq.rearrange("p (b k) -> p b k", k=16),
            in0=avq.rearrange("p (b k) -> p b k", k=16), in1=rbc,
            op=ALU.mult), after=[av, q3])
        c1 = sc.op("V", lambda e: e.tensor_scalar(
            out=b0a, in0=aq, scalar1=0.25, scalar2=None, op0=ALU.is_gt),
            after=[qa])
        tok = c1
        for thr, wgt in ((0.75, 1.0), (1.25, 1.0), (1.75, 1.0), (2.5, 2.0),
                         (3.5, 2.0), (5.0, 4.0)):
            cc = sc.op("V", lambda e, thr=thr: e.tensor_scalar(
                out=b1a, in0=aq, scalar1=thr, scalar2=None, op0=ALU.is_gt),
                after=[tok])
            if wgt == 1.0:
                tok = sc.op("V", lambda e: e.tensor_tensor(
                    out=b0a, in0=b0a, in1=b1a, op=ALU.add), after=[cc])
            else:
                tok = sc.op("V", lambda e, wgt=wgt: e.scalar_tensor_tensor(
                    out=b0a, in0=b1a, scalar=wgt, in1=b0a, op0=ALU.mult,
                    op1=ALU.add), after=[cc])
        w1 = sc.op("V", lambda e: e.tensor_tensor(
            out=wsa.rearrange("p (b k) -> p b k", k=16),
            in0=sga.rearrange("p (b k) -> p b k", k=16), in1=sbc,
            op=ALU.mult), after=[sg, q3b, qa])
        dq = sc.op("V", lambda e: e.tensor_tensor(
            out=dst_bf, in0=b0a, in1=wsa, op=ALU.mult), after=[w1, tok])
        return dq, sg

    xp_stores = []
    hq_done = []
    tile_end = {}
    prev_ps_t_reader = ld_ident
    qT_e = kT_e = None

    NT = 0 if phase == "setup" else (1 if phase == "attn1" else T)
    _CACHE.setdefault("marks", []).clear()
    for t in range(NT):
        _CACHE["marks"].append(("tile", t, sc.gidx))
        lo = max(0, t - 2)
        nkt = t - lo + 1
        W = 128 * nkt
        moff = 384 - W
        if t >= 3:
            ld_x.append(sc.op("DL", lambda e, t=t: e.dma_start(
                out=x_t(t), in_=x_d[128 * t:128 * (t + 1), :]), dma=True,
                after=list(tile_end[t - 3])))

        dep0 = [ld_x[t]] + wtoks
        xno = emit_ln(x_t(t), xnf[:], 0, dep0)
        hma = sc.op("V", lambda e: e.tensor_tensor(
            out=xnf[:], in0=xnf[:], in1=ca1b[:], op=ALU.mult),
            after=[xno] + mod_ready)
        hmb = sc.op("V", lambda e: e.tensor_tensor(
            out=xn[:], in0=xnf[:], in1=sab[:], op=ALU.add), after=[hma])

        tr = []
        for j in range(C8):
            tr.append(sc.op("T", lambda e, j=j: e.transpose(
                ps_t[:, 128 * j:128 * (j + 1)], xn[:, 128 * j:128 * (j + 1)],
                ident[:]), after=[hmb, ld_ident, prev_ps_t_reader]))
        h1toks = []
        for j in range(C8):
            h1toks.append(sc.op("V", lambda e, j=j: e.tensor_copy(
                out=cview(h1T, j, 128), in_=ps_t[:, 128 * j:128 * (j + 1)]),
                after=tr))
        prev_ps_t_reader = h1toks[-1]

        mmq = []
        for j in range(C8):
            st = (j == 0)
            sp = (j == C8 - 1)
            mmq.append(sc.op("T", lambda e, j=j, st=st, sp=sp: e.matmul(
                ps_a[:, 0:512], cview(h1T, j, 128), cview(wqb, j, D)[:, 0:512],
                start=st, stop=sp), after=[h1toks[j]] + wtoks))
            mmq.append(sc.op("T", lambda e, j=j, st=st, sp=sp: e.matmul(
                ps_a[:, 512:1024], cview(h1T, j, 128),
                cview(wqb, j, D)[:, 512:1024], start=st, stop=sp),
                after=[h1toks[j]]))
            mmq.append(sc.op("T", lambda e, j=j, st=st, sp=sp: e.matmul(
                ps_a[:, 1024:1280], cview(h1T, j, 128), cview(wkb, j, 256),
                start=st, stop=sp), after=[h1toks[j]]))
            mmq.append(sc.op("T", lambda e, j=j, st=st, sp=sp: e.matmul(
                ps_b[:, 512:768], cview(h1T, j, 128), cview(wvb, j, 256),
                start=st, stop=sp), after=[h1toks[j]]))
        qkv_done = mmq[-4:]

        sq1 = sc.op("S", lambda e: e.activation(
            out=tmq[:], in_=ps_a[:, 0:1024], func=ACT.Square), after=qkv_done)
        r1 = sc.op("V", lambda e: e.tensor_reduce(
            out=red16[:], in_=tmq[:].rearrange("p (h d) -> p h d", d=HD),
            axis=AX, op=ALU.add), after=[sq1])
        r2 = sc.op("V", lambda e: e.tensor_scalar(
            out=red16[:], in0=red16[:], scalar1=1.0 / HD, scalar2=RMS_EPS,
            op0=ALU.mult, op1=ALU.add), after=[r1])
        r3 = sc.op("S", lambda e: e.activation(
            out=red16b[:], in_=red16[:], func=ACT.Sqrt), after=[r2])
        tok = sc.op("V", lambda e: e.reciprocal(
            out=red16b[:], in_=red16b[:]), after=[r3])
        for _ in range(2):
            n1 = sc.op("V", lambda e: e.tensor_tensor(
                out=red16c[:], in0=red16b[:], in1=red16b[:], op=ALU.mult),
                after=[tok])
            n2 = sc.op("V", lambda e: e.tensor_tensor(
                out=red16c[:], in0=red16c[:], in1=red16[:], op=ALU.mult),
                after=[n1])
            n3 = sc.op("V", lambda e: e.tensor_scalar(
                out=red16c[:], in0=red16c[:], scalar1=-0.5, scalar2=1.5,
                op0=ALU.mult, op1=ALU.add), after=[n2])
            tok = sc.op("V", lambda e: e.tensor_tensor(
                out=red16b[:], in0=red16b[:], in1=red16c[:], op=ALU.mult),
                after=[n3])
        gcom = sc.op("V", lambda e: e.tensor_tensor(
            out=red16b[:], in0=red16b[:], in1=gainrep[:], op=ALU.mult),
            after=[tok] + mod_ready)
        qh = sc.op("V", lambda e: e.tensor_tensor(
            out=tmq[:].rearrange("p (h d) -> p h d", d=HD),
            in0=ps_a[:, 0:1024].rearrange("p (h d) -> p h d", d=HD),
            in1=red16b[:].unsqueeze(2).to_broadcast((128, H, HD)),
            op=ALU.mult), after=[gcom, sq1])

        cosv = cos_all[:, 32 * t:32 * (t + 1)].unsqueeze(1).to_broadcast((128, H, 32))
        sinv = sin_all[:, 32 * t:32 * (t + 1)].unsqueeze(1).to_broadcast((128, H, 32))
        qv = tmq[:].rearrange("p (h d) -> p h d", d=HD)
        w2v = tmq2[:].rearrange("p (h d) -> p h d", d=HD)
        o1 = qrope[:].rearrange("p (h d) -> p h d", d=HD)[:, :, 0:32]
        o2 = qrope[:].rearrange("p (h d) -> p h d", d=HD)[:, :, 32:64]
        rp1 = sc.op("V", lambda e, cosv=cosv: e.tensor_tensor(
            out=w2v[:, :, 0:32], in0=qv[:, :, 0:32], in1=cosv, op=ALU.mult),
            after=[qh] + trig_ready)
        rp2 = sc.op("V", lambda e, sinv=sinv: e.tensor_tensor(
            out=w2v[:, :, 32:64], in0=qv[:, :, 32:64], in1=sinv, op=ALU.mult),
            after=[qh])
        rp3 = sc.op("V", lambda e: e.tensor_tensor(
            out=o1, in0=w2v[:, :, 0:32], in1=w2v[:, :, 32:64], op=ALU.add),
            after=[rp1, rp2])
        rp4 = sc.op("V", lambda e, cosv=cosv: e.tensor_tensor(
            out=w2v[:, :, 0:32], in0=qv[:, :, 32:64], in1=cosv, op=ALU.mult),
            after=[rp3])
        rp5 = sc.op("V", lambda e, sinv=sinv: e.tensor_tensor(
            out=w2v[:, :, 32:64], in0=qv[:, :, 0:32], in1=sinv, op=ALU.mult),
            after=[rp3])
        rp6 = sc.op("V", lambda e: e.tensor_tensor(
            out=o2, in0=w2v[:, :, 0:32], in1=w2v[:, :, 32:64],
            op=ALU.subtract), after=[rp4, rp5])
        q_done = [rp3, rp6]

        sk1 = sc.op("S", lambda e: e.activation(
            out=tmq[:, 0:256], in_=ps_a[:, 1024:1280], func=ACT.Square),
            after=qkv_done + [rp6])
        k1 = sc.op("V", lambda e: e.tensor_reduce(
            out=red16[:, 0:4],
            in_=tmq[:, 0:256].rearrange("p (h d) -> p h d", d=HD), axis=AX,
            op=ALU.add), after=[sk1])
        k2 = sc.op("V", lambda e: e.tensor_scalar(
            out=red16[:, 0:4], in0=red16[:, 0:4], scalar1=1.0 / HD,
            scalar2=RMS_EPS, op0=ALU.mult, op1=ALU.add), after=[k1])
        k3 = sc.op("S", lambda e: e.activation(
            out=red16b[:, 0:4], in_=red16[:, 0:4], func=ACT.Sqrt), after=[k2])
        tok = sc.op("V", lambda e: e.reciprocal(
            out=red16b[:, 0:4], in_=red16b[:, 0:4]), after=[k3])
        for _ in range(2):
            n1 = sc.op("V", lambda e: e.tensor_tensor(
                out=red16c[:, 0:4], in0=red16b[:, 0:4], in1=red16b[:, 0:4],
                op=ALU.mult), after=[tok])
            n2 = sc.op("V", lambda e: e.tensor_tensor(
                out=red16c[:, 0:4], in0=red16c[:, 0:4], in1=red16[:, 0:4],
                op=ALU.mult), after=[n1])
            n3 = sc.op("V", lambda e: e.tensor_scalar(
                out=red16c[:, 0:4], in0=red16c[:, 0:4], scalar1=-0.5,
                scalar2=1.5, op0=ALU.mult, op1=ALU.add), after=[n2])
            tok = sc.op("V", lambda e: e.tensor_tensor(
                out=red16b[:, 0:4], in0=red16b[:, 0:4], in1=red16c[:, 0:4],
                op=ALU.mult), after=[n3])
        khx = sc.op("V", lambda e: e.tensor_tensor(
            out=tmq[:, 0:256].rearrange("p (h d) -> p h d", d=HD),
            in0=ps_a[:, 1024:1280].rearrange("p (h d) -> p h d", d=HD),
            in1=red16b[:, 0:4].unsqueeze(2).to_broadcast((128, 4, HD)),
            op=ALU.mult), after=[tok, sk1])
        kv4 = tmq[:, 0:256].rearrange("p (h d) -> p h d", d=HD)
        kw4 = tmq2[:, 0:256].rearrange("p (h d) -> p h d", d=HD)
        cos4 = cos_all[:, 32 * t:32 * (t + 1)].unsqueeze(1).to_broadcast((128, 4, 32))
        sin4 = sin_all[:, 32 * t:32 * (t + 1)].unsqueeze(1).to_broadcast((128, 4, 32))
        kr2v = krope2[:].rearrange("p (g r d) -> p g r d", g=4, r=2)
        kp1 = sc.op("V", lambda e, cos4=cos4: e.tensor_tensor(
            out=kw4[:, :, 0:32], in0=kv4[:, :, 0:32], in1=cos4, op=ALU.mult),
            after=[khx])
        kp2 = sc.op("V", lambda e, sin4=sin4: e.tensor_tensor(
            out=kw4[:, :, 32:64], in0=kv4[:, :, 32:64], in1=sin4,
            op=ALU.mult), after=[khx])
        kp3 = sc.op("V", lambda e: e.tensor_tensor(
            out=kr2v[:, :, 0, 0:32], in0=kw4[:, :, 0:32],
            in1=kw4[:, :, 32:64], op=ALU.add), after=[kp1, kp2])
        kp4 = sc.op("V", lambda e, cos4=cos4: e.tensor_tensor(
            out=kw4[:, :, 0:32], in0=kv4[:, :, 32:64], in1=cos4, op=ALU.mult),
            after=[kp3])
        kp5 = sc.op("V", lambda e, sin4=sin4: e.tensor_tensor(
            out=kw4[:, :, 32:64], in0=kv4[:, :, 0:32], in1=sin4,
            op=ALU.mult), after=[kp3])
        kp6 = sc.op("V", lambda e: e.tensor_tensor(
            out=kr2v[:, :, 0, 32:64], in0=kw4[:, :, 0:32],
            in1=kw4[:, :, 32:64], op=ALU.subtract), after=[kp4, kp5])
        kp7 = sc.op("V", lambda e: e.tensor_copy(
            out=kr2v[:, :, 1, :], in_=kr2v[:, :, 0, :]), after=[kp3, kp6])
        vev = sc.op("V", lambda e, t=t: e.tensor_copy(
            out=vtr[:, 256 * (t % 4):256 * (t % 4 + 1)],
            in_=ps_b[:, 512:768]), after=qkv_done + [khx])

        trq = []
        for j in range(C8):
            trq.append(sc.op("T", lambda e, j=j: e.transpose(
                ps_t[:, 128 * j:128 * (j + 1)], qrope[:, 128 * j:128 * (j + 1)],
                ident[:]), after=q_done + [prev_ps_t_reader]))
        qT_e = sc.op("V", lambda e: e.tensor_copy(out=qT[:], in_=ps_t[:]),
                     after=trq)
        trk = []
        for g in range(4):
            trk.append(sc.op("T", lambda e, g=g: e.transpose(
                ps_t[:, 128 * g:128 * (g + 1)],
                krope2[:, 128 * g:128 * (g + 1)], ident[:]),
                after=[kp7, qT_e]))
        kT_e = sc.op("V", lambda e, t=t: e.tensor_copy(
            out=kTr[:].rearrange("p (g r n) -> p g r n", g=KVH, r=4)[:, :, t % 4, :],
            in_=ps_t[:, 0:512].rearrange("p (g n) -> p g n", g=4)), after=trk)

        yready = []
        PT_prev = kT_e
        y_prev = None
        for h in range(H):
            g = h // 4
            qoff = 64 * (h % 2)
            qc = h // 2
            mm_s = []
            for kt2 in range(nkt):
                slot = (lo + kt2) % 4
                mm_s.append(sc.op("T", lambda e, kt2=kt2, slot=slot, g=g,
                                  qoff=qoff, qc=qc: e.matmul(
                    ps_b[:, 128 * kt2:128 * (kt2 + 1)],
                    cview(qT, qc, 128)[qoff:qoff + 64, :],
                    kTr[:].rearrange("p (g r n) -> p g r n", g=KVH, r=4)
                    [qoff:qoff + 64, g, slot, :],
                    start=True, stop=True), after=[qT_e, kT_e]))
            msk = sc.op("V", lambda e, W=W, moff=moff: e.tensor_tensor(
                out=msS[:, 0:W], in0=maskb[:, moff:moff + W],
                in1=ps_b[:, 0:W], op=ALU.add), after=mm_s + [ld_mask])
            mx = sc.op("V", lambda e, W=W: e.tensor_reduce(
                out=mrow[:], in_=msS[:, 0:W], axis=AX, op=ALU.max),
                after=[msk])
            ng = sc.op("V", lambda e: e.tensor_scalar(
                out=mrow[:], in0=mrow[:], scalar1=-1.0, scalar2=None,
                op0=ALU.mult), after=[mx])
            expo = sc.op("S", lambda e, W=W: e.activation(
                out=Pf[:, 0:W], in_=msS[:, 0:W], func=ACT.Exp, bias=mrow[:],
                scale=1.0, accum_out=rsum[:]), after=[ng])
            rr = sc.op("V", lambda e: e.reciprocal(out=rsum[:], in_=rsum[:]),
                       after=[expo])
            pn = sc.op("V", lambda e, W=W: e.tensor_scalar(
                out=Pb[:, 0:W], in0=Pf[:, 0:W], scalar1=rsum[:], scalar2=None,
                op0=ALU.mult), after=[rr])
            trp = []
            for kt2 in range(nkt):
                trp.append(sc.op("T", lambda e, kt2=kt2: e.transpose(
                    ps_t[:, 128 * kt2:128 * (kt2 + 1)],
                    Pb[:, 128 * kt2:128 * (kt2 + 1)], ident[:]),
                    after=[pn, PT_prev]))
            PT_e = sc.op("V", lambda e, W=W: e.tensor_copy(
                out=PTs[:, 0:W], in_=ps_t[:, 0:W]), after=trp)
            PT_prev = PT_e
            mm_y = []
            po = 64 * (h % 2)
            for kt2 in range(nkt):
                slot = (lo + kt2) % 4
                kwargs = {"tile_position": (0, 64)} if po else {}
                mm_y.append(sc.op("T", lambda e, kt2=kt2, slot=slot, g=g,
                                  po=po, kw=kwargs, nkt=nkt: e.matmul(
                    ps_b[po:po + 64, 1024:1152],
                    vtr[:, 256 * slot + 64 * g:256 * slot + 64 * (g + 1)],
                    PTs[:, 128 * kt2:128 * (kt2 + 1)],
                    start=(kt2 == 0), stop=(kt2 == nkt - 1), **kw),
                    after=[PT_e, vev] + ([y_prev] if y_prev else [])))
            yev = sc.op("V", lambda e, qc=qc, po=po: e.tensor_copy(
                out=cview(yT, qc, 128)[po:po + 64, :],
                in_=ps_b[po:po + 64, 1024:1152]), after=mm_y)
            y_prev = yev
            yready.append(yev)

        mwo = []
        for j in range(C8):
            st = (j == 0)
            sp = (j == C8 - 1)
            mwo.append(sc.op("T", lambda e, j=j, st=st, sp=sp: e.matmul(
                ps_a[:, 0:512], cview(yT, j, 128), cview(wob, j, D)[:, 0:512],
                start=st, stop=sp), after=yready))
            mwo.append(sc.op("T", lambda e, j=j, st=st, sp=sp: e.matmul(
                ps_a[:, 512:1024], cview(yT, j, 128),
                cview(wob, j, D)[:, 512:1024], start=st, stop=sp),
                after=yready))
        att_done = mwo[-2:]
        rs1 = sc.op("V", lambda e: e.tensor_tensor(
            out=stg[:, 0:D], in0=ps_a[:, 0:1024], in1=gab[:], op=ALU.mult),
            after=att_done + mod_ready)
        rs2 = sc.op("V", lambda e, t=t: e.tensor_tensor(
            out=x_t(t), in0=x_t(t), in1=stg[:, 0:D], op=ALU.add),
            after=[rs1])
        xst = sc.op("DS", lambda e, t=t: e.dma_start(
            out=xprime_d[128 * t:128 * (t + 1), :], in_=x_t(t)), dma=True,
            after=[rs2])
        xp_stores.append(xst)

        xn2 = emit_ln(x_t(t), h2f[:], 4, [rs2])
        hm1 = sc.op("V", lambda e: e.tensor_tensor(
            out=h2f[:], in0=h2f[:], in1=cb1b[:], op=ALU.mult),
            after=[xn2] + mod_ready)
        hm2 = sc.op("V", lambda e: e.tensor_tensor(
            out=h2f[:], in0=h2f[:], in1=sbb[:], op=ALU.add), after=[hm1])
        dq, q_sg = emit_quant(h2f[:], xn[:], D, [hm2], tmq[:], tmq2[:],
                              qb0[:], qb1[:], sgq[:], tmq[:], m64[:], s64[:],
                              r64[:])
        trh = []
        for j in range(C8):
            trh.append(sc.op("T", lambda e, j=j: e.transpose(
                ps_t[:, 128 * j:128 * (j + 1)], xn[:, 128 * j:128 * (j + 1)],
                ident[:]), after=[dq, PT_prev] + mwo))
        hqT_e = sc.op("V", lambda e, t=t: e.tensor_copy(
            out=hqTb[:].rearrange("p (j s) -> p j s", j=C8)[:, :, 128 * t:128 * (t + 1)],
            in_=ps_t[:].rearrange("p (j n) -> p j n", j=C8)), after=trh)
        hq_done.append(hqT_e)
        prev_ps_t_reader = hqT_e
        tile_end[t] = (hqT_e, q_sg, xst)

        if _DEBUG and t == 0:
            _tap_prev = [hqT_e]

            def tap(name, src_ap, deps, cast=True):
                wdt = 1
                for s_ in src_ap.shape[1:]:
                    wdt *= int(s_)
                if cast:
                    cpy = sc.op("V", lambda e, src_ap=src_ap, wdt=wdt: e.tensor_copy(
                        out=stg[:, 0:wdt], in_=src_ap),
                        after=deps + _tap_prev)
                    st_ = sc.op("DS", lambda e, name=name, wdt=wdt: e.dma_start(
                        out=dbg[name], in_=stg[:, 0:wdt]),
                        dma=True, after=[cpy])
                else:
                    st_ = sc.op("DS", lambda e, name=name, src_ap=src_ap: e.dma_start(
                        out=dbg[name], in_=src_ap), dma=True,
                        after=deps + _tap_prev)
                _tap_prev[:] = [st_]
            tap("dbg_h1T", h1T[:], h1toks)
            tap("dbg_q", qrope[:], q_done)
            tap("dbg_k",
                krope2[:].rearrange("p (g r d) -> p g r d", g=4, r=2)[:, :, 0, :],
                [kp7])
            tap("dbg_y", yT[:], yready)
            tap("dbg_hq", xn[:], [dq])
            tap("dbg_xp", x_t(t), [rs2, xst], cast=False)

    attn_exit = list(hq_done) + xp_stores + ([qT_e, kT_e] if NT else [])
    if not attn_exit:
        attn_exit = first_gate
    attn.close()
    outer.close()

    ffn = contextlib.ExitStack()

    def sbf(name, shape, dt=F32):
        name = "sbf_" + name
        return ffn.enter_context(nc.sbuf_tensor(name, shape, dt))

    wgT = sbf("wgT", [128, C8 * DI], BF16)
    wuT = sbf("wuT", [128, C8 * DI], BF16)
    wdT = sbf("wdT", [128, C22 * D], BF16)
    fstg = sbf("fstg", [128, 1024])
    wqnt = sbf("wqnt", [128, 1024], BF16)
    favq = sbf("favq", [128, 1024])
    faq = sbf("faq", [128, 1024])
    fb0 = sbf("fb0", [128, 1024], BF16)
    fb1 = sbf("fb1", [128, 1024], BF16)
    fsg = sbf("fsg", [128, 1024], BF16)
    fh3 = sbf("fh3", [128, 512])
    fm = sbf("fm", [128, 64])
    fs = sbf("fs", [128, 64])
    fr = sbf("fr", [128, 64])
    xpt = sbf("xpt", [128, D])
    hidqT = sbf("hidqT", [128, C22 * 128], BF16)
    fout = sbf("fout", [128, D])

    def wgT_v(j):
        return wgT[:].rearrange("p (j n) -> p j n", j=C8)[:, j, :]

    def wuT_v(j):
        return wuT[:].rearrange("p (j n) -> p j n", j=C8)[:, j, :]

    def wdT_v(r):
        return wdT[:].rearrange("p (r n) -> p r n", r=C22)[:, r, :]

    def hidqT_v(r):
        return hidqT[:].rearrange("p (r n) -> p r n", r=C22)[:, r, :]

    def fquant(src_ap, dst_bf, width, dep):
        return emit_quant(src_ap, dst_bf, width, dep, favq[:, 0:width],
                          faq[:, 0:width], fb0[:, 0:width], fb1[:, 0:width],
                          fsg[:, 0:width], favq[:, 0:width],
                          fm[:, 0:width // 16], fs[:, 0:width // 16],
                          fr[:, 0:width // 16])

    wg_ready = []
    prevw = None
    prev_ps_t_f = hq_done[-1] if hq_done else ld_ident

    def prep_w(src_d, nrow, dstT, jdim, col_splits, deps0):
        nonlocal prevw, prev_ps_t_f
        toks = []
        for r in range(nrow):
            for (c0, c1, jg0) in col_splits:
                wd_ = c1 - c0
                l = sc.op("DL", lambda e, r=r, c0=c0, c1=c1: e.dma_start(
                    out=fstg[:, 0:c1 - c0] if c1 - c0 <= 1024 else None,
                    in_=src_d[128 * r:128 * (r + 1), c0:c1]), dma=True,
                    after=[prevw] if prevw else deps0)
                qd, _sg = fquant(fstg[:, 0:wd_], wqnt[:, 0:wd_], wd_, [l])
                nblk = wd_ // 128
                sub = []
                for bi in range(nblk):
                    sub.append(sc.op("T", lambda e, bi=bi: e.transpose(
                        ps_t[:, 128 * bi:128 * (bi + 1)],
                        wqnt[:, 128 * bi:128 * (bi + 1)], ident[:]),
                        after=[qd, prev_ps_t_f]))
                ev = sc.op("V", lambda e, r=r, jg0=jg0, nblk=nblk, dstT=dstT,
                           jdim=jdim: e.tensor_copy(
                    out=dstT[:].rearrange("p (j n) -> p j n", j=jdim)
                    [:, jg0:jg0 + nblk, 128 * r:128 * (r + 1)],
                    in_=ps_t[:, 0:128 * nblk].rearrange(
                        "p (b n) -> p b n", b=nblk)), after=sub)
                prevw = ev
                prev_ps_t_f = ev
                toks.append(ev)
        return toks

    if phase == "full":
        wg_ready += prep_w(wg_d, C22, wgT, C8, [(0, 1024, 0)], attn_exit)
        wg_ready += prep_w(wu_d, C22, wuT, C8, [(0, 1024, 0)], [])
        wg_ready += prep_w(wd_d, C8, wdT, C22,
                           [(0, 1024, 0), (1024, 2048, 8), (2048, 2816, 16)], [])

    out_stores = []
    prevf = []
    prev_xl = None
    for t in range(T if phase == "full" else 0):
        _CACHE["marks"].append(("ffn", t, sc.gidx))
        gate_xl = [xp_stores[t]] + prevf
        xl = sc.op("DL", lambda e, t=t: e.dma_start(
            out=xpt[:], in_=xprime_d[128 * t:128 * (t + 1), :]), dma=True,
            after=gate_xl + ([prev_xl] if prev_xl else [wg_ready[-1]]))
        prev_xl = xl
        hid_ready = []
        prev_h3 = None
        for half in range(2):
            mmg = []
            for j in range(C8):
                st = (j == 0)
                sp = (j == C8 - 1)
                for (c0, c1) in ((0, 512), (512, 1024), (1024, 1408)):
                    off = 1408 * half
                    mmg.append(sc.op("T", lambda e, j=j, st=st, sp=sp, c0=c0,
                                     c1=c1, off=off, t=t: e.matmul(
                        ps_a[:, c0:c1], hqT_v(j, t),
                        wgT_v(j)[:, off + c0:off + c1], start=st, stop=sp),
                        after=[hq_done[t], wg_ready[-1]] + prevf))
                    mmg.append(sc.op("T", lambda e, j=j, st=st, sp=sp, c0=c0,
                                     c1=c1, off=off, t=t: e.matmul(
                        ps_b[:, c0:c1], hqT_v(j, t),
                        wuT_v(j)[:, off + c0:off + c1], start=st, stop=sp),
                        after=[hq_done[t]]))
            gu_done = mmg[-2:]
            for sl_i, (c0, c1) in enumerate(((0, 512), (512, 1024),
                                             (1024, 1408))):
                wd_ = c1 - c0
                slv = sc.op("S", lambda e, c0=c0, c1=c1: e.activation(
                    out=fstg[:, 0:c1 - c0], in_=ps_a[:, c0:c1], func=ACT.Silu),
                    after=gu_done + ([prev_h3] if prev_h3 else []))
                h3v = sc.op("V", lambda e, c0=c0, c1=c1, wd_=wd_: e.tensor_tensor(
                    out=fh3[:, 0:wd_], in0=fstg[:, 0:wd_], in1=ps_b[:, c0:c1],
                    op=ALU.mult), after=[slv])
                prev_h3 = h3v
                hqd, _sg2 = fquant(fh3[:, 0:wd_], wqnt[:, 0:wd_], wd_, [h3v])
                nblk = wd_ // 128
                sub = []
                for bi in range(nblk):
                    sub.append(sc.op("T", lambda e, bi=bi: e.transpose(
                        ps_t[:, 128 * bi:128 * (bi + 1)],
                        wqnt[:, 128 * bi:128 * (bi + 1)], ident[:]),
                        after=[hqd, prev_ps_t_f]))
                base_r = half * 11 + (0, 4, 8)[sl_i]
                hv = sc.op("V", lambda e, base_r=base_r, nblk=nblk: e.tensor_copy(
                    out=hidqT[:].rearrange("p (r n) -> p r n", r=C22)
                    [:, base_r:base_r + nblk, :],
                    in_=ps_t[:, 0:128 * nblk].rearrange("p (b n) -> p b n",
                                                        b=nblk)), after=sub)
                prev_ps_t_f = hv
                hid_ready.append(hv)
        mmd = []
        for r in range(C22):
            st = (r == 0)
            sp = (r == C22 - 1)
            mmd.append(sc.op("T", lambda e, r=r, st=st, sp=sp: e.matmul(
                ps_a[:, 0:512], hidqT_v(r), wdT_v(r)[:, 0:512], start=st,
                stop=sp), after=hid_ready))
            mmd.append(sc.op("T", lambda e, r=r, st=st, sp=sp: e.matmul(
                ps_a[:, 512:1024], hidqT_v(r), wdT_v(r)[:, 512:1024],
                start=st, stop=sp), after=hid_ready))
        dn_done = mmd[-2:]
        f1 = sc.op("V", lambda e: e.tensor_tensor(
            out=fout[:], in0=ps_a[:, 0:1024], in1=gbb[:], op=ALU.mult),
            after=dn_done + bc_gb)
        f2 = sc.op("V", lambda e: e.tensor_tensor(
            out=fout[:], in0=fout[:], in1=xpt[:], op=ALU.add),
            after=[f1, xl])
        ost = sc.op("DS", lambda e, t=t: e.dma_start(
            out=out_d[128 * t:128 * (t + 1), :], in_=fout[:]), dma=True,
            after=[f2])
        out_stores.append(ost)
        prevf = [f2]

    ffn.close()

    if KCUT:
        sc.cut(KCUT)
    sc.finalize()
    sems = {}
    for e in ENGS:
        sems[e] = stack.enter_context(nc.semaphore("sem_" + e))
    n_ds = sum(16 for r in sc.ops["DS"] if r["inc"] and r["dma"])
    n_dl = sum(16 for r in sc.ops["DL"] if r["inc"] and r["dma"])
    with nc.Block() as block:
        @block.tensor
        def _(eng):
            sc.replay(eng, "T", sems)

        @block.vector
        def _(eng):
            sc.replay(eng, "V", sems)

        @block.scalar
        def _(eng):
            sc.replay(eng, "S", sems)

        @block.gpsimd
        def _(eng):
            sc.replay(eng, "G", sems)

        @block.sync
        def _(eng):
            sc.replay_merged(eng, ["DL", "DS"], sems)
            if n_dl:
                eng.wait_ge(sems["DL"], n_dl)
            if n_ds:
                eng.wait_ge(sems["DS"], n_ds)

    stack.close()
    _CACHE[key] = nc
    return nc


def kernel(L0_attn_x, L0_attn_Wq, L0_attn_Wk, L0_attn_Wv, L0_attn_Wo,
           L0_attn_q_gain, L0_attn_pair_mix, L0_ffn_w_gate, L0_ffn_w_up,
           L0_ffn_w_down, L0_c, L0_wmod):
    f = np.float32
    x = np.ascontiguousarray(np.asarray(L0_attn_x, f))
    wqT = np.ascontiguousarray(np.asarray(L0_attn_Wq, f).T)
    wkT = np.ascontiguousarray(np.asarray(L0_attn_Wk, f).T)
    wvT = np.ascontiguousarray(np.asarray(L0_attn_Wv, f).T)
    woT = np.ascontiguousarray(np.asarray(L0_attn_Wo, f).T)
    perm = np.arange(D).reshape(C8, 2, 64)[:, ::-1, :].reshape(D)
    woTsw = np.ascontiguousarray(woT[perm])
    wmod = np.ascontiguousarray(np.asarray(L0_wmod, f))
    wg = np.ascontiguousarray(np.asarray(L0_ffn_w_gate, f))
    wu = np.ascontiguousarray(np.asarray(L0_ffn_w_up, f))
    wd = np.ascontiguousarray(np.asarray(L0_ffn_w_down, f))
    cfull = np.asarray(L0_c, f)
    gain = np.ascontiguousarray(np.asarray(L0_attn_q_gain, f).reshape(1, H))
    pm = np.ascontiguousarray(np.asarray(L0_attn_pair_mix, f).reshape(1, 32))
    maskb = _mask_host()
    ident = np.eye(128, dtype=ml_dtypes.bfloat16)

    nc = build()
    shared = {"wqT": wqT, "wkT": wkT, "wvT": wvT, "woT": woT,
              "woTsw": woTsw, "wmod": wmod, "wg": wg, "wu": wu, "wd": wd,
              "gain": gain, "pm": pm, "maskb": maskb, "ident": ident}
    in_maps = []
    for b in range(NCORE):
        m = dict(shared)
        m["x"] = np.ascontiguousarray(x[b])
        m["crow"] = np.ascontiguousarray(cfull[b:b + 1])
        in_maps.append(m)
    res = run_bass_kernel_spmd(nc, in_maps, core_ids=list(range(NCORE)))
    out = np.stack([np.asarray(r["out"]) for r in res.results], axis=0)
    return out.astype(np.float32)


# version 08
# speedup vs baseline: 1.0227x; 1.0227x over previous
import math
import numpy as np
import ml_dtypes
import concourse.bass as bass
import concourse.mybir as mybir
from concourse.bass_utils import run_bass_kernel_spmd

F32 = mybir.dt.float32
BF16 = mybir.dt.bfloat16
I32 = mybir.dt.int32
ALU = mybir.AluOpType
ACT = mybir.ActivationFunctionType
AX = mybir.AxisListType.X

S, D, H, KVH, HD = 2048, 1024, 16, 4, 64
T = S // 128
C8 = D // 128
DI = 2816
C22 = DI // 128
NCORE = 8
LN_EPS = 1e-5
RMS_EPS = 1e-6
ROPE_BASE = 10000.0
NEG = -1.0e30
PI = float(np.float32(math.pi))
TWO_PI = 2.0 * math.pi
PI_CLAMP = 3.1415925
TP_HI = float(np.float32(TWO_PI))
TP_LO = TWO_PI - TP_HI
_ES = -math.log(ROPE_BASE) / 32.0
ES_HI = float(np.float32(_ES))
ES_LO = _ES - ES_HI
_C6 = 1.0 / 6.0
C6_HI = float(np.float32(_C6))
C6_LO = _C6 - C6_HI
_C12 = 1.0 / 12.0
C12_HI = float(np.float32(_C12))
C12_LO = _C12 - C12_HI
CLAMP_HALF = float(np.float32(1e-12)) / 2.0
_PI2 = math.pi / 2.0
PI2_HI = float(np.float32(_PI2))
PI2_LO = _PI2 - PI2_HI

_DEBUG = False

ENGS = ("T", "V", "S", "G", "DL", "DS")


class Script:
    def __init__(self):
        self.ops = {e: [] for e in ENGS}
        self.gidx = 0

    def op(self, eng, fn, after=(), dma=False):
        deps = []
        for t in after:
            if t is None:
                continue
            if t["eng"] == eng and not t["dma"]:
                continue
            t["inc"] = True
            deps.append(t)
        rec = {"eng": eng, "fn": fn, "after": deps, "dma": dma,
               "inc": dma, "val": None, "gidx": self.gidx}
        self.gidx += 1
        self.ops[eng].append(rec)
        return rec

    def cut(self, maxg):
        kept = set()
        for e in ENGS:
            self.ops[e] = [r for r in self.ops[e] if r["gidx"] < maxg]
            for r in self.ops[e]:
                kept.add(id(r))
        for e in ENGS:
            for r in self.ops[e]:
                r["after"] = [t for t in r["after"] if id(t) in kept]

    def finalize(self):
        for e in ("V", "S", "G"):
            for rec in self.ops[e]:
                rec["inc"] = True
        for e in ENGS:
            run = 0
            for rec in self.ops[e]:
                if rec["inc"]:
                    run += 16 if rec["dma"] else 1
                rec["val"] = run

    def replay_merged(self, eng, enames, sems):
        ops = sorted([r for e in enames for r in self.ops[e]],
                     key=lambda r: r["gidx"])
        self._replay_list(eng, ops, sems)

    def replay(self, eng, ename, sems):
        self._replay_list(eng, self.ops[ename], sems)

    def _replay_list(self, eng, ops, sems):
        selfserial = ops and ops[0]["eng"] in ("V", "S", "G")
        nprev = 0
        for rec in ops:
            seen = {}
            for t in rec["after"]:
                seen[t["eng"]] = max(seen.get(t["eng"], 0), t["val"])
            if selfserial and nprev > 0:
                seen[rec["eng"]] = max(seen.get(rec["eng"], 0), nprev)
            for key, v in seen.items():
                if v > 0:
                    eng.wait_ge(sems[key], v)
            instr = rec["fn"](eng)
            if rec["inc"]:
                instr.then_inc(sems[rec["eng"]], 16 if rec["dma"] else 1)
            if selfserial:
                nprev = rec["val"]


_CACHE = {}


def _mask_host():
    i = np.arange(128)[:, None]
    c = np.arange(384)[None, :]
    valid = (c > i) & (c <= i + 256)
    return np.where(valid, 0.0, NEG).astype(np.float32)


def _rope_tables_host():
    import jax
    import jax.numpy as jnp
    with jax.default_device(jax.devices("cpu")[0]):
        inv_freq = 1.0 / ROPE_BASE ** (jnp.arange(0, 64, 2,
                                                  dtype=jnp.float32) / 64)
        t = jnp.arange(S, dtype=jnp.float32)
        freqs = jnp.outer(t, inv_freq)
        cosj = np.asarray(jnp.cos(freqs))
        sinj = np.asarray(jnp.sin(freqs))
    ctbl = np.ascontiguousarray(
        cosj.reshape(T, 128, 32).transpose(1, 0, 2).reshape(128, T * 32))
    stbl = np.ascontiguousarray(
        sinj.reshape(T, 128, 32).transpose(1, 0, 2).reshape(128, T * 32))
    return ctbl, stbl


def build(phase="full"):
    import os as _os
    KCUT = int(_os.environ.get("KCUT", "0"))
    key = (phase, KCUT)
    if key in _CACHE:
        return _CACHE[key]
    nc = bass.Bass("TRN2", target_bir_lowering=False,
                   detect_race_conditions=False)
    sc = Script()

    def din(name, shape, dt=F32):
        return nc.dram_tensor(name, shape, dt, kind="ExternalInput").ap()

    x_d = din("x", [S, D])
    wqT_d = din("wqT", [D, D])
    wkT_d = din("wkT", [D, KVH * HD])
    wvT_d = din("wvT", [D, KVH * HD])
    woT_d = din("woT", [D, D])
    woTsw_d = din("woTsw", [D, D])
    wmod_d = din("wmod", [D, 6 * D])
    wg_d = din("wg", [DI, D])
    wu_d = din("wu", [DI, D])
    wd_d = din("wd", [D, DI])
    crow_d = din("crow", [1, D])
    gain_d = din("gain", [1, H])
    pm_d = din("pm", [1, 32])
    maskb_d = din("maskb", [128, 384])
    ctbl_d = din("ctbl", [128, T * 32])
    stbl_d = din("stbl", [128, T * 32])
    ident_d = din("ident", [128, 128], BF16)
    out_d = nc.dram_tensor("out", [S, D], F32, kind="ExternalOutput").ap()
    xprime_d = nc.dram_tensor("xprime", [S, D], F32).ap()
    dbg = {}
    if _DEBUG:
        for nm, shp in [("dbg_mod", [1, 6 * D]), ("dbg_h1T", [128, C8 * 128]),
                        ("dbg_q", [128, D]), ("dbg_k", [128, 256]),
                        ("dbg_y", [128, C8 * 128]), ("dbg_hq", [128, D]),
                        ("dbg_xp", [128, D])]:
            dbg[nm] = nc.dram_tensor(nm, shp, F32, kind="ExternalOutput").ap()

    import contextlib
    stack = contextlib.ExitStack()

    ident = stack.enter_context(nc.sbuf_tensor("ident_s", [128, 128], BF16))
    gbb = stack.enter_context(nc.sbuf_tensor("gbb", [128, D], F32))
    hqTb = stack.enter_context(nc.sbuf_tensor("hqTb", [128, C8 * S], BF16))

    ps_a = stack.enter_context(nc.psum_tensor("ps_a", [128, 1536], F32))
    ps_b = stack.enter_context(nc.psum_tensor("ps_b", [128, 1536], F32))
    ps_t = stack.enter_context(nc.psum_tensor("ps_t", [128, 1024], BF16))
    ps_c = stack.enter_context(nc.psum_tensor("ps_c", [128, 512], F32))

    ld_ident = sc.op("DL", lambda e: e.dma_start(out=ident[:], in_=ident_d),
                     dma=True)

    outer = contextlib.ExitStack()

    def sbo(name, shape, dt=F32):
        name = "sbo_" + name
        return outer.enter_context(nc.sbuf_tensor(name, shape, dt))

    maskb = sbo("maskb", [128, 384])
    sab = sbo("sab", [128, D])
    ca1b = sbo("ca1b", [128, D])
    gab = sbo("gab", [128, D])
    cb1b = sbo("cb1b", [128, D])
    sbb = sbo("sbb", [128, D])
    gainrep = sbo("gainrep", [128, H])
    pmA = sbo("pmA", [128, C8])
    pmB = sbo("pmB", [128, C8])
    cos_all = sbo("cos_all", [128, T * 32])
    sin_all = sbo("sin_all", [128, T * 32])
    x_ring = sbo("x_ring", [128, 3 * D])

    def x_t(t):
        return x_ring[:, D * (t % 3):D * (t % 3 + 1)]

    ld_mask = sc.op("DL", lambda e: e.dma_start(out=maskb[:], in_=maskb_d),
                    dma=True)
    ld_x = []
    for t in range(3):
        ld_x.append(sc.op("DL", lambda e, t=t: e.dma_start(
            out=x_t(t), in_=x_d[128 * t:128 * (t + 1), :]), dma=True))

    setup = contextlib.ExitStack()

    def sb0(name, shape, dt=F32):
        name = "sb0_" + name
        return setup.enter_context(nc.sbuf_tensor(name, shape, dt))

    modrow = sb0("modrow", [1, 6 * D])
    c_col = sb0("c_col", [128, C8])
    silu_cb = sb0("silu_cb", [128, C8], BF16)
    ones_f = sb0("ones_f", [1, 128])
    stage = sb0("stage", [128, 512])
    stage_b = sb0("stage_b", [128, 512], BF16)
    garow = sb0("garow", [1, 2 * D])
    carow = sb0("carow", [1, 2 * D])
    tmp1 = sb0("tmp1", [1, 2 * D])
    tmp2 = sb0("tmp2", [1, 2 * D])
    grow = sb0("grow", [1, H])
    pmrow = sb0("pmrow", [1, 32])

    ld_c = sc.op("DL", lambda e: e.dma_start(
        out=c_col[:], in_=crow_d.rearrange("o (j p) -> (o p) j", p=128),
        allow_slow_non_contiguous=True), dma=True)
    ld_g = sc.op("DL", lambda e: e.dma_start(out=grow[:], in_=gain_d),
                 dma=True)
    ld_pm = sc.op("DL", lambda e: e.dma_start(out=pmrow[:], in_=pm_d),
                  dma=True)
    ones1 = sc.op("V", lambda e: e.memset(ones_f[:], 1.0))

    sl = sc.op("S", lambda e: e.activation(out=c_col[:], in_=c_col[:],
                                           func=ACT.Silu), after=[ld_c])
    slb = sc.op("V", lambda e: e.tensor_copy(out=silu_cb[:], in_=c_col[:]),
                after=[sl])

    ev_mod = []
    prev_ev = None
    for j in range(12):
        mm_prev = None
        for k in range(C8):
            dep = ([mm_prev] if mm_prev is not None
                   else ([prev_ev] if prev_ev else []))
            ldst = sc.op("DL", lambda e, k=k, j=j: e.dma_start(
                out=stage[:],
                in_=wmod_d[128 * k:128 * (k + 1), 512 * j:512 * (j + 1)]),
                dma=True, after=dep)
            cst = sc.op("V", lambda e: e.tensor_copy(
                out=stage_b[:], in_=stage[:]), after=[ldst])
            mm_prev = sc.op("T", lambda e, k=k: e.matmul(
                ps_c[0:1, 0:512], silu_cb[:, k:k + 1], stage_b[:],
                start=(k == 0), stop=(k == C8 - 1)),
                after=[cst, slb] + ([prev_ev] if prev_ev else []))
        prev_ev = sc.op("S", lambda e, j=j: e.copy(
            out=modrow[0:1, 512 * j:512 * (j + 1)], in_=ps_c[0:1, 0:512]),
            after=[mm_prev])
        ev_mod.append(prev_ev)
    mA = list(ev_mod)

    g1 = sc.op("V", lambda e: e.tensor_copy(
        out=tmp1[0:1, 0:D], in_=modrow[0:1, 2 * D:3 * D]), after=mA)
    g2 = sc.op("V", lambda e: e.tensor_copy(
        out=tmp1[0:1, D:2 * D], in_=modrow[0:1, 5 * D:6 * D]), after=mA)
    sg1 = sc.op("V", lambda e: e.tensor_scalar(
        out=tmp2[:], in0=tmp1[:], scalar1=0.0, scalar2=None, op0=ALU.is_ge),
        after=[g1, g2])
    sg2 = sc.op("V", lambda e: e.tensor_scalar(
        out=tmp2[:], in0=tmp2[:], scalar1=2.0, scalar2=-1.0, op0=ALU.mult,
        op1=ALU.add), after=[sg1])
    mg1 = sc.op("S", lambda e: e.activation(out=tmp1[:], in_=tmp1[:],
                                            func=ACT.Abs),
                after=[g1, g2, sg1])
    mg2 = sc.op("V", lambda e: e.tensor_scalar(
        out=tmp1[:], in0=tmp1[:], scalar1=0.25, scalar2=None, op0=ALU.max),
        after=[mg1])
    gprod = sc.op("V", lambda e: e.tensor_tensor(
        out=garow[:], in0=tmp2[:], in1=tmp1[:], op=ALU.mult),
        after=[mg2, sg2])
    cc1 = sc.op("V", lambda e: e.tensor_copy(
        out=carow[0:1, 0:D], in_=modrow[0:1, D:2 * D]), after=mA)
    cc2 = sc.op("V", lambda e: e.tensor_copy(
        out=carow[0:1, D:2 * D], in_=modrow[0:1, 4 * D:5 * D]), after=mA)
    ca_p1 = sc.op("V", lambda e: e.tensor_scalar(
        out=carow[:], in0=carow[:], scalar1=1.0, scalar2=None, op0=ALU.add),
        after=[cc1, cc2])



    dbg_mod_tok = None
    if _DEBUG:
        dbg_mod_tok = sc.op("DS", lambda e: e.dma_start(
            out=dbg["dbg_mod"], in_=modrow[:]), dma=True, after=mA)

    psc_last = mA[-1]
    mod_ready = []

    def bcast(dst, row_ap, dep):
        nonlocal psc_last
        toks = []
        for hh in range(2):
            mm = sc.op("T", lambda e, hh=hh: e.matmul(
                ps_c[:, 0:512], ones_f[:], row_ap[0:1, 512 * hh:512 * (hh + 1)],
                start=True, stop=True), after=dep + [ones1, psc_last])
            cp = sc.op("V", lambda e, hh=hh: e.tensor_copy(
                out=dst[:, 512 * hh:512 * (hh + 1)], in_=ps_c[:, 0:512]),
                after=[mm])
            psc_last = cp
            toks.append(cp)
        return toks

    bc_gb = bcast(gbb, garow[0:1, D:2 * D], [gprod])
    mod_ready += bc_gb
    mod_ready += bcast(gab, garow[0:1, 0:D], [gprod])
    mod_ready += bcast(cb1b, carow[0:1, D:2 * D], [ca_p1])
    mod_ready += bcast(sbb, modrow[0:1, 3 * D:4 * D], mA)
    mod_ready += bcast(ca1b, carow[0:1, 0:D], [ca_p1])
    mod_ready += bcast(sab, modrow[0:1, 0:D], mA)

    mmg = sc.op("T", lambda e: e.matmul(
        ps_c[:, 0:H], ones_f[:], grow[:], start=True, stop=True),
        after=[ld_g, ones1, psc_last])
    grep = sc.op("V", lambda e: e.tensor_scalar(
        out=gainrep[:], in0=ps_c[:, 0:H], scalar1=0.125, scalar2=None,
        op0=ALU.mult), after=[mmg])
    psc_last = grep
    mod_ready.append(grep)

    for dst, etop, ebot in ((pmA, 0, 3), (pmB, 2, 1)):
        mt = sc.op("T", lambda e, etop=etop: e.matmul(
            ps_c[0:64, 0:C8], ones_f[0:1, 0:64], pmrow[0:1, etop:etop + 29:4],
            start=True, stop=True), after=[ld_pm, ones1, psc_last])
        mb = sc.op("T", lambda e, ebot=ebot: e.matmul(
            ps_c[64:128, 0:C8], ones_f[0:1, 0:64], pmrow[0:1, ebot:ebot + 29:4],
            start=True, stop=True, tile_position=(0, 64)), after=[mt])
        cp = sc.op("V", lambda e, dst=dst: e.tensor_copy(
            out=dst[:], in_=ps_c[:, 0:C8]), after=[mb])
        psc_last = cp
        mod_ready.append(cp)

    trig_cos = sc.op("DL", lambda e: e.dma_start(out=cos_all[:], in_=ctbl_d),
                     dma=True)
    trig_sin = sc.op("DL", lambda e: e.dma_start(out=sin_all[:], in_=stbl_d),
                     dma=True)
    trig_ready = [trig_sin, trig_cos]
    first_gate = mod_ready + trig_ready + ([dbg_mod_tok] if dbg_mod_tok else [])

    setup.close()

    attn = contextlib.ExitStack()

    def sba(name, shape, dt=F32):
        name = "sba_" + name
        return attn.enter_context(nc.sbuf_tensor(name, shape, dt))

    wqb = sba("wqb", [128, C8 * D], BF16)
    wkb = sba("wkb", [128, C8 * 256], BF16)
    wvb = sba("wvb", [128, C8 * 256], BF16)
    wob = sba("wob", [128, C8 * D], BF16)
    kTr = sba("kTr", [128, KVH * 4 * 128], BF16)
    vtr = sba("vtr", [128, 4 * 256], BF16)
    stg = sba("stg", [128, 1024])
    xn = sba("xn", [128, D], BF16)
    h1T = sba("h1T", [128, C8 * 128], BF16)
    tmq = sba("tmq", [128, D])
    tmq2 = sba("tmq2", [128, D])
    qrope = sba("qrope", [128, D], BF16)
    krope2 = sba("krope2", [128, 512], BF16)
    qT = sba("qT", [128, C8 * 128], BF16)
    red16 = sba("red16", [128, 16])
    red16b = sba("red16b", [128, 16])
    red16c = sba("red16c", [128, 16])
    ln_sc = sba("ln_sc", [128, 8])
    msS = sba("msS", [128, 384])
    Pb = sba("Pb", [128, 384], BF16)
    PTs = sba("PTs", [128, 384], BF16)
    rsum = sba("rsum", [128, 1])
    mrow = sba("mrow", [128, 1])
    yT = sba("yT", [128, C8 * 128], BF16)
    h2f = sba("h2f", [128, D])
    xnf = sba("xnf", [128, D])
    Pf = sba("Pf", [128, 384])
    qb0 = sba("qb0", [128, D], BF16)
    qb1 = sba("qb1", [128, D], BF16)
    sgq = sba("sgq", [128, D], BF16)
    m64 = sba("m64", [128, 64])
    s64 = sba("s64", [128, 64])
    r64 = sba("r64", [128, 64])

    wtoks = []

    def loadcast(dst, src_d, ncol):
        toks = []
        for j in range(C8):
            dep = ([toks[-1]] if toks else
                   ([wtoks[-1]] if wtoks else list(first_gate)))
            l = sc.op("DL", lambda e, j=j, ncol=ncol: e.dma_start(
                out=stg[:, 0:ncol], in_=src_d[128 * j:128 * (j + 1), :]),
                dma=True, after=dep)
            cvt = sc.op("V", lambda e, j=j, ncol=ncol: e.tensor_copy(
                out=dst[:, ncol * j:ncol * (j + 1)], in_=stg[:, 0:ncol]),
                after=[l])
            toks.append(cvt)
        return toks

    wtoks += loadcast(wqb, wqT_d, D)
    wtoks += loadcast(wkb, wkT_d, 256)
    wtoks += loadcast(wvb, wvT_d, 256)
    for j in range(C8):
        l1 = sc.op("DL", lambda e, j=j: e.dma_start(
            out=stg[:, 0:D], in_=woT_d[128 * j:128 * (j + 1), :]), dma=True,
            after=[wtoks[-1]] if j == 0 else [wtoks[-1]])
        l2 = sc.op("DL", lambda e, j=j: e.dma_start(
            out=tmq[:], in_=woTsw_d[128 * j:128 * (j + 1), :]), dma=True,
            after=[l1] + (list(first_gate) if j == 0 else []))
        t1 = sc.op("V", lambda e, j=j: e.tensor_scalar(
            out=tmq[:], in0=tmq[:], scalar1=pmB[:, j:j + 1], scalar2=None,
            op0=ALU.mult), after=[l2] + mod_ready)
        t2 = sc.op("V", lambda e, j=j: e.scalar_tensor_tensor(
            out=wob[:, D * j:D * (j + 1)], in0=stg[:, 0:D],
            scalar=pmA[:, j:j + 1], in1=tmq[:], op0=ALU.mult, op1=ALU.add),
            after=[t1])
        wtoks.append(t2)

    def cview(buf, j, n):
        return buf[:].rearrange("p (j n) -> p j n", n=n)[:, j, :]

    def hqT_v(j, t):
        return hqTb[:].rearrange("p (j s) -> p j s", j=C8)[:, j, 128 * t:128 * (t + 1)]

    def emit_ln(xsrc, dst, col, dep):
        a = ln_sc[:, col:col + 1]
        b = ln_sc[:, col + 1:col + 2]
        c_ = ln_sc[:, col + 2:col + 3]
        d_ = ln_sc[:, col + 3:col + 4]
        t1 = sc.op("V", lambda e: e.tensor_reduce(
            out=a, in_=xsrc, axis=AX, op=ALU.add), after=dep)
        t2 = sc.op("S", lambda e: e.activation(
            out=stg[:, 0:D], in_=xsrc, func=ACT.Square), after=dep)
        t3 = sc.op("V", lambda e: e.tensor_reduce(
            out=b, in_=stg[:, 0:D], axis=AX, op=ALU.add), after=[t2])
        t4 = sc.op("V", lambda e: e.tensor_scalar(
            out=a, in0=a, scalar1=1.0 / D, scalar2=None, op0=ALU.mult),
            after=[t1])
        t5 = sc.op("V", lambda e: e.tensor_scalar(
            out=b, in0=b, scalar1=1.0 / D, scalar2=None, op0=ALU.mult),
            after=[t3])
        t6 = sc.op("V", lambda e: e.tensor_tensor(
            out=c_, in0=a, in1=a, op=ALU.mult), after=[t4])
        t7 = sc.op("V", lambda e: e.scalar_tensor_tensor(
            out=b, in0=c_, scalar=-1.0, in1=b, op0=ALU.mult, op1=ALU.add),
            after=[t5, t6])
        t8 = sc.op("V", lambda e: e.tensor_scalar(
            out=b, in0=b, scalar1=LN_EPS, scalar2=None, op0=ALU.add),
            after=[t7])
        t9 = sc.op("S", lambda e: e.activation(
            out=c_, in_=b, func=ACT.Sqrt), after=[t8])
        tok = sc.op("V", lambda e: e.reciprocal(out=c_, in_=c_), after=[t9])
        for _ in range(2):
            n1 = sc.op("V", lambda e: e.tensor_tensor(
                out=d_, in0=c_, in1=c_, op=ALU.mult), after=[tok])
            n2 = sc.op("V", lambda e: e.tensor_tensor(
                out=d_, in0=d_, in1=b, op=ALU.mult), after=[n1])
            n3 = sc.op("V", lambda e: e.tensor_scalar(
                out=d_, in0=d_, scalar1=-0.5, scalar2=1.5, op0=ALU.mult,
                op1=ALU.add), after=[n2])
            tok = sc.op("V", lambda e: e.tensor_tensor(
                out=c_, in0=c_, in1=d_, op=ALU.mult), after=[n3])
        nb = sc.op("V", lambda e: e.scalar_tensor_tensor(
            out=d_, in0=a, scalar=-1.0, in1=c_, op0=ALU.mult, op1=ALU.mult),
            after=[tok])
        xno = sc.op("V", lambda e: e.tensor_scalar(
            out=dst, in0=xsrc, scalar1=c_, scalar2=d_, op0=ALU.mult,
            op1=ALU.add), after=[nb])
        return xno

    def emit_quant(src_ap, dst_bf, width, dep, avq, aq, b0a, b1a, sga, wsa,
                   mm_, ss_, rr_):
        nb16 = width // 16
        v3 = src_ap.rearrange("p (b k) -> p b k", k=16)
        q1 = sc.op("V", lambda e: e.tensor_reduce(
            out=mm_, in_=v3, axis=AX, op=ALU.max, apply_absolute_value=True),
            after=dep)
        q2a = sc.op("V", lambda e: e.tensor_scalar(
            out=ss_, in0=mm_, scalar1=C12_HI, scalar2=None, op0=ALU.mult),
            after=[q1])
        q2 = sc.op("V", lambda e: e.scalar_tensor_tensor(
            out=ss_, in0=mm_, scalar=C12_LO, in1=ss_, op0=ALU.mult,
            op1=ALU.add), after=[q2a])
        q2 = sc.op("V", lambda e: e.tensor_scalar(
            out=ss_, in0=ss_, scalar1=CLAMP_HALF, scalar2=None, op0=ALU.max),
            after=[q2])
        q3 = sc.op("V", lambda e: e.reciprocal(out=rr_, in_=ss_), after=[q2])
        av = sc.op("S", lambda e: e.activation(
            out=avq, in_=src_ap, func=ACT.Abs), after=dep)
        sg = sc.op("S", lambda e: e.activation(
            out=sga, in_=src_ap, func=ACT.Sign), after=dep + [av])
        rbc = rr_.unsqueeze(2).to_broadcast((128, nb16, 16))
        sbc = ss_.unsqueeze(2).to_broadcast((128, nb16, 16))
        qa = sc.op("V", lambda e: e.tensor_tensor(
            out=aq.rearrange("p (b k) -> p b k", k=16),
            in0=avq.rearrange("p (b k) -> p b k", k=16), in1=rbc,
            op=ALU.mult), after=[av, q3])
        tokA = sc.op("V", lambda e: e.tensor_scalar(
            out=b0a, in0=aq, scalar1=0.5, scalar2=None, op0=ALU.is_gt),
            after=[qa])
        for thr in (1.5, 2.5, 3.5):
            tokA = sc.op("V", lambda e, thr=thr: e.scalar_tensor_tensor(
                out=b0a, in0=aq, scalar=thr, in1=b0a, op0=ALU.is_gt,
                op1=ALU.add), after=[tokA])
        tokB = sc.op("V", lambda e: e.tensor_scalar(
            out=b1a, in0=aq, scalar1=5.0, scalar2=None, op0=ALU.is_gt),
            after=[qa, tokA])
        for thr in (7.0, 10.0, 10.0):
            tokB = sc.op("V", lambda e, thr=thr: e.scalar_tensor_tensor(
                out=b1a, in0=aq, scalar=thr, in1=b1a, op0=ALU.is_gt,
                op1=ALU.add), after=[tokB])
        tok = sc.op("V", lambda e: e.scalar_tensor_tensor(
            out=b0a, in0=b1a, scalar=2.0, in1=b0a, op0=ALU.mult,
            op1=ALU.add), after=[tokB])
        w1 = sc.op("G", lambda e: e.tensor_tensor(
            out=wsa.rearrange("p (b k) -> p b k", k=16),
            in0=sga.rearrange("p (b k) -> p b k", k=16), in1=sbc,
            op=ALU.mult), after=[sg, q3, qa])
        dq = sc.op("V", lambda e: e.tensor_tensor(
            out=dst_bf, in0=b0a, in1=wsa, op=ALU.mult), after=[w1, tok])
        return dq, sg

    xp_stores = []
    hq_done = []
    tile_end = {}
    prev_ps_t_reader = ld_ident
    qT_e = kT_e = None

    NT = 0 if phase == "setup" else (1 if phase == "attn1" else T)
    _CACHE.setdefault("marks", []).clear()
    for t in range(NT):
        _CACHE["marks"].append(("tile", t, sc.gidx))
        lo = max(0, t - 2)
        nkt = t - lo + 1
        W = 128 * nkt
        moff = 384 - W
        if t >= 3:
            ld_x.append(sc.op("DL", lambda e, t=t: e.dma_start(
                out=x_t(t), in_=x_d[128 * t:128 * (t + 1), :]), dma=True,
                after=list(tile_end[t - 3])))

        dep0 = [ld_x[t]] + wtoks
        xno = emit_ln(x_t(t), xnf[:], 0, dep0)
        hma = sc.op("V", lambda e: e.tensor_tensor(
            out=xnf[:], in0=xnf[:], in1=ca1b[:], op=ALU.mult),
            after=[xno] + mod_ready)
        hmb = sc.op("V", lambda e: e.tensor_tensor(
            out=xn[:], in0=xnf[:], in1=sab[:], op=ALU.add), after=[hma])

        tr = []
        for j in range(C8):
            tr.append(sc.op("T", lambda e, j=j: e.transpose(
                ps_t[:, 128 * j:128 * (j + 1)], xn[:, 128 * j:128 * (j + 1)],
                ident[:]), after=[hmb, ld_ident, prev_ps_t_reader]))
        h1toks = []
        for j in range(C8):
            h1toks.append(sc.op("V", lambda e, j=j: e.tensor_copy(
                out=cview(h1T, j, 128), in_=ps_t[:, 128 * j:128 * (j + 1)]),
                after=tr))
        prev_ps_t_reader = h1toks[-1]

        mmq = []
        for j in range(C8):
            st = (j == 0)
            sp = (j == C8 - 1)
            mmq.append(sc.op("T", lambda e, j=j, st=st, sp=sp: e.matmul(
                ps_a[:, 0:512], cview(h1T, j, 128), cview(wqb, j, D)[:, 0:512],
                start=st, stop=sp), after=[h1toks[j]] + wtoks))
            mmq.append(sc.op("T", lambda e, j=j, st=st, sp=sp: e.matmul(
                ps_a[:, 512:1024], cview(h1T, j, 128),
                cview(wqb, j, D)[:, 512:1024], start=st, stop=sp),
                after=[h1toks[j]]))
            mmq.append(sc.op("T", lambda e, j=j, st=st, sp=sp: e.matmul(
                ps_a[:, 1024:1280], cview(h1T, j, 128), cview(wkb, j, 256),
                start=st, stop=sp), after=[h1toks[j]]))
            mmq.append(sc.op("T", lambda e, j=j, st=st, sp=sp: e.matmul(
                ps_b[:, 512:768], cview(h1T, j, 128), cview(wvb, j, 256),
                start=st, stop=sp), after=[h1toks[j]]))
        qkv_done = mmq[-4:]

        sq1 = sc.op("S", lambda e: e.activation(
            out=tmq[:], in_=ps_a[:, 0:1024], func=ACT.Square), after=qkv_done)
        r1 = sc.op("V", lambda e: e.tensor_reduce(
            out=red16[:], in_=tmq[:].rearrange("p (h d) -> p h d", d=HD),
            axis=AX, op=ALU.add), after=[sq1])
        r2 = sc.op("V", lambda e: e.tensor_scalar(
            out=red16[:], in0=red16[:], scalar1=1.0 / HD, scalar2=RMS_EPS,
            op0=ALU.mult, op1=ALU.add), after=[r1])
        r3 = sc.op("S", lambda e: e.activation(
            out=red16b[:], in_=red16[:], func=ACT.Sqrt), after=[r2])
        tok = sc.op("V", lambda e: e.reciprocal(
            out=red16b[:], in_=red16b[:]), after=[r3])
        for _ in range(2):
            n1 = sc.op("V", lambda e: e.tensor_tensor(
                out=red16c[:], in0=red16b[:], in1=red16b[:], op=ALU.mult),
                after=[tok])
            n2 = sc.op("V", lambda e: e.tensor_tensor(
                out=red16c[:], in0=red16c[:], in1=red16[:], op=ALU.mult),
                after=[n1])
            n3 = sc.op("V", lambda e: e.tensor_scalar(
                out=red16c[:], in0=red16c[:], scalar1=-0.5, scalar2=1.5,
                op0=ALU.mult, op1=ALU.add), after=[n2])
            tok = sc.op("V", lambda e: e.tensor_tensor(
                out=red16b[:], in0=red16b[:], in1=red16c[:], op=ALU.mult),
                after=[n3])
        gcom = sc.op("V", lambda e: e.tensor_tensor(
            out=red16b[:], in0=red16b[:], in1=gainrep[:], op=ALU.mult),
            after=[tok] + mod_ready)
        qh = sc.op("V", lambda e: e.tensor_tensor(
            out=tmq[:].rearrange("p (h d) -> p h d", d=HD),
            in0=ps_a[:, 0:1024].rearrange("p (h d) -> p h d", d=HD),
            in1=red16b[:].unsqueeze(2).to_broadcast((128, H, HD)),
            op=ALU.mult), after=[gcom, sq1])

        cosv = cos_all[:, 32 * t:32 * (t + 1)].unsqueeze(1).to_broadcast((128, H, 32))
        sinv = sin_all[:, 32 * t:32 * (t + 1)].unsqueeze(1).to_broadcast((128, H, 32))
        qv = tmq[:].rearrange("p (h d) -> p h d", d=HD)
        w2v = tmq2[:].rearrange("p (h d) -> p h d", d=HD)
        o1 = qrope[:].rearrange("p (h d) -> p h d", d=HD)[:, :, 0:32]
        o2 = qrope[:].rearrange("p (h d) -> p h d", d=HD)[:, :, 32:64]
        rp1 = sc.op("V", lambda e, cosv=cosv: e.tensor_tensor(
            out=w2v[:, :, 0:32], in0=qv[:, :, 0:32], in1=cosv, op=ALU.mult),
            after=[qh] + trig_ready)
        rp2 = sc.op("V", lambda e, sinv=sinv: e.tensor_tensor(
            out=w2v[:, :, 32:64], in0=qv[:, :, 32:64], in1=sinv, op=ALU.mult),
            after=[qh])
        rp3 = sc.op("V", lambda e: e.tensor_tensor(
            out=o1, in0=w2v[:, :, 0:32], in1=w2v[:, :, 32:64], op=ALU.add),
            after=[rp1, rp2])
        rp4 = sc.op("V", lambda e, cosv=cosv: e.tensor_tensor(
            out=w2v[:, :, 0:32], in0=qv[:, :, 32:64], in1=cosv, op=ALU.mult),
            after=[rp3])
        rp5 = sc.op("V", lambda e, sinv=sinv: e.tensor_tensor(
            out=w2v[:, :, 32:64], in0=qv[:, :, 0:32], in1=sinv, op=ALU.mult),
            after=[rp3])
        rp6 = sc.op("V", lambda e: e.tensor_tensor(
            out=o2, in0=w2v[:, :, 0:32], in1=w2v[:, :, 32:64],
            op=ALU.subtract), after=[rp4, rp5])
        q_done = [rp3, rp6]

        sk1 = sc.op("S", lambda e: e.activation(
            out=tmq[:, 0:256], in_=ps_a[:, 1024:1280], func=ACT.Square),
            after=qkv_done + [rp6])
        k1 = sc.op("V", lambda e: e.tensor_reduce(
            out=red16[:, 0:4],
            in_=tmq[:, 0:256].rearrange("p (h d) -> p h d", d=HD), axis=AX,
            op=ALU.add), after=[sk1])
        k2 = sc.op("V", lambda e: e.tensor_scalar(
            out=red16[:, 0:4], in0=red16[:, 0:4], scalar1=1.0 / HD,
            scalar2=RMS_EPS, op0=ALU.mult, op1=ALU.add), after=[k1])
        k3 = sc.op("S", lambda e: e.activation(
            out=red16b[:, 0:4], in_=red16[:, 0:4], func=ACT.Sqrt), after=[k2])
        tok = sc.op("V", lambda e: e.reciprocal(
            out=red16b[:, 0:4], in_=red16b[:, 0:4]), after=[k3])
        for _ in range(2):
            n1 = sc.op("V", lambda e: e.tensor_tensor(
                out=red16c[:, 0:4], in0=red16b[:, 0:4], in1=red16b[:, 0:4],
                op=ALU.mult), after=[tok])
            n2 = sc.op("V", lambda e: e.tensor_tensor(
                out=red16c[:, 0:4], in0=red16c[:, 0:4], in1=red16[:, 0:4],
                op=ALU.mult), after=[n1])
            n3 = sc.op("V", lambda e: e.tensor_scalar(
                out=red16c[:, 0:4], in0=red16c[:, 0:4], scalar1=-0.5,
                scalar2=1.5, op0=ALU.mult, op1=ALU.add), after=[n2])
            tok = sc.op("V", lambda e: e.tensor_tensor(
                out=red16b[:, 0:4], in0=red16b[:, 0:4], in1=red16c[:, 0:4],
                op=ALU.mult), after=[n3])
        khx = sc.op("V", lambda e: e.tensor_tensor(
            out=tmq[:, 0:256].rearrange("p (h d) -> p h d", d=HD),
            in0=ps_a[:, 1024:1280].rearrange("p (h d) -> p h d", d=HD),
            in1=red16b[:, 0:4].unsqueeze(2).to_broadcast((128, 4, HD)),
            op=ALU.mult), after=[tok, sk1])
        kv4 = tmq[:, 0:256].rearrange("p (h d) -> p h d", d=HD)
        kw4 = tmq2[:, 0:256].rearrange("p (h d) -> p h d", d=HD)
        cos4 = cos_all[:, 32 * t:32 * (t + 1)].unsqueeze(1).to_broadcast((128, 4, 32))
        sin4 = sin_all[:, 32 * t:32 * (t + 1)].unsqueeze(1).to_broadcast((128, 4, 32))
        kr2v = krope2[:].rearrange("p (g r d) -> p g r d", g=4, r=2)
        kp1 = sc.op("V", lambda e, cos4=cos4: e.tensor_tensor(
            out=kw4[:, :, 0:32], in0=kv4[:, :, 0:32], in1=cos4, op=ALU.mult),
            after=[khx])
        kp2 = sc.op("V", lambda e, sin4=sin4: e.tensor_tensor(
            out=kw4[:, :, 32:64], in0=kv4[:, :, 32:64], in1=sin4,
            op=ALU.mult), after=[khx])
        kp3 = sc.op("V", lambda e: e.tensor_tensor(
            out=kr2v[:, :, 0, 0:32], in0=kw4[:, :, 0:32],
            in1=kw4[:, :, 32:64], op=ALU.add), after=[kp1, kp2])
        kp4 = sc.op("V", lambda e, cos4=cos4: e.tensor_tensor(
            out=kw4[:, :, 0:32], in0=kv4[:, :, 32:64], in1=cos4, op=ALU.mult),
            after=[kp3])
        kp5 = sc.op("V", lambda e, sin4=sin4: e.tensor_tensor(
            out=kw4[:, :, 32:64], in0=kv4[:, :, 0:32], in1=sin4,
            op=ALU.mult), after=[kp3])
        kp6 = sc.op("V", lambda e: e.tensor_tensor(
            out=kr2v[:, :, 0, 32:64], in0=kw4[:, :, 0:32],
            in1=kw4[:, :, 32:64], op=ALU.subtract), after=[kp4, kp5])
        kp7 = sc.op("V", lambda e: e.tensor_copy(
            out=kr2v[:, :, 1, :], in_=kr2v[:, :, 0, :]), after=[kp3, kp6])
        vev = sc.op("V", lambda e, t=t: e.tensor_copy(
            out=vtr[:, 256 * (t % 4):256 * (t % 4 + 1)],
            in_=ps_b[:, 512:768]), after=qkv_done + [khx])

        trq = []
        for j in range(C8):
            trq.append(sc.op("T", lambda e, j=j: e.transpose(
                ps_t[:, 128 * j:128 * (j + 1)], qrope[:, 128 * j:128 * (j + 1)],
                ident[:]), after=q_done + [prev_ps_t_reader]))
        qT_e = sc.op("V", lambda e: e.tensor_copy(out=qT[:], in_=ps_t[:]),
                     after=trq)
        trk = []
        for g in range(4):
            trk.append(sc.op("T", lambda e, g=g: e.transpose(
                ps_t[:, 128 * g:128 * (g + 1)],
                krope2[:, 128 * g:128 * (g + 1)], ident[:]),
                after=[kp7, qT_e]))
        kT_e = sc.op("V", lambda e, t=t: e.tensor_copy(
            out=kTr[:].rearrange("p (g r n) -> p g r n", g=KVH, r=4)[:, :, t % 4, :],
            in_=ps_t[:, 0:512].rearrange("p (g n) -> p g n", g=4)), after=trk)

        yready = []
        PT_prev = kT_e
        y_prev = None
        for h in range(H):
            g = h // 4
            qoff = 64 * (h % 2)
            qc = h // 2
            mm_s = []
            for kt2 in range(nkt):
                slot = (lo + kt2) % 4
                mm_s.append(sc.op("T", lambda e, kt2=kt2, slot=slot, g=g,
                                  qoff=qoff, qc=qc: e.matmul(
                    ps_b[:, 128 * kt2:128 * (kt2 + 1)],
                    cview(qT, qc, 128)[qoff:qoff + 64, :],
                    kTr[:].rearrange("p (g r n) -> p g r n", g=KVH, r=4)
                    [qoff:qoff + 64, g, slot, :],
                    start=True, stop=True), after=[qT_e, kT_e]))
            msk = sc.op("V", lambda e, W=W, moff=moff: e.tensor_tensor(
                out=msS[:, 0:W], in0=maskb[:, moff:moff + W],
                in1=ps_b[:, 0:W], op=ALU.add), after=mm_s + [ld_mask])
            mx = sc.op("V", lambda e, W=W: e.tensor_reduce(
                out=mrow[:], in_=msS[:, 0:W], axis=AX, op=ALU.max,
                negate=True), after=[msk])
            expo = sc.op("S", lambda e, W=W: e.activation(
                out=Pf[:, 0:W], in_=msS[:, 0:W], func=ACT.Exp, bias=mrow[:],
                scale=1.0, accum_out=rsum[:]), after=[mx])
            rr = sc.op("V", lambda e: e.reciprocal(out=rsum[:], in_=rsum[:]),
                       after=[expo])
            pn = sc.op("S", lambda e, W=W: e.activation(
                out=Pb[:, 0:W], in_=Pf[:, 0:W], func=ACT.Copy,
                scale=rsum[:, 0:1]), after=[rr])
            trp = []
            for kt2 in range(nkt):
                trp.append(sc.op("T", lambda e, kt2=kt2: e.transpose(
                    ps_t[:, 128 * kt2:128 * (kt2 + 1)],
                    Pb[:, 128 * kt2:128 * (kt2 + 1)], ident[:]),
                    after=[pn, PT_prev]))
            PT_e = sc.op("V", lambda e, W=W: e.tensor_copy(
                out=PTs[:, 0:W], in_=ps_t[:, 0:W]), after=trp)
            PT_prev = PT_e
            mm_y = []
            po = 64 * (h % 2)
            for kt2 in range(nkt):
                slot = (lo + kt2) % 4
                kwargs = {"tile_position": (0, 64)} if po else {}
                mm_y.append(sc.op("T", lambda e, kt2=kt2, slot=slot, g=g,
                                  po=po, kw=kwargs, nkt=nkt: e.matmul(
                    ps_b[po:po + 64, 1024:1152],
                    vtr[:, 256 * slot + 64 * g:256 * slot + 64 * (g + 1)],
                    PTs[:, 128 * kt2:128 * (kt2 + 1)],
                    start=(kt2 == 0), stop=(kt2 == nkt - 1), **kw),
                    after=[PT_e, vev] + ([y_prev] if y_prev else [])))
            yev = sc.op("V", lambda e, qc=qc, po=po: e.tensor_copy(
                out=cview(yT, qc, 128)[po:po + 64, :],
                in_=ps_b[po:po + 64, 1024:1152]), after=mm_y)
            y_prev = yev
            yready.append(yev)

        mwo = []
        for j in range(C8):
            st = (j == 0)
            sp = (j == C8 - 1)
            mwo.append(sc.op("T", lambda e, j=j, st=st, sp=sp: e.matmul(
                ps_a[:, 0:512], cview(yT, j, 128), cview(wob, j, D)[:, 0:512],
                start=st, stop=sp), after=yready))
            mwo.append(sc.op("T", lambda e, j=j, st=st, sp=sp: e.matmul(
                ps_a[:, 512:1024], cview(yT, j, 128),
                cview(wob, j, D)[:, 512:1024], start=st, stop=sp),
                after=yready))
        att_done = mwo[-2:]
        rs1 = sc.op("V", lambda e: e.tensor_tensor(
            out=stg[:, 0:D], in0=ps_a[:, 0:1024], in1=gab[:], op=ALU.mult),
            after=att_done + mod_ready)
        rs2 = sc.op("V", lambda e, t=t: e.tensor_tensor(
            out=x_t(t), in0=x_t(t), in1=stg[:, 0:D], op=ALU.add),
            after=[rs1])
        xst = sc.op("DS", lambda e, t=t: e.dma_start(
            out=xprime_d[128 * t:128 * (t + 1), :], in_=x_t(t)), dma=True,
            after=[rs2])
        xp_stores.append(xst)

        xn2 = emit_ln(x_t(t), h2f[:], 4, [rs2])
        hm1 = sc.op("V", lambda e: e.tensor_tensor(
            out=h2f[:], in0=h2f[:], in1=cb1b[:], op=ALU.mult),
            after=[xn2] + mod_ready)
        hm2 = sc.op("V", lambda e: e.tensor_tensor(
            out=h2f[:], in0=h2f[:], in1=sbb[:], op=ALU.add), after=[hm1])
        dq, q_sg = emit_quant(h2f[:], xn[:], D, [hm2], tmq[:], tmq2[:],
                              qb0[:], qb1[:], sgq[:], tmq[:], m64[:], s64[:],
                              r64[:])
        trh = []
        for j in range(C8):
            trh.append(sc.op("T", lambda e, j=j: e.transpose(
                ps_t[:, 128 * j:128 * (j + 1)], xn[:, 128 * j:128 * (j + 1)],
                ident[:]), after=[dq, PT_prev] + mwo))
        hqT_e = sc.op("V", lambda e, t=t: e.tensor_copy(
            out=hqTb[:].rearrange("p (j s) -> p j s", j=C8)[:, :, 128 * t:128 * (t + 1)],
            in_=ps_t[:].rearrange("p (j n) -> p j n", j=C8)), after=trh)
        hq_done.append(hqT_e)
        prev_ps_t_reader = hqT_e
        tile_end[t] = (hqT_e, q_sg, xst)

        if _DEBUG and t == 0:
            _tap_prev = [hqT_e]

            def tap(name, src_ap, deps, cast=True):
                wdt = 1
                for s_ in src_ap.shape[1:]:
                    wdt *= int(s_)
                if cast:
                    cpy = sc.op("V", lambda e, src_ap=src_ap, wdt=wdt: e.tensor_copy(
                        out=stg[:, 0:wdt], in_=src_ap),
                        after=deps + _tap_prev)
                    st_ = sc.op("DS", lambda e, name=name, wdt=wdt: e.dma_start(
                        out=dbg[name], in_=stg[:, 0:wdt]),
                        dma=True, after=[cpy])
                else:
                    st_ = sc.op("DS", lambda e, name=name, src_ap=src_ap: e.dma_start(
                        out=dbg[name], in_=src_ap), dma=True,
                        after=deps + _tap_prev)
                _tap_prev[:] = [st_]
            tap("dbg_h1T", h1T[:], h1toks)
            tap("dbg_q", qrope[:], q_done)
            tap("dbg_k",
                krope2[:].rearrange("p (g r d) -> p g r d", g=4, r=2)[:, :, 0, :],
                [kp7])
            tap("dbg_y", yT[:], yready)
            tap("dbg_hq", xn[:], [dq])
            tap("dbg_xp", x_t(t), [rs2, xst], cast=False)

    attn_exit = list(hq_done) + xp_stores + ([qT_e, kT_e] if NT else [])
    if not attn_exit:
        attn_exit = first_gate
    attn.close()
    outer.close()

    ffn = contextlib.ExitStack()

    def sbf(name, shape, dt=F32):
        name = "sbf_" + name
        return ffn.enter_context(nc.sbuf_tensor(name, shape, dt))

    wgT = sbf("wgT", [128, C8 * DI], BF16)
    wuT = sbf("wuT", [128, C8 * DI], BF16)
    wdT = sbf("wdT", [128, C22 * D], BF16)
    fstg = sbf("fstg", [128, 1024])
    wqnt = sbf("wqnt", [128, 1024], BF16)
    favq = sbf("favq", [128, 1024])
    faq = sbf("faq", [128, 1024])
    fb0 = sbf("fb0", [128, 1024], BF16)
    fb1 = sbf("fb1", [128, 1024], BF16)
    fsg = sbf("fsg", [128, 1024], BF16)
    fh3 = sbf("fh3", [128, 512])
    fm = sbf("fm", [128, 64])
    fs = sbf("fs", [128, 64])
    fr = sbf("fr", [128, 64])
    xpt = sbf("xpt", [128, D])
    hidqT = sbf("hidqT", [128, C22 * 128], BF16)
    fout = sbf("fout", [128, D])

    def wgT_v(j):
        return wgT[:].rearrange("p (j n) -> p j n", j=C8)[:, j, :]

    def wuT_v(j):
        return wuT[:].rearrange("p (j n) -> p j n", j=C8)[:, j, :]

    def wdT_v(r):
        return wdT[:].rearrange("p (r n) -> p r n", r=C22)[:, r, :]

    def hidqT_v(r):
        return hidqT[:].rearrange("p (r n) -> p r n", r=C22)[:, r, :]

    def fquant(src_ap, dst_bf, width, dep):
        return emit_quant(src_ap, dst_bf, width, dep, favq[:, 0:width],
                          faq[:, 0:width], fb0[:, 0:width], fb1[:, 0:width],
                          fsg[:, 0:width], favq[:, 0:width],
                          fm[:, 0:width // 16], fs[:, 0:width // 16],
                          fr[:, 0:width // 16])

    wg_ready = []
    prevw = None
    prev_ps_t_f = hq_done[-1] if hq_done else ld_ident

    def prep_w(src_d, nrow, dstT, jdim, col_splits, deps0):
        nonlocal prevw, prev_ps_t_f
        toks = []
        for r in range(nrow):
            for (c0, c1, jg0) in col_splits:
                wd_ = c1 - c0
                l = sc.op("DL", lambda e, r=r, c0=c0, c1=c1: e.dma_start(
                    out=fstg[:, 0:c1 - c0] if c1 - c0 <= 1024 else None,
                    in_=src_d[128 * r:128 * (r + 1), c0:c1]), dma=True,
                    after=[prevw] if prevw else deps0)
                qd, _sg = fquant(fstg[:, 0:wd_], wqnt[:, 0:wd_], wd_, [l])
                nblk = wd_ // 128
                sub = []
                for bi in range(nblk):
                    sub.append(sc.op("T", lambda e, bi=bi: e.transpose(
                        ps_t[:, 128 * bi:128 * (bi + 1)],
                        wqnt[:, 128 * bi:128 * (bi + 1)], ident[:]),
                        after=[qd, prev_ps_t_f]))
                ev = sc.op("V", lambda e, r=r, jg0=jg0, nblk=nblk, dstT=dstT,
                           jdim=jdim: e.tensor_copy(
                    out=dstT[:].rearrange("p (j n) -> p j n", j=jdim)
                    [:, jg0:jg0 + nblk, 128 * r:128 * (r + 1)],
                    in_=ps_t[:, 0:128 * nblk].rearrange(
                        "p (b n) -> p b n", b=nblk)), after=sub)
                prevw = ev
                prev_ps_t_f = ev
                toks.append(ev)
        return toks

    if phase == "full":
        wg_ready += prep_w(wg_d, C22, wgT, C8, [(0, 1024, 0)], attn_exit)
        wg_ready += prep_w(wu_d, C22, wuT, C8, [(0, 1024, 0)], [])
        wg_ready += prep_w(wd_d, C8, wdT, C22,
                           [(0, 1024, 0), (1024, 2048, 8), (2048, 2816, 16)], [])

    out_stores = []
    prevf = []
    prev_xl = None
    for t in range(T if phase == "full" else 0):
        _CACHE["marks"].append(("ffn", t, sc.gidx))
        gate_xl = [xp_stores[t]] + prevf
        xl = sc.op("DL", lambda e, t=t: e.dma_start(
            out=xpt[:], in_=xprime_d[128 * t:128 * (t + 1), :]), dma=True,
            after=gate_xl + ([prev_xl] if prev_xl else [wg_ready[-1]]))
        prev_xl = xl
        hid_ready = []
        prev_h3 = None
        for half in range(2):
            mmg = []
            for j in range(C8):
                st = (j == 0)
                sp = (j == C8 - 1)
                for (c0, c1) in ((0, 512), (512, 1024), (1024, 1408)):
                    off = 1408 * half
                    mmg.append(sc.op("T", lambda e, j=j, st=st, sp=sp, c0=c0,
                                     c1=c1, off=off, t=t: e.matmul(
                        ps_a[:, c0:c1], hqT_v(j, t),
                        wgT_v(j)[:, off + c0:off + c1], start=st, stop=sp),
                        after=[hq_done[t], wg_ready[-1]] + prevf))
                    mmg.append(sc.op("T", lambda e, j=j, st=st, sp=sp, c0=c0,
                                     c1=c1, off=off, t=t: e.matmul(
                        ps_b[:, c0:c1], hqT_v(j, t),
                        wuT_v(j)[:, off + c0:off + c1], start=st, stop=sp),
                        after=[hq_done[t]]))
            gu_done = mmg[-2:]
            for sl_i, (c0, c1) in enumerate(((0, 512), (512, 1024),
                                             (1024, 1408))):
                wd_ = c1 - c0
                slv = sc.op("S", lambda e, c0=c0, c1=c1: e.activation(
                    out=fstg[:, 0:c1 - c0], in_=ps_a[:, c0:c1], func=ACT.Silu),
                    after=gu_done + ([prev_h3] if prev_h3 else []))
                h3v = sc.op("V", lambda e, c0=c0, c1=c1, wd_=wd_: e.tensor_tensor(
                    out=fh3[:, 0:wd_], in0=fstg[:, 0:wd_], in1=ps_b[:, c0:c1],
                    op=ALU.mult), after=[slv])
                prev_h3 = h3v
                hqd, _sg2 = fquant(fh3[:, 0:wd_], wqnt[:, 0:wd_], wd_, [h3v])
                nblk = wd_ // 128
                sub = []
                for bi in range(nblk):
                    sub.append(sc.op("T", lambda e, bi=bi: e.transpose(
                        ps_t[:, 128 * bi:128 * (bi + 1)],
                        wqnt[:, 128 * bi:128 * (bi + 1)], ident[:]),
                        after=[hqd, prev_ps_t_f]))
                base_r = half * 11 + (0, 4, 8)[sl_i]
                hv = sc.op("V", lambda e, base_r=base_r, nblk=nblk: e.tensor_copy(
                    out=hidqT[:].rearrange("p (r n) -> p r n", r=C22)
                    [:, base_r:base_r + nblk, :],
                    in_=ps_t[:, 0:128 * nblk].rearrange("p (b n) -> p b n",
                                                        b=nblk)), after=sub)
                prev_ps_t_f = hv
                hid_ready.append(hv)
        mmd = []
        for r in range(C22):
            st = (r == 0)
            sp = (r == C22 - 1)
            mmd.append(sc.op("T", lambda e, r=r, st=st, sp=sp: e.matmul(
                ps_a[:, 0:512], hidqT_v(r), wdT_v(r)[:, 0:512], start=st,
                stop=sp), after=hid_ready))
            mmd.append(sc.op("T", lambda e, r=r, st=st, sp=sp: e.matmul(
                ps_a[:, 512:1024], hidqT_v(r), wdT_v(r)[:, 512:1024],
                start=st, stop=sp), after=hid_ready))
        dn_done = mmd[-2:]
        f1 = sc.op("V", lambda e: e.tensor_tensor(
            out=fout[:], in0=ps_a[:, 0:1024], in1=gbb[:], op=ALU.mult),
            after=dn_done + bc_gb)
        f2 = sc.op("V", lambda e: e.tensor_tensor(
            out=fout[:], in0=fout[:], in1=xpt[:], op=ALU.add),
            after=[f1, xl])
        ost = sc.op("DS", lambda e, t=t: e.dma_start(
            out=out_d[128 * t:128 * (t + 1), :], in_=fout[:]), dma=True,
            after=[f2])
        out_stores.append(ost)
        prevf = [f2]

    ffn.close()

    if KCUT:
        sc.cut(KCUT)
    sc.finalize()
    sems = {}
    for e in ENGS:
        sems[e] = stack.enter_context(nc.semaphore("sem_" + e))
    n_ds = sum(16 for r in sc.ops["DS"] if r["inc"] and r["dma"])
    n_dl = sum(16 for r in sc.ops["DL"] if r["inc"] and r["dma"])
    with nc.Block() as block:
        @block.tensor
        def _(eng):
            sc.replay(eng, "T", sems)

        @block.vector
        def _(eng):
            sc.replay(eng, "V", sems)

        @block.scalar
        def _(eng):
            sc.replay(eng, "S", sems)

        @block.gpsimd
        def _(eng):
            sc.replay(eng, "G", sems)

        @block.sync
        def _(eng):
            sc.replay_merged(eng, ["DL", "DS"], sems)
            if n_dl:
                eng.wait_ge(sems["DL"], n_dl)
            if n_ds:
                eng.wait_ge(sems["DS"], n_ds)

    stack.close()
    _CACHE[key] = nc
    return nc


def kernel(L0_attn_x, L0_attn_Wq, L0_attn_Wk, L0_attn_Wv, L0_attn_Wo,
           L0_attn_q_gain, L0_attn_pair_mix, L0_ffn_w_gate, L0_ffn_w_up,
           L0_ffn_w_down, L0_c, L0_wmod):
    f = np.float32
    x = np.ascontiguousarray(np.asarray(L0_attn_x, f))
    wqT = np.ascontiguousarray(np.asarray(L0_attn_Wq, f).T)
    wkT = np.ascontiguousarray(np.asarray(L0_attn_Wk, f).T)
    wvT = np.ascontiguousarray(np.asarray(L0_attn_Wv, f).T)
    woT = np.ascontiguousarray(np.asarray(L0_attn_Wo, f).T)
    perm = np.arange(D).reshape(C8, 2, 64)[:, ::-1, :].reshape(D)
    woTsw = np.ascontiguousarray(woT[perm])
    wmod = np.ascontiguousarray(np.asarray(L0_wmod, f))
    wg = np.ascontiguousarray(np.asarray(L0_ffn_w_gate, f))
    wu = np.ascontiguousarray(np.asarray(L0_ffn_w_up, f))
    wd = np.ascontiguousarray(np.asarray(L0_ffn_w_down, f))
    cfull = np.asarray(L0_c, f)
    gain = np.ascontiguousarray(np.asarray(L0_attn_q_gain, f).reshape(1, H))
    pm = np.ascontiguousarray(np.asarray(L0_attn_pair_mix, f).reshape(1, 32))
    maskb = _mask_host()
    ctbl, stbl = _rope_tables_host()
    ident = np.eye(128, dtype=ml_dtypes.bfloat16)

    nc = build()
    shared = {"wqT": wqT, "wkT": wkT, "wvT": wvT, "woT": woT,
              "woTsw": woTsw, "wmod": wmod, "wg": wg, "wu": wu, "wd": wd,
              "gain": gain, "pm": pm, "maskb": maskb, "ident": ident,
              "ctbl": ctbl, "stbl": stbl}
    in_maps = []
    for b in range(NCORE):
        m = dict(shared)
        m["x"] = np.ascontiguousarray(x[b])
        m["crow"] = np.ascontiguousarray(cfull[b:b + 1])
        in_maps.append(m)
    res = run_bass_kernel_spmd(nc, in_maps, core_ids=list(range(NCORE)))
    out = np.stack([np.asarray(r["out"]) for r in res.results], axis=0)
    return out.astype(np.float32)


# version 13
# speedup vs baseline: 1.0973x; 1.0729x over previous
import math
import numpy as np
import ml_dtypes
import concourse.bass as bass
import concourse.mybir as mybir
from concourse.bass_utils import run_bass_kernel_spmd

F32 = mybir.dt.float32
BF16 = mybir.dt.bfloat16
I32 = mybir.dt.int32
ALU = mybir.AluOpType
ACT = mybir.ActivationFunctionType
AX = mybir.AxisListType.X

S, D, H, KVH, HD = 2048, 1024, 16, 4, 64
T = S // 128
C8 = D // 128
DI = 2816
C22 = DI // 128
NCORE = 8
LN_EPS = 1e-5
RMS_EPS = 1e-6
ROPE_BASE = 10000.0
NEG = -1.0e30
PI = float(np.float32(math.pi))
TWO_PI = 2.0 * math.pi
PI_CLAMP = 3.1415925
TP_HI = float(np.float32(TWO_PI))
TP_LO = TWO_PI - TP_HI
_ES = -math.log(ROPE_BASE) / 32.0
ES_HI = float(np.float32(_ES))
ES_LO = _ES - ES_HI
_C6 = 1.0 / 6.0
C6_HI = float(np.float32(_C6))
C6_LO = _C6 - C6_HI
_C12 = 1.0 / 12.0
C12_HI = float(np.float32(_C12))
C12_LO = _C12 - C12_HI
CLAMP_HALF = float(np.float32(1e-12)) / 2.0
_PI2 = math.pi / 2.0
PI2_HI = float(np.float32(_PI2))
PI2_LO = _PI2 - PI2_HI

_DEBUG = False

ENGS = ("T", "V", "S", "G", "DL", "DS", "DP", "DW",
        "DM0", "DM1", "DM2", "DM3", "DF0", "DF1")
DMA_CLASSES = ("DL", "DS", "DP", "DW", "DM0", "DM1", "DM2", "DM3",
               "DF0", "DF1")


class Script:
    def __init__(self):
        self.ops = {e: [] for e in ENGS}
        self.gidx = 0

    def op(self, eng, fn, after=(), dma=False):
        deps = []
        for t in after:
            if t is None:
                continue
            if t["eng"] == eng and not t["dma"]:
                continue
            t["inc"] = True
            deps.append(t)
        rec = {"eng": eng, "fn": fn, "after": deps, "dma": dma,
               "inc": dma, "val": None, "gidx": self.gidx}
        self.gidx += 1
        self.ops[eng].append(rec)
        return rec

    def cut(self, maxg):
        kept = set()
        for e in ENGS:
            self.ops[e] = [r for r in self.ops[e] if r["gidx"] < maxg]
            for r in self.ops[e]:
                kept.add(id(r))
        for e in ENGS:
            for r in self.ops[e]:
                r["after"] = [t for t in r["after"] if id(t) in kept]

    def finalize(self):
        for e in ("V", "S", "G"):
            for rec in self.ops[e]:
                rec["inc"] = True
        for e in ENGS:
            run = 0
            for rec in self.ops[e]:
                if rec["inc"]:
                    run += 16 if rec["dma"] else 1
                rec["val"] = run
        tot = self.ops["DP"][-1]["val"] if self.ops["DP"] else 0
        for rec in self.ops["DP"]:
            rec["val"] = tot

    def replay_merged(self, eng, enames, sems):
        ops = sorted([r for e in enames for r in self.ops[e]],
                     key=lambda r: r["gidx"])
        self._replay_list(eng, ops, sems)

    def replay(self, eng, ename, sems):
        self._replay_list(eng, self.ops[ename], sems)

    def _replay_list(self, eng, ops, sems):
        selfserial = ops and ops[0]["eng"] in ("V", "S", "G")
        nprev = 0
        for rec in ops:
            seen = {}
            for t in rec["after"]:
                seen[t["eng"]] = max(seen.get(t["eng"], 0), t["val"])
            if selfserial and nprev > 0:
                seen[rec["eng"]] = max(seen.get(rec["eng"], 0), nprev)
            for key, v in seen.items():
                if v > 0:
                    eng.wait_ge(sems[key], v)
            instr = rec["fn"](eng)
            if rec["inc"]:
                instr.then_inc(sems[rec["eng"]], 16 if rec["dma"] else 1)
            if selfserial:
                nprev = rec["val"]


_CACHE = {}


def _mask_host():
    i = np.arange(128)[:, None]
    c = np.arange(384)[None, :]
    valid = (c > i) & (c <= i + 256)
    return np.where(valid, 0.0, NEG).astype(np.float32)


def _rope_tables_host():
    import jax
    import jax.numpy as jnp
    with jax.default_device(jax.devices("cpu")[0]):
        inv_freq = 1.0 / ROPE_BASE ** (jnp.arange(0, 64, 2,
                                                  dtype=jnp.float32) / 64)
        t = jnp.arange(S, dtype=jnp.float32)
        freqs = jnp.outer(t, inv_freq)
        cosj = np.asarray(jnp.cos(freqs))
        sinj = np.asarray(jnp.sin(freqs))
    ctbl = np.ascontiguousarray(
        cosj.reshape(T, 128, 32).transpose(1, 0, 2).reshape(128, T * 32))
    stbl = np.ascontiguousarray(
        sinj.reshape(T, 128, 32).transpose(1, 0, 2).reshape(128, T * 32))
    return ctbl, stbl


def build(phase="full"):
    import os as _os
    KCUT = int(_os.environ.get("KCUT", "0"))
    key = (phase, KCUT)
    if key in _CACHE:
        return _CACHE[key]
    nc = bass.Bass("TRN2", target_bir_lowering=False,
                   detect_race_conditions=False)
    sc = Script()

    def din(name, shape, dt=F32):
        return nc.dram_tensor(name, shape, dt, kind="ExternalInput").ap()

    x_d = din("x", [S, D])
    wqT_d = din("wqT", [D, D])
    wkT_d = din("wkT", [D, KVH * HD])
    wvT_d = din("wvT", [D, KVH * HD])
    woT_d = din("woT", [D, D])
    woTsw_d = din("woTsw", [D, D])
    wmod_d = din("wmod", [D, 6 * D])
    wg_d = din("wg", [DI, D])
    wu_d = din("wu", [DI, D])
    wd_d = din("wd", [D, DI])
    crow_d = din("crow", [1, D])
    gain_d = din("gain", [1, H])
    pm_d = din("pm", [1, 32])
    maskb_d = din("maskb", [128, 384])
    ctbl_d = din("ctbl", [128, T * 32])
    stbl_d = din("stbl", [128, T * 32])
    ident_d = din("ident", [128, 128], BF16)
    out_d = nc.dram_tensor("out", [S, D], F32, kind="ExternalOutput").ap()
    xprime_d = nc.dram_tensor("xprime", [S, D], F32).ap()
    dbg = {}
    if _DEBUG:
        for nm, shp in [("dbg_mod", [1, 6 * D]), ("dbg_h1T", [128, C8 * 128]),
                        ("dbg_q", [128, D]), ("dbg_k", [128, 256]),
                        ("dbg_y", [128, C8 * 128]), ("dbg_hq", [128, D]),
                        ("dbg_xp", [128, D])]:
            dbg[nm] = nc.dram_tensor(nm, shp, F32, kind="ExternalOutput").ap()

    import contextlib
    stack = contextlib.ExitStack()

    ident = stack.enter_context(nc.sbuf_tensor("ident_s", [128, 128], BF16))
    gbb = stack.enter_context(nc.sbuf_tensor("gbb", [128, D], F32))
    hqTb = stack.enter_context(nc.sbuf_tensor("hqTb", [128, C8 * S], BF16))

    ps_a = stack.enter_context(nc.psum_tensor("ps_a", [128, 1536], F32))
    ps_b = stack.enter_context(nc.psum_tensor("ps_b", [128, 1536], F32))
    ps_t = stack.enter_context(nc.psum_tensor("ps_t", [128, 1024], BF16))
    ps_c = stack.enter_context(nc.psum_tensor("ps_c", [128, 512], F32))

    ld_ident = sc.op("DP", lambda e: e.dma_start(out=ident[:], in_=ident_d),
                     dma=True)

    outer = contextlib.ExitStack()

    def sbo(name, shape, dt=F32):
        name = "sbo_" + name
        return outer.enter_context(nc.sbuf_tensor(name, shape, dt))

    maskb = sbo("maskb", [128, 384])
    sab = sbo("sab", [128, D])
    ca1b = sbo("ca1b", [128, D])
    gab = sbo("gab", [128, D])
    cb1b = sbo("cb1b", [128, D])
    sbb = sbo("sbb", [128, D])
    gainrep = sbo("gainrep", [128, H])
    pmA = sbo("pmA", [128, C8])
    pmB = sbo("pmB", [128, C8])
    cos_all = sbo("cos_all", [128, T * 32])
    sin_all = sbo("sin_all", [128, T * 32])
    x_ring = sbo("x_ring", [128, 3 * D])

    def x_t(t):
        return x_ring[:, D * (t % 3):D * (t % 3 + 1)]

    ld_mask = sc.op("DP", lambda e: e.dma_start(out=maskb[:], in_=maskb_d),
                    dma=True)
    ld_x = []
    for t in range(3):
        ld_x.append(sc.op("DP", lambda e, t=t: e.dma_start(
            out=x_t(t), in_=x_d[128 * t:128 * (t + 1), :]), dma=True))
    trig_cos = sc.op("DP", lambda e: e.dma_start(out=cos_all[:], in_=ctbl_d),
                     dma=True)
    trig_sin = sc.op("DP", lambda e: e.dma_start(out=sin_all[:], in_=stbl_d),
                     dma=True)

    setup = contextlib.ExitStack()

    def sb0(name, shape, dt=F32):
        name = "sb0_" + name
        return setup.enter_context(nc.sbuf_tensor(name, shape, dt))

    modrow = sb0("modrow", [1, 6 * D])
    c_col = sb0("c_col", [128, C8])
    silu_cb = sb0("silu_cb", [128, C8], BF16)
    ones_f = sb0("ones_f", [1, 128])
    stage = sb0("stage", [128, 4 * 512])
    stage_b = sb0("stage_b", [128, 4 * 512], BF16)
    garow = sb0("garow", [1, 2 * D])
    carow = sb0("carow", [1, 2 * D])
    tmp1 = sb0("tmp1", [1, 2 * D])
    tmp2 = sb0("tmp2", [1, 2 * D])
    grow = sb0("grow", [1, H])
    pmrow = sb0("pmrow", [1, 32])

    ld_c = sc.op("DP", lambda e: e.dma_start(
        out=c_col[:], in_=crow_d.rearrange("o (j p) -> (o p) j", p=128),
        allow_slow_non_contiguous=True), dma=True)
    ld_g = sc.op("DP", lambda e: e.dma_start(out=grow[:], in_=gain_d),
                 dma=True)
    ld_pm = sc.op("DP", lambda e: e.dma_start(out=pmrow[:], in_=pm_d),
                  dma=True)
    ones1 = sc.op("V", lambda e: e.memset(ones_f[:], 1.0))

    sl = sc.op("S", lambda e: e.activation(out=c_col[:], in_=c_col[:],
                                           func=ACT.Silu), after=[ld_c])
    slb = sc.op("V", lambda e: e.tensor_copy(out=silu_cb[:], in_=c_col[:]),
                after=[sl])

    ev_mod = []
    mm_hist = {}
    cst_hist = {}
    for j in range(12):
        mm_prev = None
        psum_j = ps_c[0:1, 0:512] if (j % 2 == 0) else ps_b[0:1, 0:512]
        for k in range(C8):
            gstep = j * C8 + k
            sl = gstep % 4
            ldst = sc.op("DM%d" % sl, lambda e, k=k, j=j, sl=sl: e.dma_start(
                out=stage[:, 512 * sl:512 * (sl + 1)],
                in_=wmod_d[128 * k:128 * (k + 1), 512 * j:512 * (j + 1)]),
                dma=True,
                after=[cst_hist[gstep - 4]] if gstep >= 4 else [])
            cst = sc.op("V", lambda e, sl=sl: e.tensor_copy(
                out=stage_b[:, 512 * sl:512 * (sl + 1)],
                in_=stage[:, 512 * sl:512 * (sl + 1)]),
                after=[ldst] + ([mm_hist[gstep - 4]] if gstep >= 4 else []))
            cst_hist[gstep] = cst
            mmdep = [cst, slb]
            if mm_prev is not None:
                mmdep.append(mm_prev)
            if k == 0 and j >= 2:
                mmdep.append(ev_mod[j - 2])
            mm_prev = sc.op("T", lambda e, k=k, sl=sl, psum_j=psum_j:
                            e.matmul(
                psum_j, silu_cb[:, k:k + 1],
                stage_b[:, 512 * sl:512 * (sl + 1)],
                start=(k == 0), stop=(k == C8 - 1)), after=mmdep)
            mm_hist[gstep] = mm_prev
        ev = sc.op("S", lambda e, j=j, psum_j=psum_j: e.copy(
            out=modrow[0:1, 512 * j:512 * (j + 1)], in_=psum_j),
            after=[mm_prev])
        ev_mod.append(ev)
    mA = list(ev_mod)

    g1 = sc.op("V", lambda e: e.tensor_copy(
        out=tmp1[0:1, 0:D], in_=modrow[0:1, 2 * D:3 * D]), after=mA)
    g2 = sc.op("V", lambda e: e.tensor_copy(
        out=tmp1[0:1, D:2 * D], in_=modrow[0:1, 5 * D:6 * D]), after=mA)
    sg1 = sc.op("V", lambda e: e.tensor_scalar(
        out=tmp2[:], in0=tmp1[:], scalar1=0.0, scalar2=None, op0=ALU.is_ge),
        after=[g1, g2])
    sg2 = sc.op("V", lambda e: e.tensor_scalar(
        out=tmp2[:], in0=tmp2[:], scalar1=2.0, scalar2=-1.0, op0=ALU.mult,
        op1=ALU.add), after=[sg1])
    mg1 = sc.op("S", lambda e: e.activation(out=tmp1[:], in_=tmp1[:],
                                            func=ACT.Abs),
                after=[g1, g2, sg1])
    mg2 = sc.op("V", lambda e: e.tensor_scalar(
        out=tmp1[:], in0=tmp1[:], scalar1=0.25, scalar2=None, op0=ALU.max),
        after=[mg1])
    gprod = sc.op("V", lambda e: e.tensor_tensor(
        out=garow[:], in0=tmp2[:], in1=tmp1[:], op=ALU.mult),
        after=[mg2, sg2])
    cc1 = sc.op("V", lambda e: e.tensor_copy(
        out=carow[0:1, 0:D], in_=modrow[0:1, D:2 * D]), after=mA)
    cc2 = sc.op("V", lambda e: e.tensor_copy(
        out=carow[0:1, D:2 * D], in_=modrow[0:1, 4 * D:5 * D]), after=mA)
    ca_p1 = sc.op("V", lambda e: e.tensor_scalar(
        out=carow[:], in0=carow[:], scalar1=1.0, scalar2=None, op0=ALU.add),
        after=[cc1, cc2])



    dbg_mod_tok = None
    if _DEBUG:
        dbg_mod_tok = sc.op("DS", lambda e: e.dma_start(
            out=dbg["dbg_mod"], in_=modrow[:]), dma=True, after=mA)

    psc_last = mA[-1]
    mod_ready = []

    def bcast(dst, row_ap, dep):
        nonlocal psc_last
        toks = []
        for hh in range(2):
            mm = sc.op("T", lambda e, hh=hh: e.matmul(
                ps_c[:, 0:512], ones_f[:], row_ap[0:1, 512 * hh:512 * (hh + 1)],
                start=True, stop=True), after=dep + [ones1, psc_last])
            cp = sc.op("V", lambda e, hh=hh: e.tensor_copy(
                out=dst[:, 512 * hh:512 * (hh + 1)], in_=ps_c[:, 0:512]),
                after=[mm])
            psc_last = cp
            toks.append(cp)
        return toks

    bc_gb = bcast(gbb, garow[0:1, D:2 * D], [gprod])
    mod_ready += bc_gb
    mod_ready += bcast(gab, garow[0:1, 0:D], [gprod])
    mod_ready += bcast(cb1b, carow[0:1, D:2 * D], [ca_p1])
    mod_ready += bcast(sbb, modrow[0:1, 3 * D:4 * D], mA)
    mod_ready += bcast(ca1b, carow[0:1, 0:D], [ca_p1])
    mod_ready += bcast(sab, modrow[0:1, 0:D], mA)

    mmg = sc.op("T", lambda e: e.matmul(
        ps_c[:, 0:H], ones_f[:], grow[:], start=True, stop=True),
        after=[ld_g, ones1, psc_last])
    grep = sc.op("V", lambda e: e.tensor_scalar(
        out=gainrep[:], in0=ps_c[:, 0:H], scalar1=0.125, scalar2=None,
        op0=ALU.mult), after=[mmg])
    psc_last = grep
    mod_ready.append(grep)

    for dst, etop, ebot in ((pmA, 0, 3), (pmB, 2, 1)):
        mt = sc.op("T", lambda e, etop=etop: e.matmul(
            ps_c[0:64, 0:C8], ones_f[0:1, 0:64], pmrow[0:1, etop:etop + 29:4],
            start=True, stop=True), after=[ld_pm, ones1, psc_last])
        mb = sc.op("T", lambda e, ebot=ebot: e.matmul(
            ps_c[64:128, 0:C8], ones_f[0:1, 0:64], pmrow[0:1, ebot:ebot + 29:4],
            start=True, stop=True, tile_position=(0, 64)), after=[mt])
        cp = sc.op("V", lambda e, dst=dst: e.tensor_copy(
            out=dst[:], in_=ps_c[:, 0:C8]), after=[mb])
        psc_last = cp
        mod_ready.append(cp)

    trig_ready = [trig_sin, trig_cos]
    first_gate = mod_ready + trig_ready + ([dbg_mod_tok] if dbg_mod_tok else [])

    setup.close()

    attn = contextlib.ExitStack()

    def sba(name, shape, dt=F32):
        name = "sba_" + name
        return attn.enter_context(nc.sbuf_tensor(name, shape, dt))

    wqb = sba("wqb", [128, C8 * D], BF16)
    wkb = sba("wkb", [128, C8 * 256], BF16)
    wvb = sba("wvb", [128, C8 * 256], BF16)
    wob = sba("wob", [128, C8 * D], BF16)
    kTr = sba("kTr", [128, KVH * 4 * 128], BF16)
    vtr = sba("vtr", [128, 4 * 256], BF16)
    stg = sba("stg", [128, 1024])
    xn = sba("xn", [128, D], BF16)
    h1T = sba("h1T", [128, C8 * 128], BF16)
    tmq = sba("tmq", [128, D])
    tmq2 = sba("tmq2", [128, D])
    qrope = sba("qrope", [128, D], BF16)
    krope2 = sba("krope2", [128, 512], BF16)
    qT = sba("qT", [128, C8 * 128], BF16)
    red16 = sba("red16", [128, 16])
    red16b = sba("red16b", [128, 16])
    red16c = sba("red16c", [128, 16])
    ln_sc = sba("ln_sc", [128, 8])
    msS = sba("msS", [128, 384])
    Pb = sba("Pb", [128, 384], BF16)
    PTs = sba("PTs", [128, 384], BF16)
    rsum = sba("rsum", [128, 1])
    mrow = sba("mrow", [128, 1])
    yT = sba("yT", [128, C8 * 128], BF16)
    h2f = sba("h2f", [128, D])
    xnf = sba("xnf", [128, D])
    Pf = sba("Pf", [128, 384])
    qb0 = sba("qb0", [128, D], BF16)
    qb1 = sba("qb1", [128, D], BF16)
    sgq = sba("sgq", [128, D], BF16)
    m64 = sba("m64", [128, 64])
    s64 = sba("s64", [128, 64])
    r64 = sba("r64", [128, 64])

    wtoks = []

    def loadcast(dst, src_d, ncol):
        toks = []
        for j in range(C8):
            dep = ([toks[-1]] if toks else
                   ([wtoks[-1]] if wtoks else list(first_gate)))
            l = sc.op("DW", lambda e, j=j, ncol=ncol: e.dma_start(
                out=stg[:, 0:ncol], in_=src_d[128 * j:128 * (j + 1), :]),
                dma=True, after=dep)
            cvt = sc.op("V", lambda e, j=j, ncol=ncol: e.tensor_copy(
                out=dst[:, ncol * j:ncol * (j + 1)], in_=stg[:, 0:ncol]),
                after=[l])
            toks.append(cvt)
        return toks

    wtoks += loadcast(wqb, wqT_d, D)
    wtoks += loadcast(wkb, wkT_d, 256)
    wtoks += loadcast(wvb, wvT_d, 256)
    for j in range(C8):
        l1 = sc.op("DW", lambda e, j=j: e.dma_start(
            out=stg[:, 0:D], in_=woT_d[128 * j:128 * (j + 1), :]), dma=True,
            after=[wtoks[-1]] if j == 0 else [wtoks[-1]])
        l2 = sc.op("DW", lambda e, j=j: e.dma_start(
            out=tmq[:], in_=woTsw_d[128 * j:128 * (j + 1), :]), dma=True,
            after=[l1] + (list(first_gate) if j == 0 else []))
        t1 = sc.op("V", lambda e, j=j: e.tensor_scalar(
            out=tmq[:], in0=tmq[:], scalar1=pmB[:, j:j + 1], scalar2=None,
            op0=ALU.mult), after=[l2] + mod_ready)
        t2 = sc.op("V", lambda e, j=j: e.scalar_tensor_tensor(
            out=wob[:, D * j:D * (j + 1)], in0=stg[:, 0:D],
            scalar=pmA[:, j:j + 1], in1=tmq[:], op0=ALU.mult, op1=ALU.add),
            after=[t1])
        wtoks.append(t2)

    def cview(buf, j, n):
        return buf[:].rearrange("p (j n) -> p j n", n=n)[:, j, :]

    def hqT_v(j, t):
        return hqTb[:].rearrange("p (j s) -> p j s", j=C8)[:, j, 128 * t:128 * (t + 1)]

    def emit_ln(xsrc, dst, col, dep):
        a = ln_sc[:, col:col + 1]
        b = ln_sc[:, col + 1:col + 2]
        c_ = ln_sc[:, col + 2:col + 3]
        d_ = ln_sc[:, col + 3:col + 4]
        t1 = sc.op("V", lambda e: e.tensor_reduce(
            out=a, in_=xsrc, axis=AX, op=ALU.add), after=dep)
        t2 = sc.op("S", lambda e: e.activation(
            out=stg[:, 0:D], in_=xsrc, func=ACT.Square), after=dep)
        t3 = sc.op("V", lambda e: e.tensor_reduce(
            out=b, in_=stg[:, 0:D], axis=AX, op=ALU.add), after=[t2])
        t4 = sc.op("V", lambda e: e.tensor_scalar(
            out=a, in0=a, scalar1=1.0 / D, scalar2=None, op0=ALU.mult),
            after=[t1])
        t5 = sc.op("V", lambda e: e.tensor_scalar(
            out=b, in0=b, scalar1=1.0 / D, scalar2=None, op0=ALU.mult),
            after=[t3])
        t6 = sc.op("V", lambda e: e.tensor_tensor(
            out=c_, in0=a, in1=a, op=ALU.mult), after=[t4])
        t7 = sc.op("V", lambda e: e.scalar_tensor_tensor(
            out=b, in0=c_, scalar=-1.0, in1=b, op0=ALU.mult, op1=ALU.add),
            after=[t5, t6])
        t8 = sc.op("V", lambda e: e.tensor_scalar(
            out=b, in0=b, scalar1=LN_EPS, scalar2=None, op0=ALU.add),
            after=[t7])
        t9 = sc.op("S", lambda e: e.activation(
            out=c_, in_=b, func=ACT.Sqrt), after=[t8])
        tok = sc.op("V", lambda e: e.reciprocal(out=c_, in_=c_), after=[t9])
        for _ in range(2):
            n1 = sc.op("V", lambda e: e.tensor_tensor(
                out=d_, in0=c_, in1=c_, op=ALU.mult), after=[tok])
            n2 = sc.op("V", lambda e: e.tensor_tensor(
                out=d_, in0=d_, in1=b, op=ALU.mult), after=[n1])
            n3 = sc.op("V", lambda e: e.tensor_scalar(
                out=d_, in0=d_, scalar1=-0.5, scalar2=1.5, op0=ALU.mult,
                op1=ALU.add), after=[n2])
            tok = sc.op("V", lambda e: e.tensor_tensor(
                out=c_, in0=c_, in1=d_, op=ALU.mult), after=[n3])
        nb = sc.op("V", lambda e: e.scalar_tensor_tensor(
            out=d_, in0=a, scalar=-1.0, in1=c_, op0=ALU.mult, op1=ALU.mult),
            after=[tok])
        xno = sc.op("V", lambda e: e.tensor_scalar(
            out=dst, in0=xsrc, scalar1=c_, scalar2=d_, op0=ALU.mult,
            op1=ALU.add), after=[nb])
        return xno

    def emit_quant(src_ap, dst_bf, width, dep, avq, aq, b0a, b1a, sga, wsa,
                   mm_, ss_, rr_, qa_eng="G"):
        nb16 = width // 16
        v3 = src_ap.rearrange("p (b k) -> p b k", k=16)
        q1 = sc.op("V", lambda e: e.tensor_reduce(
            out=mm_, in_=v3, axis=AX, op=ALU.max, apply_absolute_value=True),
            after=dep)
        q2a = sc.op("V", lambda e: e.tensor_scalar(
            out=ss_, in0=mm_, scalar1=C12_HI, scalar2=None, op0=ALU.mult),
            after=[q1])
        q2 = sc.op("V", lambda e: e.scalar_tensor_tensor(
            out=ss_, in0=mm_, scalar=C12_LO, in1=ss_, op0=ALU.mult,
            op1=ALU.add), after=[q2a])
        q2 = sc.op("V", lambda e: e.tensor_scalar(
            out=ss_, in0=ss_, scalar1=CLAMP_HALF, scalar2=None, op0=ALU.max),
            after=[q2])
        q3 = sc.op("V", lambda e: e.reciprocal(out=rr_, in_=ss_), after=[q2])
        av = sc.op("S", lambda e: e.activation(
            out=avq, in_=src_ap, func=ACT.Abs), after=dep)
        sg = sc.op("S", lambda e: e.activation(
            out=sga, in_=src_ap, func=ACT.Sign), after=dep + [av])
        rbc = rr_.unsqueeze(2).to_broadcast((128, nb16, 16))
        sbc = ss_.unsqueeze(2).to_broadcast((128, nb16, 16))
        qa = sc.op(qa_eng, lambda e: e.tensor_tensor(
            out=aq.rearrange("p (b k) -> p b k", k=16),
            in0=avq.rearrange("p (b k) -> p b k", k=16), in1=rbc,
            op=ALU.mult), after=[av, q3])
        tokA = sc.op("V", lambda e: e.tensor_scalar(
            out=b0a, in0=aq, scalar1=0.5, scalar2=None, op0=ALU.is_gt),
            after=[qa])
        for thr in (1.5, 2.5, 3.5):
            tokA = sc.op("V", lambda e, thr=thr: e.scalar_tensor_tensor(
                out=b0a, in0=aq, scalar=thr, in1=b0a, op0=ALU.is_gt,
                op1=ALU.add), after=[tokA])
        tokB = sc.op("V", lambda e: e.tensor_scalar(
            out=b1a, in0=aq, scalar1=5.0, scalar2=None, op0=ALU.is_gt),
            after=[qa, tokA])
        for thr in (7.0, 10.0, 10.0):
            tokB = sc.op("V", lambda e, thr=thr: e.scalar_tensor_tensor(
                out=b1a, in0=aq, scalar=thr, in1=b1a, op0=ALU.is_gt,
                op1=ALU.add), after=[tokB])
        tok = sc.op("V", lambda e: e.scalar_tensor_tensor(
            out=b0a, in0=b1a, scalar=2.0, in1=b0a, op0=ALU.mult,
            op1=ALU.add), after=[tokB])
        w1 = sc.op("G", lambda e: e.tensor_tensor(
            out=wsa.rearrange("p (b k) -> p b k", k=16),
            in0=sga.rearrange("p (b k) -> p b k", k=16), in1=sbc,
            op=ALU.mult), after=[sg, q3, qa])
        dq = sc.op("V", lambda e: e.tensor_tensor(
            out=dst_bf, in0=b0a, in1=wsa, op=ALU.mult), after=[w1, tok])
        return dq, sg

    xp_stores = []
    hq_done = []
    tile_end = {}
    prev_ps_t_reader = ld_ident
    qT_e = kT_e = None

    NT = 0 if phase == "setup" else (1 if phase == "attn1" else T)
    _CACHE.setdefault("marks", []).clear()
    for t in range(NT):
        _CACHE["marks"].append(("tile", t, sc.gidx))
        lo = max(0, t - 2)
        nkt = t - lo + 1
        W = 128 * nkt
        moff = 384 - W
        if t >= 3:
            ld_x.append(sc.op("DL", lambda e, t=t: e.dma_start(
                out=x_t(t), in_=x_d[128 * t:128 * (t + 1), :]), dma=True,
                after=list(tile_end[t - 3])))

        dep0 = [ld_x[t]] + wtoks
        xno = emit_ln(x_t(t), xnf[:], 0, dep0)
        hma = sc.op("V", lambda e: e.tensor_tensor(
            out=xnf[:], in0=xnf[:], in1=ca1b[:], op=ALU.mult),
            after=[xno] + mod_ready)
        hmb = sc.op("V", lambda e: e.tensor_tensor(
            out=xn[:], in0=xnf[:], in1=sab[:], op=ALU.add), after=[hma])

        tr = []
        for j in range(C8):
            tr.append(sc.op("T", lambda e, j=j: e.transpose(
                ps_t[:, 128 * j:128 * (j + 1)], xn[:, 128 * j:128 * (j + 1)],
                ident[:]), after=[hmb, ld_ident, prev_ps_t_reader]))
        h1toks = []
        for j in range(C8):
            h1toks.append(sc.op("V", lambda e, j=j: e.tensor_copy(
                out=cview(h1T, j, 128), in_=ps_t[:, 128 * j:128 * (j + 1)]),
                after=tr))
        prev_ps_t_reader = h1toks[-1]

        mmq = []
        for j in range(C8):
            st = (j == 0)
            sp = (j == C8 - 1)
            mmq.append(sc.op("T", lambda e, j=j, st=st, sp=sp: e.matmul(
                ps_a[:, 0:512], cview(h1T, j, 128), cview(wqb, j, D)[:, 0:512],
                start=st, stop=sp), after=[h1toks[j]] + wtoks))
            mmq.append(sc.op("T", lambda e, j=j, st=st, sp=sp: e.matmul(
                ps_a[:, 512:1024], cview(h1T, j, 128),
                cview(wqb, j, D)[:, 512:1024], start=st, stop=sp),
                after=[h1toks[j]]))
            mmq.append(sc.op("T", lambda e, j=j, st=st, sp=sp: e.matmul(
                ps_a[:, 1024:1280], cview(h1T, j, 128), cview(wkb, j, 256),
                start=st, stop=sp), after=[h1toks[j]]))
            mmq.append(sc.op("T", lambda e, j=j, st=st, sp=sp: e.matmul(
                ps_b[:, 512:768], cview(h1T, j, 128), cview(wvb, j, 256),
                start=st, stop=sp), after=[h1toks[j]]))
        qkv_done = mmq[-4:]

        sq1 = sc.op("S", lambda e: e.activation(
            out=tmq[:], in_=ps_a[:, 0:1024], func=ACT.Square), after=qkv_done)
        r1 = sc.op("V", lambda e: e.tensor_reduce(
            out=red16[:], in_=tmq[:].rearrange("p (h d) -> p h d", d=HD),
            axis=AX, op=ALU.add), after=[sq1])
        r2 = sc.op("V", lambda e: e.tensor_scalar(
            out=red16[:], in0=red16[:], scalar1=1.0 / HD, scalar2=RMS_EPS,
            op0=ALU.mult, op1=ALU.add), after=[r1])
        r3 = sc.op("S", lambda e: e.activation(
            out=red16b[:], in_=red16[:], func=ACT.Sqrt), after=[r2])
        tok = sc.op("V", lambda e: e.reciprocal(
            out=red16b[:], in_=red16b[:]), after=[r3])
        for _ in range(2):
            n1 = sc.op("V", lambda e: e.tensor_tensor(
                out=red16c[:], in0=red16b[:], in1=red16b[:], op=ALU.mult),
                after=[tok])
            n2 = sc.op("V", lambda e: e.tensor_tensor(
                out=red16c[:], in0=red16c[:], in1=red16[:], op=ALU.mult),
                after=[n1])
            n3 = sc.op("V", lambda e: e.tensor_scalar(
                out=red16c[:], in0=red16c[:], scalar1=-0.5, scalar2=1.5,
                op0=ALU.mult, op1=ALU.add), after=[n2])
            tok = sc.op("V", lambda e: e.tensor_tensor(
                out=red16b[:], in0=red16b[:], in1=red16c[:], op=ALU.mult),
                after=[n3])
        gcom = sc.op("V", lambda e: e.tensor_tensor(
            out=red16b[:], in0=red16b[:], in1=gainrep[:], op=ALU.mult),
            after=[tok] + mod_ready)
        qh = sc.op("V", lambda e: e.tensor_tensor(
            out=tmq[:].rearrange("p (h d) -> p h d", d=HD),
            in0=ps_a[:, 0:1024].rearrange("p (h d) -> p h d", d=HD),
            in1=red16b[:].unsqueeze(2).to_broadcast((128, H, HD)),
            op=ALU.mult), after=[gcom, sq1])

        cosv = cos_all[:, 32 * t:32 * (t + 1)].unsqueeze(1).to_broadcast((128, H, 32))
        sinv = sin_all[:, 32 * t:32 * (t + 1)].unsqueeze(1).to_broadcast((128, H, 32))
        qv = tmq[:].rearrange("p (h d) -> p h d", d=HD)
        w2v = tmq2[:].rearrange("p (h d) -> p h d", d=HD)
        o1 = qrope[:].rearrange("p (h d) -> p h d", d=HD)[:, :, 0:32]
        o2 = qrope[:].rearrange("p (h d) -> p h d", d=HD)[:, :, 32:64]
        rp1 = sc.op("V", lambda e, cosv=cosv: e.tensor_tensor(
            out=w2v[:, :, 0:32], in0=qv[:, :, 0:32], in1=cosv, op=ALU.mult),
            after=[qh] + trig_ready)
        rp2 = sc.op("V", lambda e, sinv=sinv: e.tensor_tensor(
            out=w2v[:, :, 32:64], in0=qv[:, :, 32:64], in1=sinv, op=ALU.mult),
            after=[qh])
        rp3 = sc.op("V", lambda e: e.tensor_tensor(
            out=o1, in0=w2v[:, :, 0:32], in1=w2v[:, :, 32:64], op=ALU.add),
            after=[rp1, rp2])
        rp4 = sc.op("V", lambda e, cosv=cosv: e.tensor_tensor(
            out=w2v[:, :, 0:32], in0=qv[:, :, 32:64], in1=cosv, op=ALU.mult),
            after=[rp3])
        rp5 = sc.op("V", lambda e, sinv=sinv: e.tensor_tensor(
            out=w2v[:, :, 32:64], in0=qv[:, :, 0:32], in1=sinv, op=ALU.mult),
            after=[rp3])
        rp6 = sc.op("V", lambda e: e.tensor_tensor(
            out=o2, in0=w2v[:, :, 0:32], in1=w2v[:, :, 32:64],
            op=ALU.subtract), after=[rp4, rp5])
        q_done = [rp3, rp6]

        sk1 = sc.op("S", lambda e: e.activation(
            out=tmq[:, 0:256], in_=ps_a[:, 1024:1280], func=ACT.Square),
            after=qkv_done + [rp6])
        k1 = sc.op("V", lambda e: e.tensor_reduce(
            out=red16[:, 0:4],
            in_=tmq[:, 0:256].rearrange("p (h d) -> p h d", d=HD), axis=AX,
            op=ALU.add), after=[sk1])
        k2 = sc.op("V", lambda e: e.tensor_scalar(
            out=red16[:, 0:4], in0=red16[:, 0:4], scalar1=1.0 / HD,
            scalar2=RMS_EPS, op0=ALU.mult, op1=ALU.add), after=[k1])
        k3 = sc.op("S", lambda e: e.activation(
            out=red16b[:, 0:4], in_=red16[:, 0:4], func=ACT.Sqrt), after=[k2])
        tok = sc.op("V", lambda e: e.reciprocal(
            out=red16b[:, 0:4], in_=red16b[:, 0:4]), after=[k3])
        for _ in range(2):
            n1 = sc.op("V", lambda e: e.tensor_tensor(
                out=red16c[:, 0:4], in0=red16b[:, 0:4], in1=red16b[:, 0:4],
                op=ALU.mult), after=[tok])
            n2 = sc.op("V", lambda e: e.tensor_tensor(
                out=red16c[:, 0:4], in0=red16c[:, 0:4], in1=red16[:, 0:4],
                op=ALU.mult), after=[n1])
            n3 = sc.op("V", lambda e: e.tensor_scalar(
                out=red16c[:, 0:4], in0=red16c[:, 0:4], scalar1=-0.5,
                scalar2=1.5, op0=ALU.mult, op1=ALU.add), after=[n2])
            tok = sc.op("V", lambda e: e.tensor_tensor(
                out=red16b[:, 0:4], in0=red16b[:, 0:4], in1=red16c[:, 0:4],
                op=ALU.mult), after=[n3])
        khx = sc.op("V", lambda e: e.tensor_tensor(
            out=tmq[:, 0:256].rearrange("p (h d) -> p h d", d=HD),
            in0=ps_a[:, 1024:1280].rearrange("p (h d) -> p h d", d=HD),
            in1=red16b[:, 0:4].unsqueeze(2).to_broadcast((128, 4, HD)),
            op=ALU.mult), after=[tok, sk1])
        kv4 = tmq[:, 0:256].rearrange("p (h d) -> p h d", d=HD)
        kw4 = tmq2[:, 0:256].rearrange("p (h d) -> p h d", d=HD)
        cos4 = cos_all[:, 32 * t:32 * (t + 1)].unsqueeze(1).to_broadcast((128, 4, 32))
        sin4 = sin_all[:, 32 * t:32 * (t + 1)].unsqueeze(1).to_broadcast((128, 4, 32))
        kr2v = krope2[:].rearrange("p (g r d) -> p g r d", g=4, r=2)
        kp1 = sc.op("V", lambda e, cos4=cos4: e.tensor_tensor(
            out=kw4[:, :, 0:32], in0=kv4[:, :, 0:32], in1=cos4, op=ALU.mult),
            after=[khx])
        kp2 = sc.op("V", lambda e, sin4=sin4: e.tensor_tensor(
            out=kw4[:, :, 32:64], in0=kv4[:, :, 32:64], in1=sin4,
            op=ALU.mult), after=[khx])
        kp3 = sc.op("V", lambda e: e.tensor_tensor(
            out=kr2v[:, :, 0, 0:32], in0=kw4[:, :, 0:32],
            in1=kw4[:, :, 32:64], op=ALU.add), after=[kp1, kp2])
        kp4 = sc.op("V", lambda e, cos4=cos4: e.tensor_tensor(
            out=kw4[:, :, 0:32], in0=kv4[:, :, 32:64], in1=cos4, op=ALU.mult),
            after=[kp3])
        kp5 = sc.op("V", lambda e, sin4=sin4: e.tensor_tensor(
            out=kw4[:, :, 32:64], in0=kv4[:, :, 0:32], in1=sin4,
            op=ALU.mult), after=[kp3])
        kp6 = sc.op("V", lambda e: e.tensor_tensor(
            out=kr2v[:, :, 0, 32:64], in0=kw4[:, :, 0:32],
            in1=kw4[:, :, 32:64], op=ALU.subtract), after=[kp4, kp5])
        kp7 = sc.op("V", lambda e: e.tensor_copy(
            out=kr2v[:, :, 1, :], in_=kr2v[:, :, 0, :]), after=[kp3, kp6])
        vev = sc.op("V", lambda e, t=t: e.tensor_copy(
            out=vtr[:, 256 * (t % 4):256 * (t % 4 + 1)],
            in_=ps_b[:, 512:768]), after=qkv_done + [khx])

        trq = []
        for j in range(C8):
            trq.append(sc.op("T", lambda e, j=j: e.transpose(
                ps_t[:, 128 * j:128 * (j + 1)], qrope[:, 128 * j:128 * (j + 1)],
                ident[:]), after=q_done + [prev_ps_t_reader]))
        qT_e = sc.op("V", lambda e: e.tensor_copy(out=qT[:], in_=ps_t[:]),
                     after=trq)
        trk = []
        for g in range(4):
            trk.append(sc.op("T", lambda e, g=g: e.transpose(
                ps_t[:, 128 * g:128 * (g + 1)],
                krope2[:, 128 * g:128 * (g + 1)], ident[:]),
                after=[kp7, qT_e]))
        kT_e = sc.op("V", lambda e, t=t: e.tensor_copy(
            out=kTr[:].rearrange("p (g r n) -> p g r n", g=KVH, r=4)[:, :, t % 4, :],
            in_=ps_t[:, 0:512].rearrange("p (g n) -> p g n", g=4)), after=trk)

        yready = []
        PT_prev = kT_e
        y_prev = None
        for h in range(H):
            g = h // 4
            qoff = 64 * (h % 2)
            qc = h // 2
            mm_s = []
            for kt2 in range(nkt):
                slot = (lo + kt2) % 4
                mm_s.append(sc.op("T", lambda e, kt2=kt2, slot=slot, g=g,
                                  qoff=qoff, qc=qc: e.matmul(
                    ps_b[:, 128 * kt2:128 * (kt2 + 1)],
                    cview(qT, qc, 128)[qoff:qoff + 64, :],
                    kTr[:].rearrange("p (g r n) -> p g r n", g=KVH, r=4)
                    [qoff:qoff + 64, g, slot, :],
                    start=True, stop=True), after=[qT_e, kT_e]))
            msk = sc.op("V", lambda e, W=W, moff=moff: e.tensor_tensor(
                out=msS[:, 0:W], in0=maskb[:, moff:moff + W],
                in1=ps_b[:, 0:W], op=ALU.add), after=mm_s + [ld_mask])
            mx = sc.op("V", lambda e, W=W: e.tensor_reduce(
                out=mrow[:], in_=msS[:, 0:W], axis=AX, op=ALU.max,
                negate=True), after=[msk])
            expo = sc.op("S", lambda e, W=W: e.activation(
                out=Pf[:, 0:W], in_=msS[:, 0:W], func=ACT.Exp, bias=mrow[:],
                scale=1.0, accum_out=rsum[:]), after=[mx])
            rr = sc.op("V", lambda e: e.reciprocal(out=rsum[:], in_=rsum[:]),
                       after=[expo])
            pn = sc.op("S", lambda e, W=W: e.activation(
                out=Pb[:, 0:W], in_=Pf[:, 0:W], func=ACT.Copy,
                scale=rsum[:, 0:1]), after=[rr])
            trp = []
            for kt2 in range(nkt):
                trp.append(sc.op("T", lambda e, kt2=kt2: e.transpose(
                    ps_t[:, 128 * kt2:128 * (kt2 + 1)],
                    Pb[:, 128 * kt2:128 * (kt2 + 1)], ident[:]),
                    after=[pn, PT_prev]))
            PT_e = sc.op("V", lambda e, W=W: e.tensor_copy(
                out=PTs[:, 0:W], in_=ps_t[:, 0:W]), after=trp)
            PT_prev = PT_e
            mm_y = []
            po = 64 * (h % 2)
            for kt2 in range(nkt):
                slot = (lo + kt2) % 4
                kwargs = {"tile_position": (0, 64)} if po else {}
                mm_y.append(sc.op("T", lambda e, kt2=kt2, slot=slot, g=g,
                                  po=po, kw=kwargs, nkt=nkt: e.matmul(
                    ps_b[po:po + 64, 1024:1152],
                    vtr[:, 256 * slot + 64 * g:256 * slot + 64 * (g + 1)],
                    PTs[:, 128 * kt2:128 * (kt2 + 1)],
                    start=(kt2 == 0), stop=(kt2 == nkt - 1), **kw),
                    after=[PT_e, vev] + ([y_prev] if y_prev else [])))
            yev = sc.op("V", lambda e, qc=qc, po=po: e.tensor_copy(
                out=cview(yT, qc, 128)[po:po + 64, :],
                in_=ps_b[po:po + 64, 1024:1152]), after=mm_y)
            y_prev = yev
            yready.append(yev)

        mwo = []
        for j in range(C8):
            st = (j == 0)
            sp = (j == C8 - 1)
            mwo.append(sc.op("T", lambda e, j=j, st=st, sp=sp: e.matmul(
                ps_a[:, 0:512], cview(yT, j, 128), cview(wob, j, D)[:, 0:512],
                start=st, stop=sp), after=yready))
            mwo.append(sc.op("T", lambda e, j=j, st=st, sp=sp: e.matmul(
                ps_a[:, 512:1024], cview(yT, j, 128),
                cview(wob, j, D)[:, 512:1024], start=st, stop=sp),
                after=yready))
        att_done = mwo[-2:]
        rs1 = sc.op("V", lambda e: e.tensor_tensor(
            out=stg[:, 0:D], in0=ps_a[:, 0:1024], in1=gab[:], op=ALU.mult),
            after=att_done + mod_ready)
        rs2 = sc.op("V", lambda e, t=t: e.tensor_tensor(
            out=x_t(t), in0=x_t(t), in1=stg[:, 0:D], op=ALU.add),
            after=[rs1])
        xst = sc.op("DS", lambda e, t=t: e.dma_start(
            out=xprime_d[128 * t:128 * (t + 1), :], in_=x_t(t)), dma=True,
            after=[rs2])
        xp_stores.append(xst)

        xn2 = emit_ln(x_t(t), h2f[:], 4, [rs2])
        hm1 = sc.op("V", lambda e: e.tensor_tensor(
            out=h2f[:], in0=h2f[:], in1=cb1b[:], op=ALU.mult),
            after=[xn2] + mod_ready)
        hm2 = sc.op("V", lambda e: e.tensor_tensor(
            out=h2f[:], in0=h2f[:], in1=sbb[:], op=ALU.add), after=[hm1])
        dq, q_sg = emit_quant(h2f[:], xn[:], D, [hm2], tmq[:], tmq2[:],
                              qb0[:], qb1[:], sgq[:], tmq[:], m64[:], s64[:],
                              r64[:])
        trh = []
        for j in range(C8):
            trh.append(sc.op("T", lambda e, j=j: e.transpose(
                ps_t[:, 128 * j:128 * (j + 1)], xn[:, 128 * j:128 * (j + 1)],
                ident[:]), after=[dq, PT_prev] + mwo))
        hqT_e = sc.op("V", lambda e, t=t: e.tensor_copy(
            out=hqTb[:].rearrange("p (j s) -> p j s", j=C8)[:, :, 128 * t:128 * (t + 1)],
            in_=ps_t[:].rearrange("p (j n) -> p j n", j=C8)), after=trh)
        hq_done.append(hqT_e)
        prev_ps_t_reader = hqT_e
        tile_end[t] = (hqT_e, q_sg, xst)

        if _DEBUG and t == 0:
            _tap_prev = [hqT_e]

            def tap(name, src_ap, deps, cast=True):
                wdt = 1
                for s_ in src_ap.shape[1:]:
                    wdt *= int(s_)
                if cast:
                    cpy = sc.op("V", lambda e, src_ap=src_ap, wdt=wdt: e.tensor_copy(
                        out=stg[:, 0:wdt], in_=src_ap),
                        after=deps + _tap_prev)
                    st_ = sc.op("DS", lambda e, name=name, wdt=wdt: e.dma_start(
                        out=dbg[name], in_=stg[:, 0:wdt]),
                        dma=True, after=[cpy])
                else:
                    st_ = sc.op("DS", lambda e, name=name, src_ap=src_ap: e.dma_start(
                        out=dbg[name], in_=src_ap), dma=True,
                        after=deps + _tap_prev)
                _tap_prev[:] = [st_]
            tap("dbg_h1T", h1T[:], h1toks)
            tap("dbg_q", qrope[:], q_done)
            tap("dbg_k",
                krope2[:].rearrange("p (g r d) -> p g r d", g=4, r=2)[:, :, 0, :],
                [kp7])
            tap("dbg_y", yT[:], yready)
            tap("dbg_hq", xn[:], [dq])
            tap("dbg_xp", x_t(t), [rs2, xst], cast=False)

    attn_exit = list(hq_done) + xp_stores + ([qT_e, kT_e] if NT else [])
    if not attn_exit:
        attn_exit = first_gate
    attn.close()
    outer.close()

    ffn = contextlib.ExitStack()

    def sbf(name, shape, dt=F32):
        name = "sbf_" + name
        return ffn.enter_context(nc.sbuf_tensor(name, shape, dt))

    wgT = sbf("wgT", [128, C8 * DI], BF16)
    wuT = sbf("wuT", [128, C8 * DI], BF16)
    wdT = sbf("wdT", [128, C22 * D], BF16)
    fstg = sbf("fstg", [128, 1024])
    wqnt = sbf("wqnt", [128, 1024], BF16)
    favq = sbf("favq", [128, 1024])
    fb0 = sbf("fb0", [128, 1024], BF16)
    fb1 = sbf("fb1", [128, 1024], BF16)
    fsg = sbf("fsg", [128, 1024], BF16)
    fm = sbf("fm", [128, 64])
    fs = sbf("fs", [128, 64])
    fr = sbf("fr", [128, 64])
    fstg2 = sbf("fstg2", [128, 1024])
    fsg2 = sbf("fsg2", [128, 1024], BF16)
    fm2 = sbf("fm2", [128, 64])
    fs2 = sbf("fs2", [128, 64])
    fr2 = sbf("fr2", [128, 64])
    xpt = sbf("xpt", [128, D])
    favq2 = xpt
    fh3 = fstg2
    fsets = ((fstg, wqnt, favq, fstg, fb0, fb1, fsg, fm, fs, fr),
             (fstg2, wqnt, favq2, fstg2, fb0, fb1, fsg2, fm2, fs2, fr2))
    hidqT = sbf("hidqT", [128, C22 * 128], BF16)
    fout = sbf("fout", [128, D])

    def wgT_v(j):
        return wgT[:].rearrange("p (j n) -> p j n", j=C8)[:, j, :]

    def wuT_v(j):
        return wuT[:].rearrange("p (j n) -> p j n", j=C8)[:, j, :]

    def wdT_v(r):
        return wdT[:].rearrange("p (r n) -> p r n", r=C22)[:, r, :]

    def hidqT_v(r):
        return hidqT[:].rearrange("p (r n) -> p r n", r=C22)[:, r, :]

    def fquant(src_ap, dst_bf, width, dep, si=0):
        (_fstg, _wqnt, _favq, _faq, _fb0, _fb1, _fsg, _fm, _fs,
         _fr) = fsets[si]
        return emit_quant(src_ap, dst_bf, width, dep, _favq[:, 0:width],
                          _faq[:, 0:width], _fb0[:, 0:width],
                          _fb1[:, 0:width], _fsg[:, 0:width],
                          _favq[:, 0:width], _fm[:, 0:width // 16],
                          _fs[:, 0:width // 16], _fr[:, 0:width // 16])

    wg_ready = []
    prev_ps_t_f = hq_done[-1] if hq_done else ld_ident
    wpar = [0]
    wchain = [None, None]

    def prep_w(src_d, nrow, dstT, jdim, col_splits, deps0):
        nonlocal prev_ps_t_f
        toks = []
        for r in range(nrow):
            for (c0, c1, jg0) in col_splits:
                wd_ = c1 - c0
                p = wpar[0] & 1
                wpar[0] += 1
                stg_p = fsets[p][0]
                qnt_p = fsets[p][1]
                l = sc.op("DF%d" % p, lambda e, r=r, c0=c0, c1=c1, stg_p=stg_p:
                          e.dma_start(
                    out=stg_p[:, 0:c1 - c0],
                    in_=src_d[128 * r:128 * (r + 1), c0:c1]), dma=True,
                    after=[wchain[p]] if wchain[p] else deps0)
                qd, _sg = fquant(stg_p[:, 0:wd_], qnt_p[:, 0:wd_], wd_, [l],
                                 si=p)
                nblk = wd_ // 128
                sub = []
                for bi in range(nblk):
                    sub.append(sc.op("T", lambda e, bi=bi, qnt_p=qnt_p:
                                     e.transpose(
                        ps_t[:, 128 * bi:128 * (bi + 1)],
                        qnt_p[:, 128 * bi:128 * (bi + 1)], ident[:]),
                        after=[qd, prev_ps_t_f]))
                ev = sc.op("V", lambda e, r=r, jg0=jg0, nblk=nblk, dstT=dstT,
                           jdim=jdim: e.tensor_copy(
                    out=dstT[:].rearrange("p (j n) -> p j n", j=jdim)
                    [:, jg0:jg0 + nblk, 128 * r:128 * (r + 1)],
                    in_=ps_t[:, 0:128 * nblk].rearrange(
                        "p (b n) -> p b n", b=nblk)), after=sub)
                wchain[p] = qd
                prev_ps_t_f = ev
                toks.append(ev)
        return toks

    if phase == "full":
        wg_ready += prep_w(wg_d, C22, wgT, C8, [(0, 1024, 0)], attn_exit)
        wg_ready += prep_w(wu_d, C22, wuT, C8, [(0, 1024, 0)], [])
        wg_ready += prep_w(wd_d, C8, wdT, C22,
                           [(0, 1024, 0), (1024, 2048, 8), (2048, 2816, 16)], [])

    out_stores = []
    prevf = []
    f2_hist = {}
    for t in range(T if phase == "full" else 0):
        _CACHE["marks"].append(("ffn", t, sc.gidx))
        xslot = xpt[:]
        xl = sc.op("DL", lambda e, t=t, xslot=xslot: e.dma_start(
            out=xslot, in_=xprime_d[128 * t:128 * (t + 1), :]), dma=True,
            after=[xp_stores[t]] + ([f2_hist[t - 1]] if t >= 1
                                    else [wg_ready[-1]]))
        hid_ready = []
        prev_h3 = None
        for half in range(2):
            mmg = []
            for j in range(C8):
                st = (j == 0)
                sp = (j == C8 - 1)
                for (c0, c1) in ((0, 512), (512, 1024), (1024, 1408)):
                    off = 1408 * half
                    mmg.append(sc.op("T", lambda e, j=j, st=st, sp=sp, c0=c0,
                                     c1=c1, off=off, t=t: e.matmul(
                        ps_a[:, c0:c1], hqT_v(j, t),
                        wgT_v(j)[:, off + c0:off + c1], start=st, stop=sp),
                        after=[hq_done[t], wg_ready[-1]] + prevf))
                    mmg.append(sc.op("T", lambda e, j=j, st=st, sp=sp, c0=c0,
                                     c1=c1, off=off, t=t: e.matmul(
                        ps_b[:, c0:c1], hqT_v(j, t),
                        wuT_v(j)[:, off + c0:off + c1], start=st, stop=sp),
                        after=[hq_done[t]]))
            gu_done = mmg[-2:]
            for sl_i, (c0, c1) in enumerate(((0, 512), (512, 1024),
                                             (1024, 1408))):
                wd_ = c1 - c0
                slv = sc.op("S", lambda e, c0=c0, c1=c1: e.activation(
                    out=fstg[:, 0:c1 - c0], in_=ps_a[:, c0:c1], func=ACT.Silu),
                    after=gu_done + ([prev_h3] if prev_h3 else []))
                h3v = sc.op("V", lambda e, c0=c0, c1=c1, wd_=wd_: e.tensor_tensor(
                    out=fh3[:, 0:wd_], in0=fstg[:, 0:wd_], in1=ps_b[:, c0:c1],
                    op=ALU.mult), after=[slv])
                hqd, _sg2 = fquant(fh3[:, 0:wd_], wqnt[:, 0:wd_], wd_, [h3v])
                prev_h3 = hqd
                nblk = wd_ // 128
                sub = []
                for bi in range(nblk):
                    sub.append(sc.op("T", lambda e, bi=bi: e.transpose(
                        ps_t[:, 128 * bi:128 * (bi + 1)],
                        wqnt[:, 128 * bi:128 * (bi + 1)], ident[:]),
                        after=[hqd, prev_ps_t_f]))
                base_r = half * 11 + (0, 4, 8)[sl_i]
                hv = sc.op("V", lambda e, base_r=base_r, nblk=nblk: e.tensor_copy(
                    out=hidqT[:].rearrange("p (r n) -> p r n", r=C22)
                    [:, base_r:base_r + nblk, :],
                    in_=ps_t[:, 0:128 * nblk].rearrange("p (b n) -> p b n",
                                                        b=nblk)), after=sub)
                prev_ps_t_f = hv
                hid_ready.append(hv)
        mmd = []
        for r in range(C22):
            st = (r == 0)
            sp = (r == C22 - 1)
            mmd.append(sc.op("T", lambda e, r=r, st=st, sp=sp: e.matmul(
                ps_a[:, 0:512], hidqT_v(r), wdT_v(r)[:, 0:512], start=st,
                stop=sp), after=hid_ready))
            mmd.append(sc.op("T", lambda e, r=r, st=st, sp=sp: e.matmul(
                ps_a[:, 512:1024], hidqT_v(r), wdT_v(r)[:, 512:1024],
                start=st, stop=sp), after=hid_ready))
        dn_done = mmd[-2:]
        f1 = sc.op("V", lambda e: e.tensor_tensor(
            out=fout[:], in0=ps_a[:, 0:1024], in1=gbb[:], op=ALU.mult),
            after=dn_done + bc_gb + out_stores[-1:])
        f2 = sc.op("V", lambda e, xslot=xslot: e.tensor_tensor(
            out=fout[:], in0=fout[:], in1=xslot, op=ALU.add),
            after=[f1, xl])
        ost = sc.op("DS", lambda e, t=t: e.dma_start(
            out=out_d[128 * t:128 * (t + 1), :], in_=fout[:]), dma=True,
            after=[f2])
        out_stores.append(ost)
        prevf = [f2]
        f2_hist[t] = f2

    ffn.close()

    if KCUT:
        sc.cut(KCUT)
    sc.finalize()
    sems = {}
    for e in ENGS:
        sems[e] = stack.enter_context(nc.semaphore("sem_" + e))
    n_dma = {c: sum(16 for r in sc.ops[c] if r["inc"] and r["dma"])
             for c in DMA_CLASSES}
    with nc.Block() as block:
        @block.tensor
        def _(eng):
            sc.replay(eng, "T", sems)

        @block.vector
        def _(eng):
            sc.replay(eng, "V", sems)

        @block.scalar
        def _(eng):
            sc.replay(eng, "S", sems)

        @block.gpsimd
        def _(eng):
            sc.replay(eng, "G", sems)

        @block.sync
        def _(eng):
            sc.replay_merged(eng, list(DMA_CLASSES), sems)
            for c in DMA_CLASSES:
                if n_dma[c]:
                    eng.wait_ge(sems[c], n_dma[c])

    stack.close()
    _CACHE[key] = nc
    return nc


def kernel(L0_attn_x, L0_attn_Wq, L0_attn_Wk, L0_attn_Wv, L0_attn_Wo,
           L0_attn_q_gain, L0_attn_pair_mix, L0_ffn_w_gate, L0_ffn_w_up,
           L0_ffn_w_down, L0_c, L0_wmod):
    f = np.float32
    x = np.ascontiguousarray(np.asarray(L0_attn_x, f))
    wqT = np.ascontiguousarray(np.asarray(L0_attn_Wq, f).T)
    wkT = np.ascontiguousarray(np.asarray(L0_attn_Wk, f).T)
    wvT = np.ascontiguousarray(np.asarray(L0_attn_Wv, f).T)
    woT = np.ascontiguousarray(np.asarray(L0_attn_Wo, f).T)
    perm = np.arange(D).reshape(C8, 2, 64)[:, ::-1, :].reshape(D)
    woTsw = np.ascontiguousarray(woT[perm])
    wmod = np.ascontiguousarray(np.asarray(L0_wmod, f))
    wg = np.ascontiguousarray(np.asarray(L0_ffn_w_gate, f))
    wu = np.ascontiguousarray(np.asarray(L0_ffn_w_up, f))
    wd = np.ascontiguousarray(np.asarray(L0_ffn_w_down, f))
    cfull = np.asarray(L0_c, f)
    gain = np.ascontiguousarray(np.asarray(L0_attn_q_gain, f).reshape(1, H))
    pm = np.ascontiguousarray(np.asarray(L0_attn_pair_mix, f).reshape(1, 32))
    maskb = _mask_host()
    ctbl, stbl = _rope_tables_host()
    ident = np.eye(128, dtype=ml_dtypes.bfloat16)

    nc = build()
    shared = {"wqT": wqT, "wkT": wkT, "wvT": wvT, "woT": woT,
              "woTsw": woTsw, "wmod": wmod, "wg": wg, "wu": wu, "wd": wd,
              "gain": gain, "pm": pm, "maskb": maskb, "ident": ident,
              "ctbl": ctbl, "stbl": stbl}
    in_maps = []
    for b in range(NCORE):
        m = dict(shared)
        m["x"] = np.ascontiguousarray(x[b])
        m["crow"] = np.ascontiguousarray(cfull[b:b + 1])
        in_maps.append(m)
    res = run_bass_kernel_spmd(nc, in_maps, core_ids=list(range(NCORE)))
    out = np.stack([np.asarray(r["out"]) for r in res.results], axis=0)
    return out.astype(np.float32)


# version 14
# speedup vs baseline: 1.1139x; 1.0151x over previous
import math
import numpy as np
import ml_dtypes
import concourse.bass as bass
import concourse.mybir as mybir
from concourse.bass_utils import run_bass_kernel_spmd

F32 = mybir.dt.float32
BF16 = mybir.dt.bfloat16
I32 = mybir.dt.int32
ALU = mybir.AluOpType
ACT = mybir.ActivationFunctionType
AX = mybir.AxisListType.X

S, D, H, KVH, HD = 2048, 1024, 16, 4, 64
T = S // 128
C8 = D // 128
DI = 2816
C22 = DI // 128
NCORE = 8
LN_EPS = 1e-5
RMS_EPS = 1e-6
ROPE_BASE = 10000.0
NEG = -1.0e30
PI = float(np.float32(math.pi))
TWO_PI = 2.0 * math.pi
PI_CLAMP = 3.1415925
TP_HI = float(np.float32(TWO_PI))
TP_LO = TWO_PI - TP_HI
_ES = -math.log(ROPE_BASE) / 32.0
ES_HI = float(np.float32(_ES))
ES_LO = _ES - ES_HI
_C6 = 1.0 / 6.0
C6_HI = float(np.float32(_C6))
C6_LO = _C6 - C6_HI
_C12 = 1.0 / 12.0
C12_HI = float(np.float32(_C12))
C12_LO = _C12 - C12_HI
CLAMP_HALF = float(np.float32(1e-12)) / 2.0
_PI2 = math.pi / 2.0
PI2_HI = float(np.float32(_PI2))
PI2_LO = _PI2 - PI2_HI

_DEBUG = False

ENGS = ("T", "V", "S", "G", "DL", "DS", "DP", "DW",
        "DM0", "DM1", "DM2", "DM3", "DF0", "DF1")
DMA_CLASSES = ("DL", "DS", "DP", "DW", "DM0", "DM1", "DM2", "DM3",
               "DF0", "DF1")


class Script:
    def __init__(self):
        self.ops = {e: [] for e in ENGS}
        self.gidx = 0

    def op(self, eng, fn, after=(), dma=False):
        deps = []
        for t in after:
            if t is None:
                continue
            if t["eng"] == eng and not t["dma"]:
                continue
            t["inc"] = True
            deps.append(t)
        rec = {"eng": eng, "fn": fn, "after": deps, "dma": dma,
               "inc": dma, "val": None, "gidx": self.gidx}
        self.gidx += 1
        self.ops[eng].append(rec)
        return rec

    def cut(self, maxg):
        kept = set()
        for e in ENGS:
            self.ops[e] = [r for r in self.ops[e] if r["gidx"] < maxg]
            for r in self.ops[e]:
                kept.add(id(r))
        for e in ENGS:
            for r in self.ops[e]:
                r["after"] = [t for t in r["after"] if id(t) in kept]

    def finalize(self):
        for e in ("V", "S", "G"):
            for rec in self.ops[e]:
                rec["inc"] = True
        for e in ENGS:
            run = 0
            for rec in self.ops[e]:
                if rec["inc"]:
                    run += 16 if rec["dma"] else 1
                rec["val"] = run
        tot = self.ops["DP"][-1]["val"] if self.ops["DP"] else 0
        for rec in self.ops["DP"]:
            rec["val"] = tot

    def replay_merged(self, eng, enames, sems):
        ops = sorted([r for e in enames for r in self.ops[e]],
                     key=lambda r: r["gidx"])
        self._replay_list(eng, ops, sems)

    def replay(self, eng, ename, sems):
        self._replay_list(eng, self.ops[ename], sems)

    def _replay_list(self, eng, ops, sems):
        selfserial = ops and ops[0]["eng"] in ("V", "S", "G")
        nprev = 0
        for rec in ops:
            seen = {}
            for t in rec["after"]:
                seen[t["eng"]] = max(seen.get(t["eng"], 0), t["val"])
            if selfserial and nprev > 0:
                seen[rec["eng"]] = max(seen.get(rec["eng"], 0), nprev)
            for key, v in seen.items():
                if v > 0:
                    eng.wait_ge(sems[key], v)
            instr = rec["fn"](eng)
            if rec["inc"]:
                instr.then_inc(sems[rec["eng"]], 16 if rec["dma"] else 1)
            if selfserial:
                nprev = rec["val"]


_CACHE = {}


def _mask_host():
    i = np.arange(128)[:, None]
    c = np.arange(384)[None, :]
    valid = (c > i) & (c <= i + 256)
    return np.where(valid, 0.0, NEG).astype(np.float32)


def _rope_tables_host():
    import jax
    import jax.numpy as jnp
    with jax.default_device(jax.devices("cpu")[0]):
        inv_freq = 1.0 / ROPE_BASE ** (jnp.arange(0, 64, 2,
                                                  dtype=jnp.float32) / 64)
        t = jnp.arange(S, dtype=jnp.float32)
        freqs = jnp.outer(t, inv_freq)
        cosj = np.asarray(jnp.cos(freqs))
        sinj = np.asarray(jnp.sin(freqs))
    ctbl = np.ascontiguousarray(
        cosj.reshape(T, 128, 32).transpose(1, 0, 2).reshape(128, T * 32))
    stbl = np.ascontiguousarray(
        sinj.reshape(T, 128, 32).transpose(1, 0, 2).reshape(128, T * 32))
    return ctbl, stbl


def build(phase="full"):
    import os as _os
    KCUT = int(_os.environ.get("KCUT", "0"))
    key = (phase, KCUT)
    if key in _CACHE:
        return _CACHE[key]
    nc = bass.Bass("TRN2", target_bir_lowering=False,
                   detect_race_conditions=False)
    sc = Script()

    def din(name, shape, dt=F32):
        return nc.dram_tensor(name, shape, dt, kind="ExternalInput").ap()

    x_d = din("x", [S, D])
    wqT_d = din("wqT", [D, D])
    wkT_d = din("wkT", [D, KVH * HD])
    wvT_d = din("wvT", [D, KVH * HD])
    woT_d = din("woT", [D, D])
    woTsw_d = din("woTsw", [D, D])
    wmod_d = din("wmod", [D, 6 * D])
    wg_d = din("wg", [DI, D])
    wu_d = din("wu", [DI, D])
    wd_d = din("wd", [D, DI])
    crow_d = din("crow", [1, D])
    gain_d = din("gain", [1, H])
    pm_d = din("pm", [1, 32])
    maskb_d = din("maskb", [128, 384])
    ctbl_d = din("ctbl", [128, T * 32])
    stbl_d = din("stbl", [128, T * 32])
    ident_d = din("ident", [128, 128], BF16)
    out_d = nc.dram_tensor("out", [S, D], F32, kind="ExternalOutput").ap()
    xprime_d = nc.dram_tensor("xprime", [S, D], F32).ap()
    dbg = {}
    if _DEBUG:
        for nm, shp in [("dbg_mod", [1, 6 * D]), ("dbg_h1T", [128, C8 * 128]),
                        ("dbg_q", [128, D]), ("dbg_k", [128, 256]),
                        ("dbg_y", [128, C8 * 128]), ("dbg_hq", [128, D]),
                        ("dbg_xp", [128, D])]:
            dbg[nm] = nc.dram_tensor(nm, shp, F32, kind="ExternalOutput").ap()

    import contextlib
    stack = contextlib.ExitStack()

    ident = stack.enter_context(nc.sbuf_tensor("ident_s", [128, 128], BF16))
    gbb = stack.enter_context(nc.sbuf_tensor("gbb", [128, D], F32))
    hqTb = stack.enter_context(nc.sbuf_tensor("hqTb", [128, C8 * S], BF16))

    ps_a = stack.enter_context(nc.psum_tensor("ps_a", [128, 1536], F32))
    ps_b = stack.enter_context(nc.psum_tensor("ps_b", [128, 1536], F32))
    ps_t = stack.enter_context(nc.psum_tensor("ps_t", [128, 1024], BF16))
    ps_c = stack.enter_context(nc.psum_tensor("ps_c", [128, 512], F32))

    ld_ident = sc.op("DP", lambda e: e.dma_start(out=ident[:], in_=ident_d),
                     dma=True)

    outer = contextlib.ExitStack()

    def sbo(name, shape, dt=F32):
        name = "sbo_" + name
        return outer.enter_context(nc.sbuf_tensor(name, shape, dt))

    maskb = sbo("maskb", [128, 384])
    sab = sbo("sab", [128, D])
    ca1b = sbo("ca1b", [128, D])
    gab = sbo("gab", [128, D])
    cb1b = sbo("cb1b", [128, D])
    sbb = sbo("sbb", [128, D])
    gainrep = sbo("gainrep", [128, H])
    pmA = sbo("pmA", [128, C8])
    pmB = sbo("pmB", [128, C8])
    cos_all = sbo("cos_all", [128, T * 32])
    sin_all = sbo("sin_all", [128, T * 32])
    x_ring = sbo("x_ring", [128, 3 * D])

    def x_t(t):
        return x_ring[:, D * (t % 3):D * (t % 3 + 1)]

    ld_mask = sc.op("DP", lambda e: e.dma_start(out=maskb[:], in_=maskb_d),
                    dma=True)
    ld_x = []
    for t in range(3):
        ld_x.append(sc.op("DP", lambda e, t=t: e.dma_start(
            out=x_t(t), in_=x_d[128 * t:128 * (t + 1), :]), dma=True))
    trig_cos = sc.op("DP", lambda e: e.dma_start(out=cos_all[:], in_=ctbl_d),
                     dma=True)
    trig_sin = sc.op("DP", lambda e: e.dma_start(out=sin_all[:], in_=stbl_d),
                     dma=True)

    setup = contextlib.ExitStack()

    def sb0(name, shape, dt=F32):
        name = "sb0_" + name
        return setup.enter_context(nc.sbuf_tensor(name, shape, dt))

    modrow = sb0("modrow", [1, 6 * D])
    c_col = sb0("c_col", [128, C8])
    silu_cb = sb0("silu_cb", [128, C8], BF16)
    ones_f = sb0("ones_f", [1, 128])
    stage = sb0("stage", [128, 4 * 512])
    stage_b = sb0("stage_b", [128, 4 * 512], BF16)
    garow = sb0("garow", [1, 2 * D])
    carow = sb0("carow", [1, 2 * D])
    tmp1 = sb0("tmp1", [1, 2 * D])
    tmp2 = sb0("tmp2", [1, 2 * D])
    grow = sb0("grow", [1, H])
    pmrow = sb0("pmrow", [1, 32])

    ld_c = sc.op("DP", lambda e: e.dma_start(
        out=c_col[:], in_=crow_d.rearrange("o (j p) -> (o p) j", p=128),
        allow_slow_non_contiguous=True), dma=True)
    ld_g = sc.op("DP", lambda e: e.dma_start(out=grow[:], in_=gain_d),
                 dma=True)
    ld_pm = sc.op("DP", lambda e: e.dma_start(out=pmrow[:], in_=pm_d),
                  dma=True)
    ones1 = sc.op("V", lambda e: e.memset(ones_f[:], 1.0))

    sl = sc.op("S", lambda e: e.activation(out=c_col[:], in_=c_col[:],
                                           func=ACT.Silu), after=[ld_c])
    slb = sc.op("V", lambda e: e.tensor_copy(out=silu_cb[:], in_=c_col[:]),
                after=[sl])

    ev_mod = []
    mm_hist = {}
    cst_hist = {}
    for j in range(12):
        mm_prev = None
        psum_j = ps_c[0:1, 0:512] if (j % 2 == 0) else ps_b[0:1, 0:512]
        for k in range(C8):
            gstep = j * C8 + k
            sl = gstep % 4
            ldst = sc.op("DM%d" % sl, lambda e, k=k, j=j, sl=sl: e.dma_start(
                out=stage[:, 512 * sl:512 * (sl + 1)],
                in_=wmod_d[128 * k:128 * (k + 1), 512 * j:512 * (j + 1)]),
                dma=True,
                after=[cst_hist[gstep - 4]] if gstep >= 4 else [])
            cst = sc.op("V", lambda e, sl=sl: e.tensor_copy(
                out=stage_b[:, 512 * sl:512 * (sl + 1)],
                in_=stage[:, 512 * sl:512 * (sl + 1)]),
                after=[ldst] + ([mm_hist[gstep - 4]] if gstep >= 4 else []))
            cst_hist[gstep] = cst
            mmdep = [cst, slb]
            if mm_prev is not None:
                mmdep.append(mm_prev)
            if k == 0 and j >= 2:
                mmdep.append(ev_mod[j - 2])
            mm_prev = sc.op("T", lambda e, k=k, sl=sl, psum_j=psum_j:
                            e.matmul(
                psum_j, silu_cb[:, k:k + 1],
                stage_b[:, 512 * sl:512 * (sl + 1)],
                start=(k == 0), stop=(k == C8 - 1)), after=mmdep)
            mm_hist[gstep] = mm_prev
        ev = sc.op("S", lambda e, j=j, psum_j=psum_j: e.copy(
            out=modrow[0:1, 512 * j:512 * (j + 1)], in_=psum_j),
            after=[mm_prev])
        ev_mod.append(ev)
    mA = list(ev_mod)

    g1 = sc.op("V", lambda e: e.tensor_copy(
        out=tmp1[0:1, 0:D], in_=modrow[0:1, 2 * D:3 * D]), after=mA)
    g2 = sc.op("V", lambda e: e.tensor_copy(
        out=tmp1[0:1, D:2 * D], in_=modrow[0:1, 5 * D:6 * D]), after=mA)
    sg1 = sc.op("V", lambda e: e.tensor_scalar(
        out=tmp2[:], in0=tmp1[:], scalar1=0.0, scalar2=None, op0=ALU.is_ge),
        after=[g1, g2])
    sg2 = sc.op("V", lambda e: e.tensor_scalar(
        out=tmp2[:], in0=tmp2[:], scalar1=2.0, scalar2=-1.0, op0=ALU.mult,
        op1=ALU.add), after=[sg1])
    mg1 = sc.op("S", lambda e: e.activation(out=tmp1[:], in_=tmp1[:],
                                            func=ACT.Abs),
                after=[g1, g2, sg1])
    mg2 = sc.op("V", lambda e: e.tensor_scalar(
        out=tmp1[:], in0=tmp1[:], scalar1=0.25, scalar2=None, op0=ALU.max),
        after=[mg1])
    gprod = sc.op("V", lambda e: e.tensor_tensor(
        out=garow[:], in0=tmp2[:], in1=tmp1[:], op=ALU.mult),
        after=[mg2, sg2])
    cc1 = sc.op("V", lambda e: e.tensor_copy(
        out=carow[0:1, 0:D], in_=modrow[0:1, D:2 * D]), after=mA)
    cc2 = sc.op("V", lambda e: e.tensor_copy(
        out=carow[0:1, D:2 * D], in_=modrow[0:1, 4 * D:5 * D]), after=mA)
    ca_p1 = sc.op("V", lambda e: e.tensor_scalar(
        out=carow[:], in0=carow[:], scalar1=1.0, scalar2=None, op0=ALU.add),
        after=[cc1, cc2])



    dbg_mod_tok = None
    if _DEBUG:
        dbg_mod_tok = sc.op("DS", lambda e: e.dma_start(
            out=dbg["dbg_mod"], in_=modrow[:]), dma=True, after=mA)

    psc_last = mA[-1]
    mod_ready = []

    def bcast(dst, row_ap, dep):
        nonlocal psc_last
        toks = []
        for hh in range(2):
            mm = sc.op("T", lambda e, hh=hh: e.matmul(
                ps_c[:, 0:512], ones_f[:], row_ap[0:1, 512 * hh:512 * (hh + 1)],
                start=True, stop=True), after=dep + [ones1, psc_last])
            cp = sc.op("V", lambda e, hh=hh: e.tensor_copy(
                out=dst[:, 512 * hh:512 * (hh + 1)], in_=ps_c[:, 0:512]),
                after=[mm])
            psc_last = cp
            toks.append(cp)
        return toks

    bc_gb = bcast(gbb, garow[0:1, D:2 * D], [gprod])
    mod_ready += bc_gb
    mod_ready += bcast(gab, garow[0:1, 0:D], [gprod])
    mod_ready += bcast(cb1b, carow[0:1, D:2 * D], [ca_p1])
    mod_ready += bcast(sbb, modrow[0:1, 3 * D:4 * D], mA)
    mod_ready += bcast(ca1b, carow[0:1, 0:D], [ca_p1])
    mod_ready += bcast(sab, modrow[0:1, 0:D], mA)

    mmg = sc.op("T", lambda e: e.matmul(
        ps_c[:, 0:H], ones_f[:], grow[:], start=True, stop=True),
        after=[ld_g, ones1, psc_last])
    grep = sc.op("V", lambda e: e.tensor_scalar(
        out=gainrep[:], in0=ps_c[:, 0:H], scalar1=0.125, scalar2=None,
        op0=ALU.mult), after=[mmg])
    psc_last = grep
    mod_ready.append(grep)

    for dst, etop, ebot in ((pmA, 0, 3), (pmB, 2, 1)):
        mt = sc.op("T", lambda e, etop=etop: e.matmul(
            ps_c[0:64, 0:C8], ones_f[0:1, 0:64], pmrow[0:1, etop:etop + 29:4],
            start=True, stop=True), after=[ld_pm, ones1, psc_last])
        mb = sc.op("T", lambda e, ebot=ebot: e.matmul(
            ps_c[64:128, 0:C8], ones_f[0:1, 0:64], pmrow[0:1, ebot:ebot + 29:4],
            start=True, stop=True, tile_position=(0, 64)), after=[mt])
        cp = sc.op("V", lambda e, dst=dst: e.tensor_copy(
            out=dst[:], in_=ps_c[:, 0:C8]), after=[mb])
        psc_last = cp
        mod_ready.append(cp)

    trig_ready = [trig_sin, trig_cos]
    first_gate = mod_ready + trig_ready + ([dbg_mod_tok] if dbg_mod_tok else [])

    setup.close()

    attn = contextlib.ExitStack()

    def sba(name, shape, dt=F32):
        name = "sba_" + name
        return attn.enter_context(nc.sbuf_tensor(name, shape, dt))

    wqb = sba("wqb", [128, C8 * D], BF16)
    wkb = sba("wkb", [128, C8 * 256], BF16)
    wvb = sba("wvb", [128, C8 * 256], BF16)
    wob = sba("wob", [128, C8 * D], BF16)
    kTr = sba("kTr", [128, KVH * 4 * 128], BF16)
    vtr = sba("vtr", [128, 4 * 256], BF16)
    stg = sba("stg", [128, 1024])
    xn = sba("xn", [128, D], BF16)
    h1T = sba("h1T", [128, C8 * 128], BF16)
    tmq = sba("tmq", [128, D])
    tmq2 = sba("tmq2", [128, D])
    qrope = sba("qrope", [128, D], BF16)
    krope2 = sba("krope2", [128, 512], BF16)
    qT = sba("qT", [128, C8 * 128], BF16)
    red16 = sba("red16", [128, 16])
    red16b = sba("red16b", [128, 16])
    red16c = sba("red16c", [128, 16])
    ln_sc = sba("ln_sc", [128, 8])
    msS = sba("msS", [128, 384])
    msS2 = sba("msS2", [128, 384])
    Pb = sba("Pb", [128, 384], BF16)
    Pb2 = sba("Pb2", [128, 384], BF16)
    PTs = sba("PTs", [128, 384], BF16)
    PTs2 = sba("PTs2", [128, 384], BF16)
    rsum = sba("rsum", [128, 2])
    mrow = sba("mrow", [128, 2])
    yT = sba("yT", [128, C8 * 128], BF16)
    h2f = sba("h2f", [128, D])
    xnf = sba("xnf", [128, D])
    Pf = sba("Pf", [128, 384])
    Pf2 = sba("Pf2", [128, 384])
    qb0 = sba("qb0", [128, D], BF16)
    qb1 = sba("qb1", [128, D], BF16)
    sgq = sba("sgq", [128, D], BF16)
    m64 = sba("m64", [128, 64])
    s64 = sba("s64", [128, 64])
    r64 = sba("r64", [128, 64])

    wtoks = []

    def loadcast(dst, src_d, ncol):
        toks = []
        for j in range(C8):
            dep = ([toks[-1]] if toks else
                   ([wtoks[-1]] if wtoks else list(first_gate)))
            l = sc.op("DW", lambda e, j=j, ncol=ncol: e.dma_start(
                out=stg[:, 0:ncol], in_=src_d[128 * j:128 * (j + 1), :]),
                dma=True, after=dep)
            cvt = sc.op("V", lambda e, j=j, ncol=ncol: e.tensor_copy(
                out=dst[:, ncol * j:ncol * (j + 1)], in_=stg[:, 0:ncol]),
                after=[l])
            toks.append(cvt)
        return toks

    wtoks += loadcast(wqb, wqT_d, D)
    wtoks += loadcast(wkb, wkT_d, 256)
    wtoks += loadcast(wvb, wvT_d, 256)
    for j in range(C8):
        l1 = sc.op("DW", lambda e, j=j: e.dma_start(
            out=stg[:, 0:D], in_=woT_d[128 * j:128 * (j + 1), :]), dma=True,
            after=[wtoks[-1]] if j == 0 else [wtoks[-1]])
        l2 = sc.op("DW", lambda e, j=j: e.dma_start(
            out=tmq[:], in_=woTsw_d[128 * j:128 * (j + 1), :]), dma=True,
            after=[l1] + (list(first_gate) if j == 0 else []))
        t1 = sc.op("V", lambda e, j=j: e.tensor_scalar(
            out=tmq[:], in0=tmq[:], scalar1=pmB[:, j:j + 1], scalar2=None,
            op0=ALU.mult), after=[l2] + mod_ready)
        t2 = sc.op("V", lambda e, j=j: e.scalar_tensor_tensor(
            out=wob[:, D * j:D * (j + 1)], in0=stg[:, 0:D],
            scalar=pmA[:, j:j + 1], in1=tmq[:], op0=ALU.mult, op1=ALU.add),
            after=[t1])
        wtoks.append(t2)

    def cview(buf, j, n):
        return buf[:].rearrange("p (j n) -> p j n", n=n)[:, j, :]

    def hqT_v(j, t):
        return hqTb[:].rearrange("p (j s) -> p j s", j=C8)[:, j, 128 * t:128 * (t + 1)]

    def emit_ln(xsrc, dst, col, dep):
        a = ln_sc[:, col:col + 1]
        b = ln_sc[:, col + 1:col + 2]
        c_ = ln_sc[:, col + 2:col + 3]
        d_ = ln_sc[:, col + 3:col + 4]
        t1 = sc.op("V", lambda e: e.tensor_reduce(
            out=a, in_=xsrc, axis=AX, op=ALU.add), after=dep)
        t2 = sc.op("S", lambda e: e.activation(
            out=stg[:, 0:D], in_=xsrc, func=ACT.Square), after=dep)
        t3 = sc.op("V", lambda e: e.tensor_reduce(
            out=b, in_=stg[:, 0:D], axis=AX, op=ALU.add), after=[t2])
        t4 = sc.op("V", lambda e: e.tensor_scalar(
            out=a, in0=a, scalar1=1.0 / D, scalar2=None, op0=ALU.mult),
            after=[t1])
        t5 = sc.op("V", lambda e: e.tensor_scalar(
            out=b, in0=b, scalar1=1.0 / D, scalar2=None, op0=ALU.mult),
            after=[t3])
        t6 = sc.op("V", lambda e: e.tensor_tensor(
            out=c_, in0=a, in1=a, op=ALU.mult), after=[t4])
        t7 = sc.op("V", lambda e: e.scalar_tensor_tensor(
            out=b, in0=c_, scalar=-1.0, in1=b, op0=ALU.mult, op1=ALU.add),
            after=[t5, t6])
        t8 = sc.op("V", lambda e: e.tensor_scalar(
            out=b, in0=b, scalar1=LN_EPS, scalar2=None, op0=ALU.add),
            after=[t7])
        t9 = sc.op("S", lambda e: e.activation(
            out=c_, in_=b, func=ACT.Sqrt), after=[t8])
        tok = sc.op("V", lambda e: e.reciprocal(out=c_, in_=c_), after=[t9])
        for _ in range(2):
            n1 = sc.op("V", lambda e: e.tensor_tensor(
                out=d_, in0=c_, in1=c_, op=ALU.mult), after=[tok])
            n2 = sc.op("V", lambda e: e.tensor_tensor(
                out=d_, in0=d_, in1=b, op=ALU.mult), after=[n1])
            n3 = sc.op("V", lambda e: e.tensor_scalar(
                out=d_, in0=d_, scalar1=-0.5, scalar2=1.5, op0=ALU.mult,
                op1=ALU.add), after=[n2])
            tok = sc.op("V", lambda e: e.tensor_tensor(
                out=c_, in0=c_, in1=d_, op=ALU.mult), after=[n3])
        nb = sc.op("V", lambda e: e.scalar_tensor_tensor(
            out=d_, in0=a, scalar=-1.0, in1=c_, op0=ALU.mult, op1=ALU.mult),
            after=[tok])
        xno = sc.op("V", lambda e: e.tensor_scalar(
            out=dst, in0=xsrc, scalar1=c_, scalar2=d_, op0=ALU.mult,
            op1=ALU.add), after=[nb])
        return xno

    def emit_quant(src_ap, dst_bf, width, dep, avq, aq, b0a, b1a, sga, wsa,
                   mm_, ss_, rr_, qa_eng="G"):
        nb16 = width // 16
        v3 = src_ap.rearrange("p (b k) -> p b k", k=16)
        q1 = sc.op("V", lambda e: e.tensor_reduce(
            out=mm_, in_=v3, axis=AX, op=ALU.max, apply_absolute_value=True),
            after=dep)
        q2a = sc.op("V", lambda e: e.tensor_scalar(
            out=ss_, in0=mm_, scalar1=C12_HI, scalar2=None, op0=ALU.mult),
            after=[q1])
        q2 = sc.op("V", lambda e: e.scalar_tensor_tensor(
            out=ss_, in0=mm_, scalar=C12_LO, in1=ss_, op0=ALU.mult,
            op1=ALU.add), after=[q2a])
        q2 = sc.op("V", lambda e: e.tensor_scalar(
            out=ss_, in0=ss_, scalar1=CLAMP_HALF, scalar2=None, op0=ALU.max),
            after=[q2])
        q3 = sc.op("V", lambda e: e.reciprocal(out=rr_, in_=ss_), after=[q2])
        av = sc.op("S", lambda e: e.activation(
            out=avq, in_=src_ap, func=ACT.Abs), after=dep)
        sg = sc.op("S", lambda e: e.activation(
            out=sga, in_=src_ap, func=ACT.Sign), after=dep + [av])
        rbc = rr_.unsqueeze(2).to_broadcast((128, nb16, 16))
        sbc = ss_.unsqueeze(2).to_broadcast((128, nb16, 16))
        qa = sc.op(qa_eng, lambda e: e.tensor_tensor(
            out=aq.rearrange("p (b k) -> p b k", k=16),
            in0=avq.rearrange("p (b k) -> p b k", k=16), in1=rbc,
            op=ALU.mult), after=[av, q3])
        tokA = sc.op("V", lambda e: e.tensor_scalar(
            out=b0a, in0=aq, scalar1=0.5, scalar2=None, op0=ALU.is_gt),
            after=[qa])
        for thr in (1.5, 2.5, 3.5):
            tokA = sc.op("V", lambda e, thr=thr: e.scalar_tensor_tensor(
                out=b0a, in0=aq, scalar=thr, in1=b0a, op0=ALU.is_gt,
                op1=ALU.add), after=[tokA])
        tokB = sc.op("V", lambda e: e.tensor_scalar(
            out=b1a, in0=aq, scalar1=5.0, scalar2=None, op0=ALU.is_gt),
            after=[qa, tokA])
        for thr in (7.0, 10.0, 10.0):
            tokB = sc.op("V", lambda e, thr=thr: e.scalar_tensor_tensor(
                out=b1a, in0=aq, scalar=thr, in1=b1a, op0=ALU.is_gt,
                op1=ALU.add), after=[tokB])
        tok = sc.op("V", lambda e: e.scalar_tensor_tensor(
            out=b0a, in0=b1a, scalar=2.0, in1=b0a, op0=ALU.mult,
            op1=ALU.add), after=[tokB])
        w1 = sc.op("G", lambda e: e.tensor_tensor(
            out=wsa.rearrange("p (b k) -> p b k", k=16),
            in0=sga.rearrange("p (b k) -> p b k", k=16), in1=sbc,
            op=ALU.mult), after=[sg, q3, qa])
        dq = sc.op("V", lambda e: e.tensor_tensor(
            out=dst_bf, in0=b0a, in1=wsa, op=ALU.mult), after=[w1, tok])
        return dq, sg

    xp_stores = []
    hq_done = []
    tile_end = {}
    prev_ps_t_reader = ld_ident
    qT_e = kT_e = None

    NT = 0 if phase == "setup" else (1 if phase == "attn1" else T)
    _CACHE.setdefault("marks", []).clear()
    for t in range(NT):
        _CACHE["marks"].append(("tile", t, sc.gidx))
        lo = max(0, t - 2)
        nkt = t - lo + 1
        W = 128 * nkt
        moff = 384 - W
        if t >= 3:
            ld_x.append(sc.op("DL", lambda e, t=t: e.dma_start(
                out=x_t(t), in_=x_d[128 * t:128 * (t + 1), :]), dma=True,
                after=list(tile_end[t - 3])))

        dep0 = [ld_x[t]] + wtoks
        xno = emit_ln(x_t(t), xnf[:], 0, dep0)
        hma = sc.op("V", lambda e: e.tensor_tensor(
            out=xnf[:], in0=xnf[:], in1=ca1b[:], op=ALU.mult),
            after=[xno] + mod_ready)
        hmb = sc.op("V", lambda e: e.tensor_tensor(
            out=xn[:], in0=xnf[:], in1=sab[:], op=ALU.add), after=[hma])

        tr = []
        for j in range(C8):
            tr.append(sc.op("T", lambda e, j=j: e.transpose(
                ps_t[:, 128 * j:128 * (j + 1)], xn[:, 128 * j:128 * (j + 1)],
                ident[:]), after=[hmb, ld_ident, prev_ps_t_reader]))
        h1toks = []
        for j in range(C8):
            h1toks.append(sc.op("V", lambda e, j=j: e.tensor_copy(
                out=cview(h1T, j, 128), in_=ps_t[:, 128 * j:128 * (j + 1)]),
                after=tr))
        prev_ps_t_reader = h1toks[-1]

        mmq = []
        for j in range(C8):
            st = (j == 0)
            sp = (j == C8 - 1)
            mmq.append(sc.op("T", lambda e, j=j, st=st, sp=sp: e.matmul(
                ps_a[:, 0:512], cview(h1T, j, 128), cview(wqb, j, D)[:, 0:512],
                start=st, stop=sp), after=[h1toks[j]] + wtoks))
            mmq.append(sc.op("T", lambda e, j=j, st=st, sp=sp: e.matmul(
                ps_a[:, 512:1024], cview(h1T, j, 128),
                cview(wqb, j, D)[:, 512:1024], start=st, stop=sp),
                after=[h1toks[j]]))
            mmq.append(sc.op("T", lambda e, j=j, st=st, sp=sp: e.matmul(
                ps_a[:, 1024:1280], cview(h1T, j, 128), cview(wkb, j, 256),
                start=st, stop=sp), after=[h1toks[j]]))
            mmq.append(sc.op("T", lambda e, j=j, st=st, sp=sp: e.matmul(
                ps_b[:, 512:768], cview(h1T, j, 128), cview(wvb, j, 256),
                start=st, stop=sp), after=[h1toks[j]]))
        qkv_done = mmq[-4:]

        sq1 = sc.op("S", lambda e: e.activation(
            out=tmq[:], in_=ps_a[:, 0:1024], func=ACT.Square), after=qkv_done)
        r1 = sc.op("V", lambda e: e.tensor_reduce(
            out=red16[:], in_=tmq[:].rearrange("p (h d) -> p h d", d=HD),
            axis=AX, op=ALU.add), after=[sq1])
        r2 = sc.op("V", lambda e: e.tensor_scalar(
            out=red16[:], in0=red16[:], scalar1=1.0 / HD, scalar2=RMS_EPS,
            op0=ALU.mult, op1=ALU.add), after=[r1])
        r3 = sc.op("S", lambda e: e.activation(
            out=red16b[:], in_=red16[:], func=ACT.Sqrt), after=[r2])
        tok = sc.op("V", lambda e: e.reciprocal(
            out=red16b[:], in_=red16b[:]), after=[r3])
        for _ in range(2):
            n1 = sc.op("V", lambda e: e.tensor_tensor(
                out=red16c[:], in0=red16b[:], in1=red16b[:], op=ALU.mult),
                after=[tok])
            n2 = sc.op("V", lambda e: e.tensor_tensor(
                out=red16c[:], in0=red16c[:], in1=red16[:], op=ALU.mult),
                after=[n1])
            n3 = sc.op("V", lambda e: e.tensor_scalar(
                out=red16c[:], in0=red16c[:], scalar1=-0.5, scalar2=1.5,
                op0=ALU.mult, op1=ALU.add), after=[n2])
            tok = sc.op("V", lambda e: e.tensor_tensor(
                out=red16b[:], in0=red16b[:], in1=red16c[:], op=ALU.mult),
                after=[n3])
        gcom = sc.op("V", lambda e: e.tensor_tensor(
            out=red16b[:], in0=red16b[:], in1=gainrep[:], op=ALU.mult),
            after=[tok] + mod_ready)
        qh = sc.op("V", lambda e: e.tensor_tensor(
            out=tmq[:].rearrange("p (h d) -> p h d", d=HD),
            in0=ps_a[:, 0:1024].rearrange("p (h d) -> p h d", d=HD),
            in1=red16b[:].unsqueeze(2).to_broadcast((128, H, HD)),
            op=ALU.mult), after=[gcom, sq1])

        cosv = cos_all[:, 32 * t:32 * (t + 1)].unsqueeze(1).to_broadcast((128, H, 32))
        sinv = sin_all[:, 32 * t:32 * (t + 1)].unsqueeze(1).to_broadcast((128, H, 32))
        qv = tmq[:].rearrange("p (h d) -> p h d", d=HD)
        w2v = tmq2[:].rearrange("p (h d) -> p h d", d=HD)
        o1 = qrope[:].rearrange("p (h d) -> p h d", d=HD)[:, :, 0:32]
        o2 = qrope[:].rearrange("p (h d) -> p h d", d=HD)[:, :, 32:64]
        rp1 = sc.op("V", lambda e, cosv=cosv: e.tensor_tensor(
            out=w2v[:, :, 0:32], in0=qv[:, :, 0:32], in1=cosv, op=ALU.mult),
            after=[qh] + trig_ready)
        rp2 = sc.op("V", lambda e, sinv=sinv: e.tensor_tensor(
            out=w2v[:, :, 32:64], in0=qv[:, :, 32:64], in1=sinv, op=ALU.mult),
            after=[qh])
        rp3 = sc.op("V", lambda e: e.tensor_tensor(
            out=o1, in0=w2v[:, :, 0:32], in1=w2v[:, :, 32:64], op=ALU.add),
            after=[rp1, rp2])
        rp4 = sc.op("V", lambda e, cosv=cosv: e.tensor_tensor(
            out=w2v[:, :, 0:32], in0=qv[:, :, 32:64], in1=cosv, op=ALU.mult),
            after=[rp3])
        rp5 = sc.op("V", lambda e, sinv=sinv: e.tensor_tensor(
            out=w2v[:, :, 32:64], in0=qv[:, :, 0:32], in1=sinv, op=ALU.mult),
            after=[rp3])
        rp6 = sc.op("V", lambda e: e.tensor_tensor(
            out=o2, in0=w2v[:, :, 0:32], in1=w2v[:, :, 32:64],
            op=ALU.subtract), after=[rp4, rp5])
        q_done = [rp3, rp6]

        sk1 = sc.op("S", lambda e: e.activation(
            out=tmq[:, 0:256], in_=ps_a[:, 1024:1280], func=ACT.Square),
            after=qkv_done + [rp6])
        k1 = sc.op("V", lambda e: e.tensor_reduce(
            out=red16[:, 0:4],
            in_=tmq[:, 0:256].rearrange("p (h d) -> p h d", d=HD), axis=AX,
            op=ALU.add), after=[sk1])
        k2 = sc.op("V", lambda e: e.tensor_scalar(
            out=red16[:, 0:4], in0=red16[:, 0:4], scalar1=1.0 / HD,
            scalar2=RMS_EPS, op0=ALU.mult, op1=ALU.add), after=[k1])
        k3 = sc.op("S", lambda e: e.activation(
            out=red16b[:, 0:4], in_=red16[:, 0:4], func=ACT.Sqrt), after=[k2])
        tok = sc.op("V", lambda e: e.reciprocal(
            out=red16b[:, 0:4], in_=red16b[:, 0:4]), after=[k3])
        for _ in range(2):
            n1 = sc.op("V", lambda e: e.tensor_tensor(
                out=red16c[:, 0:4], in0=red16b[:, 0:4], in1=red16b[:, 0:4],
                op=ALU.mult), after=[tok])
            n2 = sc.op("V", lambda e: e.tensor_tensor(
                out=red16c[:, 0:4], in0=red16c[:, 0:4], in1=red16[:, 0:4],
                op=ALU.mult), after=[n1])
            n3 = sc.op("V", lambda e: e.tensor_scalar(
                out=red16c[:, 0:4], in0=red16c[:, 0:4], scalar1=-0.5,
                scalar2=1.5, op0=ALU.mult, op1=ALU.add), after=[n2])
            tok = sc.op("V", lambda e: e.tensor_tensor(
                out=red16b[:, 0:4], in0=red16b[:, 0:4], in1=red16c[:, 0:4],
                op=ALU.mult), after=[n3])
        khx = sc.op("V", lambda e: e.tensor_tensor(
            out=tmq[:, 0:256].rearrange("p (h d) -> p h d", d=HD),
            in0=ps_a[:, 1024:1280].rearrange("p (h d) -> p h d", d=HD),
            in1=red16b[:, 0:4].unsqueeze(2).to_broadcast((128, 4, HD)),
            op=ALU.mult), after=[tok, sk1])
        kv4 = tmq[:, 0:256].rearrange("p (h d) -> p h d", d=HD)
        kw4 = tmq2[:, 0:256].rearrange("p (h d) -> p h d", d=HD)
        cos4 = cos_all[:, 32 * t:32 * (t + 1)].unsqueeze(1).to_broadcast((128, 4, 32))
        sin4 = sin_all[:, 32 * t:32 * (t + 1)].unsqueeze(1).to_broadcast((128, 4, 32))
        kr2v = krope2[:].rearrange("p (g r d) -> p g r d", g=4, r=2)
        kp1 = sc.op("V", lambda e, cos4=cos4: e.tensor_tensor(
            out=kw4[:, :, 0:32], in0=kv4[:, :, 0:32], in1=cos4, op=ALU.mult),
            after=[khx])
        kp2 = sc.op("V", lambda e, sin4=sin4: e.tensor_tensor(
            out=kw4[:, :, 32:64], in0=kv4[:, :, 32:64], in1=sin4,
            op=ALU.mult), after=[khx])
        kp3 = sc.op("V", lambda e: e.tensor_tensor(
            out=kr2v[:, :, 0, 0:32], in0=kw4[:, :, 0:32],
            in1=kw4[:, :, 32:64], op=ALU.add), after=[kp1, kp2])
        kp4 = sc.op("V", lambda e, cos4=cos4: e.tensor_tensor(
            out=kw4[:, :, 0:32], in0=kv4[:, :, 32:64], in1=cos4, op=ALU.mult),
            after=[kp3])
        kp5 = sc.op("V", lambda e, sin4=sin4: e.tensor_tensor(
            out=kw4[:, :, 32:64], in0=kv4[:, :, 0:32], in1=sin4,
            op=ALU.mult), after=[kp3])
        kp6 = sc.op("V", lambda e: e.tensor_tensor(
            out=kr2v[:, :, 0, 32:64], in0=kw4[:, :, 0:32],
            in1=kw4[:, :, 32:64], op=ALU.subtract), after=[kp4, kp5])
        kp7 = sc.op("V", lambda e: e.tensor_copy(
            out=kr2v[:, :, 1, :], in_=kr2v[:, :, 0, :]), after=[kp3, kp6])
        vev = sc.op("V", lambda e, t=t: e.tensor_copy(
            out=vtr[:, 256 * (t % 4):256 * (t % 4 + 1)],
            in_=ps_b[:, 512:768]), after=qkv_done + [khx])

        trq = []
        for j in range(C8):
            trq.append(sc.op("T", lambda e, j=j: e.transpose(
                ps_t[:, 128 * j:128 * (j + 1)], qrope[:, 128 * j:128 * (j + 1)],
                ident[:]), after=q_done + [prev_ps_t_reader]))
        qT_e = sc.op("V", lambda e: e.tensor_copy(out=qT[:], in_=ps_t[:]),
                     after=trq)
        trk = []
        for g in range(4):
            trk.append(sc.op("T", lambda e, g=g: e.transpose(
                ps_t[:, 128 * g:128 * (g + 1)],
                krope2[:, 128 * g:128 * (g + 1)], ident[:]),
                after=[kp7, qT_e]))
        kT_e = sc.op("V", lambda e, t=t: e.tensor_copy(
            out=kTr[:].rearrange("p (g r n) -> p g r n", g=KVH, r=4)[:, :, t % 4, :],
            in_=ps_t[:, 0:512].rearrange("p (g n) -> p g n", g=4)), after=trk)

        yready = []
        msk_h = {}
        expo_h = {}
        pte_h = {}
        mmy_h = {}
        yev_h = {}
        msS_p, Pf_p, Pb_p, PTs_p = (msS, msS2), (Pf, Pf2), (Pb, Pb2), (PTs, PTs2)

        def head_front(h):
            g = h // 4
            qoff = 64 * (h % 2)
            qc = h // 2
            par = h % 2
            so = 512 * par
            mm_s = []
            for kt2 in range(nkt):
                slot = (lo + kt2) % 4
                mm_s.append(sc.op("T", lambda e, kt2=kt2, slot=slot, g=g,
                                  qoff=qoff, qc=qc, so=so: e.matmul(
                    ps_b[:, so + 128 * kt2:so + 128 * (kt2 + 1)],
                    cview(qT, qc, 128)[qoff:qoff + 64, :],
                    kTr[:].rearrange("p (g r n) -> p g r n", g=KVH, r=4)
                    [qoff:qoff + 64, g, slot, :],
                    start=True, stop=True),
                    after=[qT_e, kT_e, vev]
                    + ([msk_h[h - 2]] if h - 2 in msk_h else [])))
            ms_buf = msS_p[par]
            msk = sc.op("V", lambda e, W=W, moff=moff, ms_buf=ms_buf, so=so:
                        e.tensor_tensor(
                out=ms_buf[:, 0:W], in0=maskb[:, moff:moff + W],
                in1=ps_b[:, so:so + W], op=ALU.add),
                after=mm_s + [ld_mask]
                + ([expo_h[h - 2]] if h - 2 in expo_h else []))
            msk_h[h] = msk
            mx = sc.op("V", lambda e, W=W, ms_buf=ms_buf, par=par:
                       e.tensor_reduce(
                out=mrow[:, par:par + 1], in_=ms_buf[:, 0:W], axis=AX,
                op=ALU.max, negate=True), after=[msk])
            expo_h[h] = sc.op("S", lambda e, W=W, ms_buf=ms_buf,
                              pf=Pf_p[par], par=par: e.activation(
                out=pf[:, 0:W], in_=ms_buf[:, 0:W], func=ACT.Exp,
                bias=mrow[:, par:par + 1], scale=1.0,
                accum_out=rsum[:, par:par + 1]), after=[mx])

        def head_back(h):
            g = h // 4
            qc = h // 2
            par = h % 2
            po = 64 * par
            pto = 512 * par
            pf, pb, pts = Pf_p[par], Pb_p[par], PTs_p[par]
            rr = sc.op("V", lambda e, par=par: e.reciprocal(
                out=rsum[:, par:par + 1], in_=rsum[:, par:par + 1]),
                after=[expo_h[h]])
            pn = sc.op("S", lambda e, W=W, pf=pf, pb=pb, par=par:
                       e.activation(
                out=pb[:, 0:W], in_=pf[:, 0:W], func=ACT.Copy,
                scale=rsum[:, par:par + 1]),
                after=[rr] + ([pte_h[h - 2]] if h - 2 in pte_h else []))
            trp = []
            for kt2 in range(nkt):
                trp.append(sc.op("T", lambda e, kt2=kt2, pb=pb, pto=pto:
                                 e.transpose(
                    ps_t[:, pto + 128 * kt2:pto + 128 * (kt2 + 1)],
                    pb[:, 128 * kt2:128 * (kt2 + 1)], ident[:]),
                    after=[pn, kT_e]
                    + ([pte_h[h - 2]] if h - 2 in pte_h else [])))
            PT_e = sc.op("V", lambda e, W=W, pts=pts, pto=pto: e.tensor_copy(
                out=pts[:, 0:W], in_=ps_t[:, pto:pto + W]),
                after=trp + (mmy_h[h - 2] if h - 2 in mmy_h else []))
            pte_h[h] = PT_e
            ybank = ps_b if par == 0 else ps_a
            mm_y = []
            for kt2 in range(nkt):
                slot = (lo + kt2) % 4
                kwargs = {"tile_position": (0, 64)} if po else {}
                mm_y.append(sc.op("T", lambda e, kt2=kt2, slot=slot, g=g,
                                  po=po, kw=kwargs, pts=pts, ybank=ybank,
                                  nkt=nkt: e.matmul(
                    ybank[po:po + 64, 1024:1152],
                    vtr[:, 256 * slot + 64 * g:256 * slot + 64 * (g + 1)],
                    pts[:, 128 * kt2:128 * (kt2 + 1)],
                    start=(kt2 == 0), stop=(kt2 == nkt - 1), **kw),
                    after=[PT_e, vev]
                    + ([yev_h[h - 2]] if h - 2 in yev_h else [])))
            mmy_h[h] = mm_y
            yev = sc.op("V", lambda e, qc=qc, po=po, ybank=ybank:
                        e.tensor_copy(
                out=cview(yT, qc, 128)[po:po + 64, :],
                in_=ybank[po:po + 64, 1024:1152]), after=mm_y)
            yev_h[h] = yev
            yready.append(yev)

        head_front(0)
        head_front(1)
        for h in range(2, H):
            head_back(h - 2)
            head_front(h)
        head_back(H - 2)
        head_back(H - 1)
        PT_prev = pte_h[H - 1]
        PT_prev2 = pte_h[H - 2]

        mwo = []
        for j in range(C8):
            st = (j == 0)
            sp = (j == C8 - 1)
            mwo.append(sc.op("T", lambda e, j=j, st=st, sp=sp: e.matmul(
                ps_a[:, 0:512], cview(yT, j, 128), cview(wob, j, D)[:, 0:512],
                start=st, stop=sp), after=yready))
            mwo.append(sc.op("T", lambda e, j=j, st=st, sp=sp: e.matmul(
                ps_a[:, 512:1024], cview(yT, j, 128),
                cview(wob, j, D)[:, 512:1024], start=st, stop=sp),
                after=yready))
        att_done = mwo[-2:]
        rs1 = sc.op("V", lambda e: e.tensor_tensor(
            out=stg[:, 0:D], in0=ps_a[:, 0:1024], in1=gab[:], op=ALU.mult),
            after=att_done + mod_ready)
        rs2 = sc.op("V", lambda e, t=t: e.tensor_tensor(
            out=x_t(t), in0=x_t(t), in1=stg[:, 0:D], op=ALU.add),
            after=[rs1])
        xst = sc.op("DS", lambda e, t=t: e.dma_start(
            out=xprime_d[128 * t:128 * (t + 1), :], in_=x_t(t)), dma=True,
            after=[rs2])
        xp_stores.append(xst)

        xn2 = emit_ln(x_t(t), h2f[:], 4, [rs2])
        hm1 = sc.op("V", lambda e: e.tensor_tensor(
            out=h2f[:], in0=h2f[:], in1=cb1b[:], op=ALU.mult),
            after=[xn2] + mod_ready)
        hm2 = sc.op("V", lambda e: e.tensor_tensor(
            out=h2f[:], in0=h2f[:], in1=sbb[:], op=ALU.add), after=[hm1])
        dq, q_sg = emit_quant(h2f[:], xn[:], D, [hm2], tmq[:], tmq2[:],
                              qb0[:], qb1[:], sgq[:], tmq[:], m64[:], s64[:],
                              r64[:])
        trh = []
        for j in range(C8):
            trh.append(sc.op("T", lambda e, j=j: e.transpose(
                ps_t[:, 128 * j:128 * (j + 1)], xn[:, 128 * j:128 * (j + 1)],
                ident[:]), after=[dq, PT_prev, PT_prev2] + mwo))
        hqT_e = sc.op("V", lambda e, t=t: e.tensor_copy(
            out=hqTb[:].rearrange("p (j s) -> p j s", j=C8)[:, :, 128 * t:128 * (t + 1)],
            in_=ps_t[:].rearrange("p (j n) -> p j n", j=C8)), after=trh)
        hq_done.append(hqT_e)
        prev_ps_t_reader = hqT_e
        tile_end[t] = (hqT_e, q_sg, xst)

        if _DEBUG and t == 0:
            _tap_prev = [hqT_e]

            def tap(name, src_ap, deps, cast=True):
                wdt = 1
                for s_ in src_ap.shape[1:]:
                    wdt *= int(s_)
                if cast:
                    cpy = sc.op("V", lambda e, src_ap=src_ap, wdt=wdt: e.tensor_copy(
                        out=stg[:, 0:wdt], in_=src_ap),
                        after=deps + _tap_prev)
                    st_ = sc.op("DS", lambda e, name=name, wdt=wdt: e.dma_start(
                        out=dbg[name], in_=stg[:, 0:wdt]),
                        dma=True, after=[cpy])
                else:
                    st_ = sc.op("DS", lambda e, name=name, src_ap=src_ap: e.dma_start(
                        out=dbg[name], in_=src_ap), dma=True,
                        after=deps + _tap_prev)
                _tap_prev[:] = [st_]
            tap("dbg_h1T", h1T[:], h1toks)
            tap("dbg_q", qrope[:], q_done)
            tap("dbg_k",
                krope2[:].rearrange("p (g r d) -> p g r d", g=4, r=2)[:, :, 0, :],
                [kp7])
            tap("dbg_y", yT[:], yready)
            tap("dbg_hq", xn[:], [dq])
            tap("dbg_xp", x_t(t), [rs2, xst], cast=False)

    attn_exit = list(hq_done) + xp_stores + ([qT_e, kT_e] if NT else [])
    if not attn_exit:
        attn_exit = first_gate
    attn.close()
    outer.close()

    ffn = contextlib.ExitStack()

    def sbf(name, shape, dt=F32):
        name = "sbf_" + name
        return ffn.enter_context(nc.sbuf_tensor(name, shape, dt))

    wgT = sbf("wgT", [128, C8 * DI], BF16)
    wuT = sbf("wuT", [128, C8 * DI], BF16)
    wdT = sbf("wdT", [128, C22 * D], BF16)
    fstg = sbf("fstg", [128, 1024])
    wqnt = sbf("wqnt", [128, 1024], BF16)
    favq = sbf("favq", [128, 1024])
    fb0 = sbf("fb0", [128, 1024], BF16)
    fb1 = sbf("fb1", [128, 1024], BF16)
    fsg = sbf("fsg", [128, 1024], BF16)
    fm = sbf("fm", [128, 64])
    fs = sbf("fs", [128, 64])
    fr = sbf("fr", [128, 64])
    fstg2 = sbf("fstg2", [128, 1024])
    fsg2 = sbf("fsg2", [128, 1024], BF16)
    fm2 = sbf("fm2", [128, 64])
    fs2 = sbf("fs2", [128, 64])
    fr2 = sbf("fr2", [128, 64])
    xpt = sbf("xpt", [128, D])
    favq2 = xpt
    fh3 = fstg2
    fsets = ((fstg, wqnt, favq, fstg, fb0, fb1, fsg, fm, fs, fr),
             (fstg2, wqnt, favq2, fstg2, fb0, fb1, fsg2, fm2, fs2, fr2))
    hidqT = sbf("hidqT", [128, C22 * 128], BF16)
    fout = sbf("fout", [128, D])

    def wgT_v(j):
        return wgT[:].rearrange("p (j n) -> p j n", j=C8)[:, j, :]

    def wuT_v(j):
        return wuT[:].rearrange("p (j n) -> p j n", j=C8)[:, j, :]

    def wdT_v(r):
        return wdT[:].rearrange("p (r n) -> p r n", r=C22)[:, r, :]

    def hidqT_v(r):
        return hidqT[:].rearrange("p (r n) -> p r n", r=C22)[:, r, :]

    def fquant(src_ap, dst_bf, width, dep, si=0):
        (_fstg, _wqnt, _favq, _faq, _fb0, _fb1, _fsg, _fm, _fs,
         _fr) = fsets[si]
        return emit_quant(src_ap, dst_bf, width, dep, _favq[:, 0:width],
                          _faq[:, 0:width], _fb0[:, 0:width],
                          _fb1[:, 0:width], _fsg[:, 0:width],
                          _favq[:, 0:width], _fm[:, 0:width // 16],
                          _fs[:, 0:width // 16], _fr[:, 0:width // 16])

    wg_ready = []
    prev_ps_t_f = hq_done[-1] if hq_done else ld_ident
    wpar = [0]
    wchain = [None, None]

    def prep_w(src_d, nrow, dstT, jdim, col_splits, deps0):
        nonlocal prev_ps_t_f
        toks = []
        for r in range(nrow):
            for (c0, c1, jg0) in col_splits:
                wd_ = c1 - c0
                p = wpar[0] & 1
                wpar[0] += 1
                stg_p = fsets[p][0]
                qnt_p = fsets[p][1]
                l = sc.op("DF%d" % p, lambda e, r=r, c0=c0, c1=c1, stg_p=stg_p:
                          e.dma_start(
                    out=stg_p[:, 0:c1 - c0],
                    in_=src_d[128 * r:128 * (r + 1), c0:c1]), dma=True,
                    after=[wchain[p]] if wchain[p] else deps0)
                qd, _sg = fquant(stg_p[:, 0:wd_], qnt_p[:, 0:wd_], wd_, [l],
                                 si=p)
                nblk = wd_ // 128
                sub = []
                for bi in range(nblk):
                    sub.append(sc.op("T", lambda e, bi=bi, qnt_p=qnt_p:
                                     e.transpose(
                        ps_t[:, 128 * bi:128 * (bi + 1)],
                        qnt_p[:, 128 * bi:128 * (bi + 1)], ident[:]),
                        after=[qd, prev_ps_t_f]))
                ev = sc.op("V", lambda e, r=r, jg0=jg0, nblk=nblk, dstT=dstT,
                           jdim=jdim: e.tensor_copy(
                    out=dstT[:].rearrange("p (j n) -> p j n", j=jdim)
                    [:, jg0:jg0 + nblk, 128 * r:128 * (r + 1)],
                    in_=ps_t[:, 0:128 * nblk].rearrange(
                        "p (b n) -> p b n", b=nblk)), after=sub)
                wchain[p] = qd
                prev_ps_t_f = ev
                toks.append(ev)
        return toks

    if phase == "full":
        wg_ready += prep_w(wg_d, C22, wgT, C8, [(0, 1024, 0)], attn_exit)
        wg_ready += prep_w(wu_d, C22, wuT, C8, [(0, 1024, 0)], [])
        wg_ready += prep_w(wd_d, C8, wdT, C22,
                           [(0, 1024, 0), (1024, 2048, 8), (2048, 2816, 16)], [])

    out_stores = []
    prevf = []
    f2_hist = {}
    for t in range(T if phase == "full" else 0):
        _CACHE["marks"].append(("ffn", t, sc.gidx))
        xslot = xpt[:]
        xl = sc.op("DL", lambda e, t=t, xslot=xslot: e.dma_start(
            out=xslot, in_=xprime_d[128 * t:128 * (t + 1), :]), dma=True,
            after=[xp_stores[t]] + ([f2_hist[t - 1]] if t >= 1
                                    else [wg_ready[-1]]))
        hid_ready = []
        prev_h3 = None
        for half in range(2):
            mmg = []
            for j in range(C8):
                st = (j == 0)
                sp = (j == C8 - 1)
                for (c0, c1) in ((0, 512), (512, 1024), (1024, 1408)):
                    off = 1408 * half
                    mmg.append(sc.op("T", lambda e, j=j, st=st, sp=sp, c0=c0,
                                     c1=c1, off=off, t=t: e.matmul(
                        ps_a[:, c0:c1], hqT_v(j, t),
                        wgT_v(j)[:, off + c0:off + c1], start=st, stop=sp),
                        after=[hq_done[t], wg_ready[-1]] + prevf))
                    mmg.append(sc.op("T", lambda e, j=j, st=st, sp=sp, c0=c0,
                                     c1=c1, off=off, t=t: e.matmul(
                        ps_b[:, c0:c1], hqT_v(j, t),
                        wuT_v(j)[:, off + c0:off + c1], start=st, stop=sp),
                        after=[hq_done[t]]))
            gu_done = mmg[-2:]
            for sl_i, (c0, c1) in enumerate(((0, 512), (512, 1024),
                                             (1024, 1408))):
                wd_ = c1 - c0
                slv = sc.op("S", lambda e, c0=c0, c1=c1: e.activation(
                    out=fstg[:, 0:c1 - c0], in_=ps_a[:, c0:c1], func=ACT.Silu),
                    after=gu_done + ([prev_h3] if prev_h3 else []))
                h3v = sc.op("V", lambda e, c0=c0, c1=c1, wd_=wd_: e.tensor_tensor(
                    out=fh3[:, 0:wd_], in0=fstg[:, 0:wd_], in1=ps_b[:, c0:c1],
                    op=ALU.mult), after=[slv])
                hqd, _sg2 = fquant(fh3[:, 0:wd_], wqnt[:, 0:wd_], wd_, [h3v])
                prev_h3 = hqd
                nblk = wd_ // 128
                sub = []
                for bi in range(nblk):
                    sub.append(sc.op("T", lambda e, bi=bi: e.transpose(
                        ps_t[:, 128 * bi:128 * (bi + 1)],
                        wqnt[:, 128 * bi:128 * (bi + 1)], ident[:]),
                        after=[hqd, prev_ps_t_f]))
                base_r = half * 11 + (0, 4, 8)[sl_i]
                hv = sc.op("V", lambda e, base_r=base_r, nblk=nblk: e.tensor_copy(
                    out=hidqT[:].rearrange("p (r n) -> p r n", r=C22)
                    [:, base_r:base_r + nblk, :],
                    in_=ps_t[:, 0:128 * nblk].rearrange("p (b n) -> p b n",
                                                        b=nblk)), after=sub)
                prev_ps_t_f = hv
                hid_ready.append(hv)
        mmd = []
        for r in range(C22):
            st = (r == 0)
            sp = (r == C22 - 1)
            mmd.append(sc.op("T", lambda e, r=r, st=st, sp=sp: e.matmul(
                ps_a[:, 0:512], hidqT_v(r), wdT_v(r)[:, 0:512], start=st,
                stop=sp), after=hid_ready))
            mmd.append(sc.op("T", lambda e, r=r, st=st, sp=sp: e.matmul(
                ps_a[:, 512:1024], hidqT_v(r), wdT_v(r)[:, 512:1024],
                start=st, stop=sp), after=hid_ready))
        dn_done = mmd[-2:]
        f1 = sc.op("V", lambda e: e.tensor_tensor(
            out=fout[:], in0=ps_a[:, 0:1024], in1=gbb[:], op=ALU.mult),
            after=dn_done + bc_gb + out_stores[-1:])
        f2 = sc.op("V", lambda e, xslot=xslot: e.tensor_tensor(
            out=fout[:], in0=fout[:], in1=xslot, op=ALU.add),
            after=[f1, xl])
        ost = sc.op("DS", lambda e, t=t: e.dma_start(
            out=out_d[128 * t:128 * (t + 1), :], in_=fout[:]), dma=True,
            after=[f2])
        out_stores.append(ost)
        prevf = [f2]
        f2_hist[t] = f2

    ffn.close()

    if KCUT:
        sc.cut(KCUT)
    sc.finalize()
    sems = {}
    for e in ENGS:
        sems[e] = stack.enter_context(nc.semaphore("sem_" + e))
    n_dma = {c: sum(16 for r in sc.ops[c] if r["inc"] and r["dma"])
             for c in DMA_CLASSES}
    with nc.Block() as block:
        @block.tensor
        def _(eng):
            sc.replay(eng, "T", sems)

        @block.vector
        def _(eng):
            sc.replay(eng, "V", sems)

        @block.scalar
        def _(eng):
            sc.replay(eng, "S", sems)

        @block.gpsimd
        def _(eng):
            sc.replay(eng, "G", sems)

        @block.sync
        def _(eng):
            sc.replay_merged(eng, list(DMA_CLASSES), sems)
            for c in DMA_CLASSES:
                if n_dma[c]:
                    eng.wait_ge(sems[c], n_dma[c])

    stack.close()
    _CACHE[key] = nc
    return nc


def kernel(L0_attn_x, L0_attn_Wq, L0_attn_Wk, L0_attn_Wv, L0_attn_Wo,
           L0_attn_q_gain, L0_attn_pair_mix, L0_ffn_w_gate, L0_ffn_w_up,
           L0_ffn_w_down, L0_c, L0_wmod):
    f = np.float32
    x = np.ascontiguousarray(np.asarray(L0_attn_x, f))
    wqT = np.ascontiguousarray(np.asarray(L0_attn_Wq, f).T)
    wkT = np.ascontiguousarray(np.asarray(L0_attn_Wk, f).T)
    wvT = np.ascontiguousarray(np.asarray(L0_attn_Wv, f).T)
    woT = np.ascontiguousarray(np.asarray(L0_attn_Wo, f).T)
    perm = np.arange(D).reshape(C8, 2, 64)[:, ::-1, :].reshape(D)
    woTsw = np.ascontiguousarray(woT[perm])
    wmod = np.ascontiguousarray(np.asarray(L0_wmod, f))
    wg = np.ascontiguousarray(np.asarray(L0_ffn_w_gate, f))
    wu = np.ascontiguousarray(np.asarray(L0_ffn_w_up, f))
    wd = np.ascontiguousarray(np.asarray(L0_ffn_w_down, f))
    cfull = np.asarray(L0_c, f)
    gain = np.ascontiguousarray(np.asarray(L0_attn_q_gain, f).reshape(1, H))
    pm = np.ascontiguousarray(np.asarray(L0_attn_pair_mix, f).reshape(1, 32))
    maskb = _mask_host()
    ctbl, stbl = _rope_tables_host()
    ident = np.eye(128, dtype=ml_dtypes.bfloat16)

    nc = build()
    shared = {"wqT": wqT, "wkT": wkT, "wvT": wvT, "woT": woT,
              "woTsw": woTsw, "wmod": wmod, "wg": wg, "wu": wu, "wd": wd,
              "gain": gain, "pm": pm, "maskb": maskb, "ident": ident,
              "ctbl": ctbl, "stbl": stbl}
    in_maps = []
    for b in range(NCORE):
        m = dict(shared)
        m["x"] = np.ascontiguousarray(x[b])
        m["crow"] = np.ascontiguousarray(cfull[b:b + 1])
        in_maps.append(m)
    res = run_bass_kernel_spmd(nc, in_maps, core_ids=list(range(NCORE)))
    out = np.stack([np.asarray(r["out"]) for r in res.results], axis=0)
    return out.astype(np.float32)


# version 15 (final)
# speedup vs baseline: 1.1277x; 1.0124x over previous
import math
import numpy as np
import ml_dtypes
import concourse.bass as bass
import concourse.mybir as mybir
from concourse.bass_utils import run_bass_kernel_spmd

F32 = mybir.dt.float32
BF16 = mybir.dt.bfloat16
I32 = mybir.dt.int32
ALU = mybir.AluOpType
ACT = mybir.ActivationFunctionType
AX = mybir.AxisListType.X

S, D, H, KVH, HD = 2048, 1024, 16, 4, 64
T = S // 128
C8 = D // 128
DI = 2816
C22 = DI // 128
NCORE = 8
LN_EPS = 1e-5
RMS_EPS = 1e-6
ROPE_BASE = 10000.0
NEG = -1.0e30
PI = float(np.float32(math.pi))
TWO_PI = 2.0 * math.pi
PI_CLAMP = 3.1415925
TP_HI = float(np.float32(TWO_PI))
TP_LO = TWO_PI - TP_HI
_ES = -math.log(ROPE_BASE) / 32.0
ES_HI = float(np.float32(_ES))
ES_LO = _ES - ES_HI
_C6 = 1.0 / 6.0
C6_HI = float(np.float32(_C6))
C6_LO = _C6 - C6_HI
_C12 = 1.0 / 12.0
C12_HI = float(np.float32(_C12))
C12_LO = _C12 - C12_HI
CLAMP_HALF = float(np.float32(1e-12)) / 2.0
_PI2 = math.pi / 2.0
PI2_HI = float(np.float32(_PI2))
PI2_LO = _PI2 - PI2_HI

_DEBUG = False

ENGS = ("T", "V", "S", "G", "DL", "DS", "DP", "DW",
        "DM0", "DM1", "DM2", "DM3", "DF0", "DF1")
DMA_CLASSES = ("DL", "DS", "DP", "DW", "DM0", "DM1", "DM2", "DM3",
               "DF0", "DF1")


class Script:
    def __init__(self):
        self.ops = {e: [] for e in ENGS}
        self.gidx = 0

    def op(self, eng, fn, after=(), dma=False):
        deps = []
        for t in after:
            if t is None:
                continue
            if t["eng"] == eng and not t["dma"]:
                continue
            t["inc"] = True
            deps.append(t)
        rec = {"eng": eng, "fn": fn, "after": deps, "dma": dma,
               "inc": dma, "val": None, "gidx": self.gidx}
        self.gidx += 1
        self.ops[eng].append(rec)
        return rec

    def cut(self, maxg):
        kept = set()
        for e in ENGS:
            self.ops[e] = [r for r in self.ops[e] if r["gidx"] < maxg]
            for r in self.ops[e]:
                kept.add(id(r))
        for e in ENGS:
            for r in self.ops[e]:
                r["after"] = [t for t in r["after"] if id(t) in kept]

    def finalize(self):
        for e in ("V", "S", "G"):
            for rec in self.ops[e]:
                rec["inc"] = True
        for e in ENGS:
            run = 0
            for rec in self.ops[e]:
                if rec["inc"]:
                    run += 16 if rec["dma"] else 1
                rec["val"] = run
        tot = self.ops["DP"][-1]["val"] if self.ops["DP"] else 0
        for rec in self.ops["DP"]:
            rec["val"] = tot

    def replay_merged(self, eng, enames, sems):
        ops = sorted([r for e in enames for r in self.ops[e]],
                     key=lambda r: r["gidx"])
        self._replay_list(eng, ops, sems)

    def replay(self, eng, ename, sems):
        self._replay_list(eng, self.ops[ename], sems)

    def _replay_list(self, eng, ops, sems):
        selfserial = ops and ops[0]["eng"] in ("V", "S", "G")
        nprev = 0
        for rec in ops:
            seen = {}
            for t in rec["after"]:
                seen[t["eng"]] = max(seen.get(t["eng"], 0), t["val"])
            if selfserial and nprev > 0:
                seen[rec["eng"]] = max(seen.get(rec["eng"], 0), nprev)
            for key, v in seen.items():
                if v > 0:
                    eng.wait_ge(sems[key], v)
            instr = rec["fn"](eng)
            if rec["inc"]:
                instr.then_inc(sems[rec["eng"]], 16 if rec["dma"] else 1)
            if selfserial:
                nprev = rec["val"]


_CACHE = {}


def _mask_host():
    i = np.arange(128)[:, None]
    c = np.arange(384)[None, :]
    valid = (c > i) & (c <= i + 256)
    return np.where(valid, 0.0, NEG).astype(np.float32)


def _rope_tables_host():
    import jax
    import jax.numpy as jnp
    with jax.default_device(jax.devices("cpu")[0]):
        inv_freq = 1.0 / ROPE_BASE ** (jnp.arange(0, 64, 2,
                                                  dtype=jnp.float32) / 64)
        t = jnp.arange(S, dtype=jnp.float32)
        freqs = jnp.outer(t, inv_freq)
        cosj = np.asarray(jnp.cos(freqs))
        sinj = np.asarray(jnp.sin(freqs))
    ctbl = np.ascontiguousarray(
        cosj.reshape(T, 128, 32).transpose(1, 0, 2).reshape(128, T * 32))
    stbl = np.ascontiguousarray(
        sinj.reshape(T, 128, 32).transpose(1, 0, 2).reshape(128, T * 32))
    return ctbl, stbl


def build(phase="full"):
    import os as _os
    KCUT = int(_os.environ.get("KCUT", "0"))
    key = (phase, KCUT)
    if key in _CACHE:
        return _CACHE[key]
    nc = bass.Bass("TRN2", target_bir_lowering=False,
                   detect_race_conditions=False)
    sc = Script()

    def din(name, shape, dt=F32):
        return nc.dram_tensor(name, shape, dt, kind="ExternalInput").ap()

    x_d = din("x", [S, D])
    wqT_d = din("wqT", [D, D])
    wkT_d = din("wkT", [D, KVH * HD])
    wvT_d = din("wvT", [D, KVH * HD])
    woT_d = din("woT", [D, D])
    woTsw_d = din("woTsw", [D, D])
    wmod_d = din("wmod", [D, 6 * D])
    wg_d = din("wg", [DI, D])
    wu_d = din("wu", [DI, D])
    wd_d = din("wd", [D, DI])
    crow_d = din("crow", [1, D])
    gain_d = din("gain", [1, H])
    pm_d = din("pm", [1, 32])
    maskb_d = din("maskb", [128, 384])
    ctbl_d = din("ctbl", [128, T * 32])
    stbl_d = din("stbl", [128, T * 32])
    ident_d = din("ident", [128, 128], BF16)
    out_d = nc.dram_tensor("out", [S, D], F32, kind="ExternalOutput").ap()
    xprime_d = nc.dram_tensor("xprime", [S, D], F32).ap()
    dbg = {}
    if _DEBUG:
        for nm, shp in [("dbg_mod", [1, 6 * D]), ("dbg_h1T", [128, C8 * 128]),
                        ("dbg_q", [128, D]), ("dbg_k", [128, 256]),
                        ("dbg_y", [128, C8 * 128]), ("dbg_hq", [128, D]),
                        ("dbg_xp", [128, D])]:
            dbg[nm] = nc.dram_tensor(nm, shp, F32, kind="ExternalOutput").ap()

    import contextlib
    stack = contextlib.ExitStack()

    ident = stack.enter_context(nc.sbuf_tensor("ident_s", [128, 128], BF16))
    gbb = stack.enter_context(nc.sbuf_tensor("gbb", [128, D], F32))
    hqTb = stack.enter_context(nc.sbuf_tensor("hqTb", [128, C8 * S], BF16))

    ps_a = stack.enter_context(nc.psum_tensor("ps_a", [128, 1536], F32))
    ps_b = stack.enter_context(nc.psum_tensor("ps_b", [128, 1536], F32))
    ps_t = stack.enter_context(nc.psum_tensor("ps_t", [128, 1024], BF16))
    ps_c = stack.enter_context(nc.psum_tensor("ps_c", [128, 512], F32))

    ld_ident = sc.op("DP", lambda e: e.dma_start(out=ident[:], in_=ident_d),
                     dma=True)

    outer = contextlib.ExitStack()

    def sbo(name, shape, dt=F32):
        name = "sbo_" + name
        return outer.enter_context(nc.sbuf_tensor(name, shape, dt))

    maskb = sbo("maskb", [128, 384])
    sab = sbo("sab", [128, D])
    ca1b = sbo("ca1b", [128, D])
    gab = sbo("gab", [128, D])
    cb1b = sbo("cb1b", [128, D])
    sbb = sbo("sbb", [128, D])
    gainrep = sbo("gainrep", [128, H])
    pmA = sbo("pmA", [128, C8])
    pmB = sbo("pmB", [128, C8])
    cos_all = sbo("cos_all", [128, T * 32])
    sin_all = sbo("sin_all", [128, T * 32])
    x_ring = sbo("x_ring", [128, 3 * D])

    def x_t(t):
        return x_ring[:, D * (t % 3):D * (t % 3 + 1)]

    ld_mask = sc.op("DP", lambda e: e.dma_start(out=maskb[:], in_=maskb_d),
                    dma=True)
    ld_x = []
    for t in range(3):
        ld_x.append(sc.op("DP", lambda e, t=t: e.dma_start(
            out=x_t(t), in_=x_d[128 * t:128 * (t + 1), :]), dma=True))
    trig_cos = sc.op("DP", lambda e: e.dma_start(out=cos_all[:], in_=ctbl_d),
                     dma=True)
    trig_sin = sc.op("DP", lambda e: e.dma_start(out=sin_all[:], in_=stbl_d),
                     dma=True)

    setup = contextlib.ExitStack()

    def sb0(name, shape, dt=F32):
        name = "sb0_" + name
        return setup.enter_context(nc.sbuf_tensor(name, shape, dt))

    modrow = sb0("modrow", [1, 6 * D])
    c_col = sb0("c_col", [128, C8])
    silu_cb = sb0("silu_cb", [128, C8], BF16)
    ones_f = sb0("ones_f", [1, 128])
    stage = sb0("stage", [128, 4 * 512])
    stage_b = sb0("stage_b", [128, 4 * 512], BF16)
    garow = sb0("garow", [1, 2 * D])
    carow = sb0("carow", [1, 2 * D])
    tmp1 = sb0("tmp1", [1, 2 * D])
    tmp2 = sb0("tmp2", [1, 2 * D])
    grow = sb0("grow", [1, H])
    pmrow = sb0("pmrow", [1, 32])

    ld_c = sc.op("DP", lambda e: e.dma_start(
        out=c_col[:], in_=crow_d.rearrange("o (j p) -> (o p) j", p=128),
        allow_slow_non_contiguous=True), dma=True)
    ld_g = sc.op("DP", lambda e: e.dma_start(out=grow[:], in_=gain_d),
                 dma=True)
    ld_pm = sc.op("DP", lambda e: e.dma_start(out=pmrow[:], in_=pm_d),
                  dma=True)
    ones1 = sc.op("V", lambda e: e.memset(ones_f[:], 1.0))

    sl = sc.op("S", lambda e: e.activation(out=c_col[:], in_=c_col[:],
                                           func=ACT.Silu), after=[ld_c])
    slb = sc.op("V", lambda e: e.tensor_copy(out=silu_cb[:], in_=c_col[:]),
                after=[sl])

    ev_mod = []
    mm_hist = {}
    cst_hist = {}
    for j in range(12):
        mm_prev = None
        psum_j = ps_c[0:1, 0:512] if (j % 2 == 0) else ps_b[0:1, 0:512]
        for k in range(C8):
            gstep = j * C8 + k
            sl = gstep % 4
            ldst = sc.op("DM%d" % sl, lambda e, k=k, j=j, sl=sl: e.dma_start(
                out=stage[:, 512 * sl:512 * (sl + 1)],
                in_=wmod_d[128 * k:128 * (k + 1), 512 * j:512 * (j + 1)]),
                dma=True,
                after=[cst_hist[gstep - 4]] if gstep >= 4 else [])
            cst = sc.op("V", lambda e, sl=sl: e.tensor_copy(
                out=stage_b[:, 512 * sl:512 * (sl + 1)],
                in_=stage[:, 512 * sl:512 * (sl + 1)]),
                after=[ldst] + ([mm_hist[gstep - 4]] if gstep >= 4 else []))
            cst_hist[gstep] = cst
            mmdep = [cst, slb]
            if mm_prev is not None:
                mmdep.append(mm_prev)
            if k == 0 and j >= 2:
                mmdep.append(ev_mod[j - 2])
            mm_prev = sc.op("T", lambda e, k=k, sl=sl, psum_j=psum_j:
                            e.matmul(
                psum_j, silu_cb[:, k:k + 1],
                stage_b[:, 512 * sl:512 * (sl + 1)],
                start=(k == 0), stop=(k == C8 - 1)), after=mmdep)
            mm_hist[gstep] = mm_prev
        ev = sc.op("S", lambda e, j=j, psum_j=psum_j: e.copy(
            out=modrow[0:1, 512 * j:512 * (j + 1)], in_=psum_j),
            after=[mm_prev])
        ev_mod.append(ev)
    mA = list(ev_mod)

    g1 = sc.op("V", lambda e: e.tensor_copy(
        out=tmp1[0:1, 0:D], in_=modrow[0:1, 2 * D:3 * D]), after=mA)
    g2 = sc.op("V", lambda e: e.tensor_copy(
        out=tmp1[0:1, D:2 * D], in_=modrow[0:1, 5 * D:6 * D]), after=mA)
    sg1 = sc.op("V", lambda e: e.tensor_scalar(
        out=tmp2[:], in0=tmp1[:], scalar1=0.0, scalar2=None, op0=ALU.is_ge),
        after=[g1, g2])
    sg2 = sc.op("V", lambda e: e.tensor_scalar(
        out=tmp2[:], in0=tmp2[:], scalar1=2.0, scalar2=-1.0, op0=ALU.mult,
        op1=ALU.add), after=[sg1])
    mg1 = sc.op("S", lambda e: e.activation(out=tmp1[:], in_=tmp1[:],
                                            func=ACT.Abs),
                after=[g1, g2, sg1])
    mg2 = sc.op("V", lambda e: e.tensor_scalar(
        out=tmp1[:], in0=tmp1[:], scalar1=0.25, scalar2=None, op0=ALU.max),
        after=[mg1])
    gprod = sc.op("V", lambda e: e.tensor_tensor(
        out=garow[:], in0=tmp2[:], in1=tmp1[:], op=ALU.mult),
        after=[mg2, sg2])
    cc1 = sc.op("V", lambda e: e.tensor_copy(
        out=carow[0:1, 0:D], in_=modrow[0:1, D:2 * D]), after=mA)
    cc2 = sc.op("V", lambda e: e.tensor_copy(
        out=carow[0:1, D:2 * D], in_=modrow[0:1, 4 * D:5 * D]), after=mA)
    ca_p1 = sc.op("V", lambda e: e.tensor_scalar(
        out=carow[:], in0=carow[:], scalar1=1.0, scalar2=None, op0=ALU.add),
        after=[cc1, cc2])



    dbg_mod_tok = None
    if _DEBUG:
        dbg_mod_tok = sc.op("DS", lambda e: e.dma_start(
            out=dbg["dbg_mod"], in_=modrow[:]), dma=True, after=mA)

    psc_last = mA[-1]
    mod_ready = []

    def bcast(dst, row_ap, dep):
        nonlocal psc_last
        toks = []
        for hh in range(2):
            mm = sc.op("T", lambda e, hh=hh: e.matmul(
                ps_c[:, 0:512], ones_f[:], row_ap[0:1, 512 * hh:512 * (hh + 1)],
                start=True, stop=True), after=dep + [ones1, psc_last])
            cp = sc.op("V", lambda e, hh=hh: e.tensor_copy(
                out=dst[:, 512 * hh:512 * (hh + 1)], in_=ps_c[:, 0:512]),
                after=[mm])
            psc_last = cp
            toks.append(cp)
        return toks

    bc_gb = bcast(gbb, garow[0:1, D:2 * D], [gprod])
    mod_ready += bc_gb
    mod_ready += bcast(gab, garow[0:1, 0:D], [gprod])
    mod_ready += bcast(cb1b, carow[0:1, D:2 * D], [ca_p1])
    mod_ready += bcast(sbb, modrow[0:1, 3 * D:4 * D], mA)
    mod_ready += bcast(ca1b, carow[0:1, 0:D], [ca_p1])
    mod_ready += bcast(sab, modrow[0:1, 0:D], mA)

    mmg = sc.op("T", lambda e: e.matmul(
        ps_c[:, 0:H], ones_f[:], grow[:], start=True, stop=True),
        after=[ld_g, ones1, psc_last])
    grep = sc.op("V", lambda e: e.tensor_scalar(
        out=gainrep[:], in0=ps_c[:, 0:H], scalar1=0.125, scalar2=None,
        op0=ALU.mult), after=[mmg])
    psc_last = grep
    mod_ready.append(grep)

    for dst, etop, ebot in ((pmA, 0, 3), (pmB, 2, 1)):
        mt = sc.op("T", lambda e, etop=etop: e.matmul(
            ps_c[0:64, 0:C8], ones_f[0:1, 0:64], pmrow[0:1, etop:etop + 29:4],
            start=True, stop=True), after=[ld_pm, ones1, psc_last])
        mb = sc.op("T", lambda e, ebot=ebot: e.matmul(
            ps_c[64:128, 0:C8], ones_f[0:1, 0:64], pmrow[0:1, ebot:ebot + 29:4],
            start=True, stop=True, tile_position=(0, 64)), after=[mt])
        cp = sc.op("V", lambda e, dst=dst: e.tensor_copy(
            out=dst[:], in_=ps_c[:, 0:C8]), after=[mb])
        psc_last = cp
        mod_ready.append(cp)

    trig_ready = [trig_sin, trig_cos]
    first_gate = mod_ready + trig_ready + ([dbg_mod_tok] if dbg_mod_tok else [])

    setup.close()

    attn = contextlib.ExitStack()

    def sba(name, shape, dt=F32):
        name = "sba_" + name
        return attn.enter_context(nc.sbuf_tensor(name, shape, dt))

    wqb = sba("wqb", [128, C8 * D], BF16)
    wkb = sba("wkb", [128, C8 * 256], BF16)
    wvb = sba("wvb", [128, C8 * 256], BF16)
    wob = sba("wob", [128, C8 * D], BF16)
    kTr = sba("kTr", [128, KVH * 4 * 128], BF16)
    vtr = sba("vtr", [128, 4 * 256], BF16)
    stg = sba("stg", [128, 1024])
    xn = sba("xn", [128, D], BF16)
    h1T = sba("h1T", [128, C8 * 128], BF16)
    tmq = sba("tmq", [128, D])
    tmq2 = sba("tmq2", [128, D])
    qrope = sba("qrope", [128, D], BF16)
    krope2 = sba("krope2", [128, 512], BF16)
    qT = sba("qT", [128, C8 * 128], BF16)
    red16 = sba("red16", [128, 16])
    red16b = sba("red16b", [128, 16])
    red16c = sba("red16c", [128, 16])
    ln_sc = sba("ln_sc", [128, 8])
    msS = sba("msS", [128, 384])
    msS2 = sba("msS2", [128, 384])
    Pb = sba("Pb", [128, 384], BF16)
    Pb2 = sba("Pb2", [128, 384], BF16)
    PTs = sba("PTs", [128, 384], BF16)
    PTs2 = sba("PTs2", [128, 384], BF16)
    rsum = sba("rsum", [128, 2])
    mrow = sba("mrow", [128, 2])
    yT = sba("yT", [128, C8 * 128], BF16)
    h2f = sba("h2f", [128, D])
    xnf = sba("xnf", [128, D])
    Pf = sba("Pf", [128, 384])
    Pf2 = sba("Pf2", [128, 384])
    qb0 = sba("qb0", [128, D], BF16)
    qb1 = sba("qb1", [128, D], BF16)
    sgq = sba("sgq", [128, D], BF16)
    m64 = sba("m64", [128, 64])
    s64 = sba("s64", [128, 64])
    r64 = sba("r64", [128, 64])

    wtoks = []

    def loadcast(dst, src_d, ncol):
        toks = []
        for j in range(C8):
            dep = ([toks[-1]] if toks else
                   ([wtoks[-1]] if wtoks else list(first_gate)))
            l = sc.op("DW", lambda e, j=j, ncol=ncol: e.dma_start(
                out=stg[:, 0:ncol], in_=src_d[128 * j:128 * (j + 1), :]),
                dma=True, after=dep)
            cvt = sc.op("V", lambda e, j=j, ncol=ncol: e.tensor_copy(
                out=dst[:, ncol * j:ncol * (j + 1)], in_=stg[:, 0:ncol]),
                after=[l])
            toks.append(cvt)
        return toks

    wtoks += loadcast(wqb, wqT_d, D)
    wtoks += loadcast(wkb, wkT_d, 256)
    wtoks += loadcast(wvb, wvT_d, 256)
    for j in range(C8):
        l1 = sc.op("DW", lambda e, j=j: e.dma_start(
            out=stg[:, 0:D], in_=woT_d[128 * j:128 * (j + 1), :]), dma=True,
            after=[wtoks[-1]] if j == 0 else [wtoks[-1]])
        l2 = sc.op("DW", lambda e, j=j: e.dma_start(
            out=tmq[:], in_=woTsw_d[128 * j:128 * (j + 1), :]), dma=True,
            after=[l1] + (list(first_gate) if j == 0 else []))
        t1 = sc.op("V", lambda e, j=j: e.tensor_scalar(
            out=tmq[:], in0=tmq[:], scalar1=pmB[:, j:j + 1], scalar2=None,
            op0=ALU.mult), after=[l2] + mod_ready)
        t2 = sc.op("V", lambda e, j=j: e.scalar_tensor_tensor(
            out=wob[:, D * j:D * (j + 1)], in0=stg[:, 0:D],
            scalar=pmA[:, j:j + 1], in1=tmq[:], op0=ALU.mult, op1=ALU.add),
            after=[t1])
        wtoks.append(t2)

    def cview(buf, j, n):
        return buf[:].rearrange("p (j n) -> p j n", n=n)[:, j, :]

    def hqT_v(j, t):
        return hqTb[:].rearrange("p (j s) -> p j s", j=C8)[:, j, 128 * t:128 * (t + 1)]

    def emit_ln(xsrc, dst, col, dep):
        a = ln_sc[:, col:col + 1]
        b = ln_sc[:, col + 1:col + 2]
        c_ = ln_sc[:, col + 2:col + 3]
        d_ = ln_sc[:, col + 3:col + 4]
        t1 = sc.op("V", lambda e: e.tensor_reduce(
            out=a, in_=xsrc, axis=AX, op=ALU.add), after=dep)
        t2 = sc.op("S", lambda e: e.activation(
            out=stg[:, 0:D], in_=xsrc, func=ACT.Square), after=dep)
        t3 = sc.op("V", lambda e: e.tensor_reduce(
            out=b, in_=stg[:, 0:D], axis=AX, op=ALU.add), after=[t2])
        t4 = sc.op("V", lambda e: e.tensor_scalar(
            out=a, in0=a, scalar1=1.0 / D, scalar2=None, op0=ALU.mult),
            after=[t1])
        t5 = sc.op("V", lambda e: e.tensor_scalar(
            out=b, in0=b, scalar1=1.0 / D, scalar2=None, op0=ALU.mult),
            after=[t3])
        t6 = sc.op("V", lambda e: e.tensor_tensor(
            out=c_, in0=a, in1=a, op=ALU.mult), after=[t4])
        t7 = sc.op("V", lambda e: e.scalar_tensor_tensor(
            out=b, in0=c_, scalar=-1.0, in1=b, op0=ALU.mult, op1=ALU.add),
            after=[t5, t6])
        t8 = sc.op("V", lambda e: e.tensor_scalar(
            out=b, in0=b, scalar1=LN_EPS, scalar2=None, op0=ALU.add),
            after=[t7])
        t9 = sc.op("S", lambda e: e.activation(
            out=c_, in_=b, func=ACT.Sqrt), after=[t8])
        tok = sc.op("V", lambda e: e.reciprocal(out=c_, in_=c_), after=[t9])
        for _ in range(2):
            n1 = sc.op("V", lambda e: e.tensor_tensor(
                out=d_, in0=c_, in1=c_, op=ALU.mult), after=[tok])
            n2 = sc.op("V", lambda e: e.tensor_tensor(
                out=d_, in0=d_, in1=b, op=ALU.mult), after=[n1])
            n3 = sc.op("V", lambda e: e.tensor_scalar(
                out=d_, in0=d_, scalar1=-0.5, scalar2=1.5, op0=ALU.mult,
                op1=ALU.add), after=[n2])
            tok = sc.op("V", lambda e: e.tensor_tensor(
                out=c_, in0=c_, in1=d_, op=ALU.mult), after=[n3])
        nb = sc.op("V", lambda e: e.scalar_tensor_tensor(
            out=d_, in0=a, scalar=-1.0, in1=c_, op0=ALU.mult, op1=ALU.mult),
            after=[tok])
        xno = sc.op("V", lambda e: e.tensor_scalar(
            out=dst, in0=xsrc, scalar1=c_, scalar2=d_, op0=ALU.mult,
            op1=ALU.add), after=[nb])
        return xno

    def emit_quant(src_ap, dst_bf, width, dep, avq, aq, b0a, b1a, sga, wsa,
                   mm_, ss_, rr_, qa_eng="G"):
        nb16 = width // 16
        v3 = src_ap.rearrange("p (b k) -> p b k", k=16)
        q1 = sc.op("V", lambda e: e.tensor_reduce(
            out=mm_, in_=v3, axis=AX, op=ALU.max, apply_absolute_value=True),
            after=dep)
        q2a = sc.op("V", lambda e: e.tensor_scalar(
            out=ss_, in0=mm_, scalar1=C12_HI, scalar2=None, op0=ALU.mult),
            after=[q1])
        q2 = sc.op("V", lambda e: e.scalar_tensor_tensor(
            out=ss_, in0=mm_, scalar=C12_LO, in1=ss_, op0=ALU.mult,
            op1=ALU.add), after=[q2a])
        q2 = sc.op("V", lambda e: e.tensor_scalar(
            out=ss_, in0=ss_, scalar1=CLAMP_HALF, scalar2=None, op0=ALU.max),
            after=[q2])
        q3 = sc.op("V", lambda e: e.reciprocal(out=rr_, in_=ss_), after=[q2])
        av = sc.op("S", lambda e: e.activation(
            out=avq, in_=src_ap, func=ACT.Abs), after=dep)
        sg = sc.op("S", lambda e: e.activation(
            out=sga, in_=src_ap, func=ACT.Sign), after=dep + [av])
        rbc = rr_.unsqueeze(2).to_broadcast((128, nb16, 16))
        sbc = ss_.unsqueeze(2).to_broadcast((128, nb16, 16))
        qa = sc.op(qa_eng, lambda e: e.tensor_tensor(
            out=aq.rearrange("p (b k) -> p b k", k=16),
            in0=avq.rearrange("p (b k) -> p b k", k=16), in1=rbc,
            op=ALU.mult), after=[av, q3])
        tokA = sc.op("V", lambda e: e.tensor_scalar(
            out=b0a, in0=aq, scalar1=0.5, scalar2=None, op0=ALU.is_gt),
            after=[qa])
        for thr in (1.5, 2.5, 3.5):
            tokA = sc.op("V", lambda e, thr=thr: e.scalar_tensor_tensor(
                out=b0a, in0=aq, scalar=thr, in1=b0a, op0=ALU.is_gt,
                op1=ALU.add), after=[tokA])
        tokB = sc.op("V", lambda e: e.tensor_scalar(
            out=b1a, in0=aq, scalar1=5.0, scalar2=None, op0=ALU.is_gt),
            after=[qa, tokA])
        for thr in (7.0, 10.0, 10.0):
            tokB = sc.op("V", lambda e, thr=thr: e.scalar_tensor_tensor(
                out=b1a, in0=aq, scalar=thr, in1=b1a, op0=ALU.is_gt,
                op1=ALU.add), after=[tokB])
        tok = sc.op("V", lambda e: e.scalar_tensor_tensor(
            out=b0a, in0=b1a, scalar=2.0, in1=b0a, op0=ALU.mult,
            op1=ALU.add), after=[tokB])
        w1 = sc.op("G", lambda e: e.tensor_tensor(
            out=wsa.rearrange("p (b k) -> p b k", k=16),
            in0=sga.rearrange("p (b k) -> p b k", k=16), in1=sbc,
            op=ALU.mult), after=[sg, q3, qa])
        dq = sc.op("V", lambda e: e.tensor_tensor(
            out=dst_bf, in0=b0a, in1=wsa, op=ALU.mult), after=[w1, tok])
        return dq, sg

    xp_stores = []
    hq_done = []
    tile_end = {}
    prev_ps_t_reader = ld_ident
    qT_e = kT_e = None

    NT = 0 if phase == "setup" else (1 if phase == "attn1" else T)
    _CACHE.setdefault("marks", []).clear()
    for t in range(NT):
        _CACHE["marks"].append(("tile", t, sc.gidx))
        lo = max(0, t - 2)
        nkt = t - lo + 1
        W = 128 * nkt
        moff = 384 - W
        if t >= 3:
            ld_x.append(sc.op("DL", lambda e, t=t: e.dma_start(
                out=x_t(t), in_=x_d[128 * t:128 * (t + 1), :]), dma=True,
                after=list(tile_end[t - 3])))

        dep0 = [ld_x[t]] + wtoks
        xno = emit_ln(x_t(t), xnf[:], 0, dep0)
        hma = sc.op("V", lambda e: e.tensor_tensor(
            out=xnf[:], in0=xnf[:], in1=ca1b[:], op=ALU.mult),
            after=[xno] + mod_ready)
        hmb = sc.op("V", lambda e: e.tensor_tensor(
            out=xn[:], in0=xnf[:], in1=sab[:], op=ALU.add), after=[hma])

        tr = []
        for j in range(C8):
            tr.append(sc.op("T", lambda e, j=j: e.transpose(
                ps_t[:, 128 * j:128 * (j + 1)], xn[:, 128 * j:128 * (j + 1)],
                ident[:]), after=[hmb, ld_ident, prev_ps_t_reader]))
        h1toks = []
        for j in range(C8):
            h1toks.append(sc.op("V", lambda e, j=j: e.tensor_copy(
                out=cview(h1T, j, 128), in_=ps_t[:, 128 * j:128 * (j + 1)]),
                after=tr))
        prev_ps_t_reader = h1toks[-1]

        mmq = []
        for j in range(C8):
            st = (j == 0)
            sp = (j == C8 - 1)
            mmq.append(sc.op("T", lambda e, j=j, st=st, sp=sp: e.matmul(
                ps_a[:, 0:512], cview(h1T, j, 128), cview(wqb, j, D)[:, 0:512],
                start=st, stop=sp), after=[h1toks[j]] + wtoks))
            mmq.append(sc.op("T", lambda e, j=j, st=st, sp=sp: e.matmul(
                ps_a[:, 512:1024], cview(h1T, j, 128),
                cview(wqb, j, D)[:, 512:1024], start=st, stop=sp),
                after=[h1toks[j]]))
            mmq.append(sc.op("T", lambda e, j=j, st=st, sp=sp: e.matmul(
                ps_a[:, 1024:1280], cview(h1T, j, 128), cview(wkb, j, 256),
                start=st, stop=sp), after=[h1toks[j]]))
            mmq.append(sc.op("T", lambda e, j=j, st=st, sp=sp: e.matmul(
                ps_b[:, 512:768], cview(h1T, j, 128), cview(wvb, j, 256),
                start=st, stop=sp), after=[h1toks[j]]))
        qkv_done = mmq[-4:]

        sq1 = sc.op("S", lambda e: e.activation(
            out=tmq[:], in_=ps_a[:, 0:1024], func=ACT.Square), after=qkv_done)
        r1 = sc.op("V", lambda e: e.tensor_reduce(
            out=red16[:], in_=tmq[:].rearrange("p (h d) -> p h d", d=HD),
            axis=AX, op=ALU.add), after=[sq1])
        r2 = sc.op("V", lambda e: e.tensor_scalar(
            out=red16[:], in0=red16[:], scalar1=1.0 / HD, scalar2=RMS_EPS,
            op0=ALU.mult, op1=ALU.add), after=[r1])
        r3 = sc.op("S", lambda e: e.activation(
            out=red16b[:], in_=red16[:], func=ACT.Sqrt), after=[r2])
        tok = sc.op("V", lambda e: e.reciprocal(
            out=red16b[:], in_=red16b[:]), after=[r3])
        for _ in range(2):
            n1 = sc.op("V", lambda e: e.tensor_tensor(
                out=red16c[:], in0=red16b[:], in1=red16b[:], op=ALU.mult),
                after=[tok])
            n2 = sc.op("V", lambda e: e.tensor_tensor(
                out=red16c[:], in0=red16c[:], in1=red16[:], op=ALU.mult),
                after=[n1])
            n3 = sc.op("V", lambda e: e.tensor_scalar(
                out=red16c[:], in0=red16c[:], scalar1=-0.5, scalar2=1.5,
                op0=ALU.mult, op1=ALU.add), after=[n2])
            tok = sc.op("V", lambda e: e.tensor_tensor(
                out=red16b[:], in0=red16b[:], in1=red16c[:], op=ALU.mult),
                after=[n3])
        gcom = sc.op("V", lambda e: e.tensor_tensor(
            out=red16b[:], in0=red16b[:], in1=gainrep[:], op=ALU.mult),
            after=[tok] + mod_ready)
        qh = sc.op("V", lambda e: e.tensor_tensor(
            out=tmq[:].rearrange("p (h d) -> p h d", d=HD),
            in0=ps_a[:, 0:1024].rearrange("p (h d) -> p h d", d=HD),
            in1=red16b[:].unsqueeze(2).to_broadcast((128, H, HD)),
            op=ALU.mult), after=[gcom, sq1])

        cosv = cos_all[:, 32 * t:32 * (t + 1)].unsqueeze(1).to_broadcast((128, H, 32))
        sinv = sin_all[:, 32 * t:32 * (t + 1)].unsqueeze(1).to_broadcast((128, H, 32))
        qv = tmq[:].rearrange("p (h d) -> p h d", d=HD)
        w2v = tmq2[:].rearrange("p (h d) -> p h d", d=HD)
        o1 = qrope[:].rearrange("p (h d) -> p h d", d=HD)[:, :, 0:32]
        o2 = qrope[:].rearrange("p (h d) -> p h d", d=HD)[:, :, 32:64]
        rp1 = sc.op("V", lambda e, cosv=cosv: e.tensor_tensor(
            out=w2v[:, :, 0:32], in0=qv[:, :, 0:32], in1=cosv, op=ALU.mult),
            after=[qh] + trig_ready)
        rp2 = sc.op("V", lambda e, sinv=sinv: e.tensor_tensor(
            out=w2v[:, :, 32:64], in0=qv[:, :, 32:64], in1=sinv, op=ALU.mult),
            after=[qh])
        rp3 = sc.op("V", lambda e: e.tensor_tensor(
            out=o1, in0=w2v[:, :, 0:32], in1=w2v[:, :, 32:64], op=ALU.add),
            after=[rp1, rp2])
        rp4 = sc.op("V", lambda e, cosv=cosv: e.tensor_tensor(
            out=w2v[:, :, 0:32], in0=qv[:, :, 32:64], in1=cosv, op=ALU.mult),
            after=[rp3])
        rp5 = sc.op("V", lambda e, sinv=sinv: e.tensor_tensor(
            out=w2v[:, :, 32:64], in0=qv[:, :, 0:32], in1=sinv, op=ALU.mult),
            after=[rp3])
        rp6 = sc.op("V", lambda e: e.tensor_tensor(
            out=o2, in0=w2v[:, :, 0:32], in1=w2v[:, :, 32:64],
            op=ALU.subtract), after=[rp4, rp5])
        q_done = [rp3, rp6]

        sk1 = sc.op("S", lambda e: e.activation(
            out=tmq[:, 0:256], in_=ps_a[:, 1024:1280], func=ACT.Square),
            after=qkv_done + [rp6])
        k1 = sc.op("V", lambda e: e.tensor_reduce(
            out=red16[:, 0:4],
            in_=tmq[:, 0:256].rearrange("p (h d) -> p h d", d=HD), axis=AX,
            op=ALU.add), after=[sk1])
        k2 = sc.op("V", lambda e: e.tensor_scalar(
            out=red16[:, 0:4], in0=red16[:, 0:4], scalar1=1.0 / HD,
            scalar2=RMS_EPS, op0=ALU.mult, op1=ALU.add), after=[k1])
        k3 = sc.op("S", lambda e: e.activation(
            out=red16b[:, 0:4], in_=red16[:, 0:4], func=ACT.Sqrt), after=[k2])
        tok = sc.op("V", lambda e: e.reciprocal(
            out=red16b[:, 0:4], in_=red16b[:, 0:4]), after=[k3])
        for _ in range(2):
            n1 = sc.op("V", lambda e: e.tensor_tensor(
                out=red16c[:, 0:4], in0=red16b[:, 0:4], in1=red16b[:, 0:4],
                op=ALU.mult), after=[tok])
            n2 = sc.op("V", lambda e: e.tensor_tensor(
                out=red16c[:, 0:4], in0=red16c[:, 0:4], in1=red16[:, 0:4],
                op=ALU.mult), after=[n1])
            n3 = sc.op("V", lambda e: e.tensor_scalar(
                out=red16c[:, 0:4], in0=red16c[:, 0:4], scalar1=-0.5,
                scalar2=1.5, op0=ALU.mult, op1=ALU.add), after=[n2])
            tok = sc.op("V", lambda e: e.tensor_tensor(
                out=red16b[:, 0:4], in0=red16b[:, 0:4], in1=red16c[:, 0:4],
                op=ALU.mult), after=[n3])
        khx = sc.op("V", lambda e: e.tensor_tensor(
            out=tmq[:, 0:256].rearrange("p (h d) -> p h d", d=HD),
            in0=ps_a[:, 1024:1280].rearrange("p (h d) -> p h d", d=HD),
            in1=red16b[:, 0:4].unsqueeze(2).to_broadcast((128, 4, HD)),
            op=ALU.mult), after=[tok, sk1])
        kv4 = tmq[:, 0:256].rearrange("p (h d) -> p h d", d=HD)
        kw4 = tmq2[:, 0:256].rearrange("p (h d) -> p h d", d=HD)
        cos4 = cos_all[:, 32 * t:32 * (t + 1)].unsqueeze(1).to_broadcast((128, 4, 32))
        sin4 = sin_all[:, 32 * t:32 * (t + 1)].unsqueeze(1).to_broadcast((128, 4, 32))
        kr2v = krope2[:].rearrange("p (g r d) -> p g r d", g=4, r=2)
        kp1 = sc.op("V", lambda e, cos4=cos4: e.tensor_tensor(
            out=kw4[:, :, 0:32], in0=kv4[:, :, 0:32], in1=cos4, op=ALU.mult),
            after=[khx])
        kp2 = sc.op("V", lambda e, sin4=sin4: e.tensor_tensor(
            out=kw4[:, :, 32:64], in0=kv4[:, :, 32:64], in1=sin4,
            op=ALU.mult), after=[khx])
        kp3 = sc.op("V", lambda e: e.tensor_tensor(
            out=kr2v[:, :, 0, 0:32], in0=kw4[:, :, 0:32],
            in1=kw4[:, :, 32:64], op=ALU.add), after=[kp1, kp2])
        kp4 = sc.op("V", lambda e, cos4=cos4: e.tensor_tensor(
            out=kw4[:, :, 0:32], in0=kv4[:, :, 32:64], in1=cos4, op=ALU.mult),
            after=[kp3])
        kp5 = sc.op("V", lambda e, sin4=sin4: e.tensor_tensor(
            out=kw4[:, :, 32:64], in0=kv4[:, :, 0:32], in1=sin4,
            op=ALU.mult), after=[kp3])
        kp6 = sc.op("V", lambda e: e.tensor_tensor(
            out=kr2v[:, :, 0, 32:64], in0=kw4[:, :, 0:32],
            in1=kw4[:, :, 32:64], op=ALU.subtract), after=[kp4, kp5])
        kp7 = sc.op("V", lambda e: e.tensor_copy(
            out=kr2v[:, :, 1, :], in_=kr2v[:, :, 0, :]), after=[kp3, kp6])
        vev = sc.op("V", lambda e, t=t: e.tensor_copy(
            out=vtr[:, 256 * (t % 4):256 * (t % 4 + 1)],
            in_=ps_b[:, 512:768]), after=qkv_done + [khx])

        trq = []
        for j in range(C8):
            trq.append(sc.op("T", lambda e, j=j: e.transpose(
                ps_t[:, 128 * j:128 * (j + 1)], qrope[:, 128 * j:128 * (j + 1)],
                ident[:]), after=q_done + [prev_ps_t_reader]))
        qT_e = sc.op("V", lambda e: e.tensor_copy(out=qT[:], in_=ps_t[:]),
                     after=trq)
        trk = []
        for g in range(4):
            trk.append(sc.op("T", lambda e, g=g: e.transpose(
                ps_t[:, 128 * g:128 * (g + 1)],
                krope2[:, 128 * g:128 * (g + 1)], ident[:]),
                after=[kp7, qT_e]))
        kT_e = sc.op("V", lambda e, t=t: e.tensor_copy(
            out=kTr[:].rearrange("p (g r n) -> p g r n", g=KVH, r=4)[:, :, t % 4, :],
            in_=ps_t[:, 0:512].rearrange("p (g n) -> p g n", g=4)), after=trk)

        yready = []
        msk_h = {}
        expo_h = {}
        pte_h = {}
        mmy_h = {}
        yev_h = {}
        msS_p, Pf_p, Pb_p, PTs_p = (msS, msS2), (Pf, Pf2), (Pb, Pb2), (PTs, PTs2)

        def head_front(h):
            g = h // 4
            qoff = 64 * (h % 2)
            qc = h // 2
            par = h % 2
            so = 512 * par
            mm_s = []
            for kt2 in range(nkt):
                slot = (lo + kt2) % 4
                mm_s.append(sc.op("T", lambda e, kt2=kt2, slot=slot, g=g,
                                  qoff=qoff, qc=qc, so=so: e.matmul(
                    ps_b[:, so + 128 * kt2:so + 128 * (kt2 + 1)],
                    cview(qT, qc, 128)[qoff:qoff + 64, :],
                    kTr[:].rearrange("p (g r n) -> p g r n", g=KVH, r=4)
                    [qoff:qoff + 64, g, slot, :],
                    start=True, stop=True),
                    after=[qT_e, kT_e, vev]
                    + ([msk_h[h - 2]] if h - 2 in msk_h else [])))
            ms_buf = msS_p[par]
            msk = sc.op("V", lambda e, W=W, moff=moff, ms_buf=ms_buf, so=so:
                        e.tensor_tensor(
                out=ms_buf[:, 0:W], in0=maskb[:, moff:moff + W],
                in1=ps_b[:, so:so + W], op=ALU.add),
                after=mm_s + [ld_mask]
                + ([expo_h[h - 2]] if h - 2 in expo_h else []))
            msk_h[h] = msk
            mx = sc.op("V", lambda e, W=W, ms_buf=ms_buf, par=par:
                       e.tensor_reduce(
                out=mrow[:, par:par + 1], in_=ms_buf[:, 0:W], axis=AX,
                op=ALU.max, negate=True), after=[msk])
            expo_h[h] = sc.op("S", lambda e, W=W, ms_buf=ms_buf,
                              pf=Pf_p[par], par=par: e.activation(
                out=pf[:, 0:W], in_=ms_buf[:, 0:W], func=ACT.Exp,
                bias=mrow[:, par:par + 1], scale=1.0,
                accum_out=rsum[:, par:par + 1]), after=[mx])

        def head_back(h):
            g = h // 4
            qc = h // 2
            par = h % 2
            po = 64 * par
            pto = 512 * par
            pf, pb, pts = Pf_p[par], Pb_p[par], PTs_p[par]
            rr = sc.op("V", lambda e, par=par: e.reciprocal(
                out=rsum[:, par:par + 1], in_=rsum[:, par:par + 1]),
                after=[expo_h[h]])
            pn = sc.op("S", lambda e, W=W, pf=pf, pb=pb, par=par:
                       e.activation(
                out=pb[:, 0:W], in_=pf[:, 0:W], func=ACT.Copy,
                scale=rsum[:, par:par + 1]),
                after=[rr] + ([pte_h[h - 2]] if h - 2 in pte_h else []))
            trp = []
            for kt2 in range(nkt):
                trp.append(sc.op("T", lambda e, kt2=kt2, pb=pb, pto=pto:
                                 e.transpose(
                    ps_t[:, pto + 128 * kt2:pto + 128 * (kt2 + 1)],
                    pb[:, 128 * kt2:128 * (kt2 + 1)], ident[:]),
                    after=[pn, kT_e]
                    + ([pte_h[h - 2]] if h - 2 in pte_h else [])))
            PT_e = sc.op("V", lambda e, W=W, pts=pts, pto=pto: e.tensor_copy(
                out=pts[:, 0:W], in_=ps_t[:, pto:pto + W]),
                after=trp + (mmy_h[h - 2] if h - 2 in mmy_h else []))
            pte_h[h] = PT_e
            ybank = ps_b if par == 0 else ps_a
            mm_y = []
            for kt2 in range(nkt):
                slot = (lo + kt2) % 4
                kwargs = {"tile_position": (0, 64)} if po else {}
                mm_y.append(sc.op("T", lambda e, kt2=kt2, slot=slot, g=g,
                                  po=po, kw=kwargs, pts=pts, ybank=ybank,
                                  nkt=nkt: e.matmul(
                    ybank[po:po + 64, 1024:1152],
                    vtr[:, 256 * slot + 64 * g:256 * slot + 64 * (g + 1)],
                    pts[:, 128 * kt2:128 * (kt2 + 1)],
                    start=(kt2 == 0), stop=(kt2 == nkt - 1), **kw),
                    after=[PT_e, vev]
                    + ([yev_h[h - 2]] if h - 2 in yev_h else [])))
            mmy_h[h] = mm_y
            yev = sc.op("V", lambda e, qc=qc, po=po, ybank=ybank:
                        e.tensor_copy(
                out=cview(yT, qc, 128)[po:po + 64, :],
                in_=ybank[po:po + 64, 1024:1152]), after=mm_y)
            yev_h[h] = yev
            yready.append(yev)

        head_front(0)
        head_front(1)
        for h in range(2, H):
            head_back(h - 2)
            head_front(h)
        head_back(H - 2)
        head_back(H - 1)
        PT_prev = pte_h[H - 1]
        PT_prev2 = pte_h[H - 2]

        mwo = []
        for j in range(C8):
            st = (j == 0)
            sp = (j == C8 - 1)
            mwo.append(sc.op("T", lambda e, j=j, st=st, sp=sp: e.matmul(
                ps_a[:, 0:512], cview(yT, j, 128), cview(wob, j, D)[:, 0:512],
                start=st, stop=sp), after=yready))
            mwo.append(sc.op("T", lambda e, j=j, st=st, sp=sp: e.matmul(
                ps_a[:, 512:1024], cview(yT, j, 128),
                cview(wob, j, D)[:, 512:1024], start=st, stop=sp),
                after=yready))
        att_done = mwo[-2:]
        rs1 = sc.op("V", lambda e: e.tensor_tensor(
            out=stg[:, 0:D], in0=ps_a[:, 0:1024], in1=gab[:], op=ALU.mult),
            after=att_done + mod_ready)
        rs2 = sc.op("V", lambda e, t=t: e.tensor_tensor(
            out=x_t(t), in0=x_t(t), in1=stg[:, 0:D], op=ALU.add),
            after=[rs1])
        xst = sc.op("DS", lambda e, t=t: e.dma_start(
            out=xprime_d[128 * t:128 * (t + 1), :], in_=x_t(t)), dma=True,
            after=[rs2])
        xp_stores.append(xst)

        xn2 = emit_ln(x_t(t), h2f[:], 4, [rs2])
        hm1 = sc.op("V", lambda e: e.tensor_tensor(
            out=h2f[:], in0=h2f[:], in1=cb1b[:], op=ALU.mult),
            after=[xn2] + mod_ready)
        hm2 = sc.op("V", lambda e: e.tensor_tensor(
            out=h2f[:], in0=h2f[:], in1=sbb[:], op=ALU.add), after=[hm1])
        dq, q_sg = emit_quant(h2f[:], xn[:], D, [hm2], tmq[:], tmq2[:],
                              qb0[:], qb1[:], sgq[:], tmq[:], m64[:], s64[:],
                              r64[:])
        trh = []
        for j in range(C8):
            trh.append(sc.op("T", lambda e, j=j: e.transpose(
                ps_t[:, 128 * j:128 * (j + 1)], xn[:, 128 * j:128 * (j + 1)],
                ident[:]), after=[dq, PT_prev, PT_prev2] + mwo))
        hqT_e = sc.op("V", lambda e, t=t: e.tensor_copy(
            out=hqTb[:].rearrange("p (j s) -> p j s", j=C8)[:, :, 128 * t:128 * (t + 1)],
            in_=ps_t[:].rearrange("p (j n) -> p j n", j=C8)), after=trh)
        hq_done.append(hqT_e)
        prev_ps_t_reader = hqT_e
        tile_end[t] = (hqT_e, q_sg, xst)

        if _DEBUG and t == 0:
            _tap_prev = [hqT_e]

            def tap(name, src_ap, deps, cast=True):
                wdt = 1
                for s_ in src_ap.shape[1:]:
                    wdt *= int(s_)
                if cast:
                    cpy = sc.op("V", lambda e, src_ap=src_ap, wdt=wdt: e.tensor_copy(
                        out=stg[:, 0:wdt], in_=src_ap),
                        after=deps + _tap_prev)
                    st_ = sc.op("DS", lambda e, name=name, wdt=wdt: e.dma_start(
                        out=dbg[name], in_=stg[:, 0:wdt]),
                        dma=True, after=[cpy])
                else:
                    st_ = sc.op("DS", lambda e, name=name, src_ap=src_ap: e.dma_start(
                        out=dbg[name], in_=src_ap), dma=True,
                        after=deps + _tap_prev)
                _tap_prev[:] = [st_]
            tap("dbg_h1T", h1T[:], h1toks)
            tap("dbg_q", qrope[:], q_done)
            tap("dbg_k",
                krope2[:].rearrange("p (g r d) -> p g r d", g=4, r=2)[:, :, 0, :],
                [kp7])
            tap("dbg_y", yT[:], yready)
            tap("dbg_hq", xn[:], [dq])
            tap("dbg_xp", x_t(t), [rs2, xst], cast=False)

    attn_exit = list(hq_done) + xp_stores + ([qT_e, kT_e] if NT else [])
    if not attn_exit:
        attn_exit = first_gate
    attn.close()
    outer.close()

    ffn = contextlib.ExitStack()

    def sbf(name, shape, dt=F32):
        name = "sbf_" + name
        return ffn.enter_context(nc.sbuf_tensor(name, shape, dt))

    wgT = sbf("wgT", [128, C8 * DI], BF16)
    wuT = sbf("wuT", [128, C8 * DI], BF16)
    wdT = sbf("wdT", [128, C22 * D], BF16)
    fstg = sbf("fstg", [128, 1024])
    wqnt = sbf("wqnt", [128, 1024], BF16)
    favq = sbf("favq", [128, 1024])
    fb0 = sbf("fb0", [128, 1024], BF16)
    fb1 = sbf("fb1", [128, 1024], BF16)
    fsg = sbf("fsg", [128, 1024], BF16)
    fm = sbf("fm", [128, 64])
    fs = sbf("fs", [128, 64])
    fr = sbf("fr", [128, 64])
    fstg2 = sbf("fstg2", [128, 1024])
    fsg2 = sbf("fsg2", [128, 1024], BF16)
    fm2 = sbf("fm2", [128, 64])
    fs2 = sbf("fs2", [128, 64])
    fr2 = sbf("fr2", [128, 64])
    xpt = sbf("xpt", [128, D])
    favq2 = xpt
    fh3 = fstg2
    fsets = ((fstg, wqnt, favq, fstg, fb0, fb1, fsg, fm, fs, fr),
             (fstg2, wqnt, favq2, fstg2, fb0, fb1, fsg2, fm2, fs2, fr2))
    hidqT = sbf("hidqT", [128, C22 * 128], BF16)
    fout = sbf("fout", [128, D])

    def wgT_v(j):
        return wgT[:].rearrange("p (j n) -> p j n", j=C8)[:, j, :]

    def wuT_v(j):
        return wuT[:].rearrange("p (j n) -> p j n", j=C8)[:, j, :]

    def wdT_v(r):
        return wdT[:].rearrange("p (r n) -> p r n", r=C22)[:, r, :]

    def hidqT_v(r):
        return hidqT[:].rearrange("p (r n) -> p r n", r=C22)[:, r, :]

    def fquant(src_ap, dst_bf, width, dep, si=0):
        (_fstg, _wqnt, _favq, _faq, _fb0, _fb1, _fsg, _fm, _fs,
         _fr) = fsets[si]
        return emit_quant(src_ap, dst_bf, width, dep, _favq[:, 0:width],
                          _faq[:, 0:width], _fb0[:, 0:width],
                          _fb1[:, 0:width], _fsg[:, 0:width],
                          _favq[:, 0:width], _fm[:, 0:width // 16],
                          _fs[:, 0:width // 16], _fr[:, 0:width // 16])

    wg_ready = []
    prev_ps_t_f = hq_done[-1] if hq_done else ld_ident
    wpar = [0]
    wchain = [None, None]

    def prep_w(src_d, nrow, dstT, jdim, col_splits, deps0):
        nonlocal prev_ps_t_f
        toks = []
        for r in range(nrow):
            for (c0, c1, jg0) in col_splits:
                wd_ = c1 - c0
                p = wpar[0] & 1
                wpar[0] += 1
                stg_p = fsets[p][0]
                qnt_p = fsets[p][1]
                l = sc.op("DF%d" % p, lambda e, r=r, c0=c0, c1=c1, stg_p=stg_p:
                          e.dma_start(
                    out=stg_p[:, 0:c1 - c0],
                    in_=src_d[128 * r:128 * (r + 1), c0:c1]), dma=True,
                    after=[wchain[p]] if wchain[p] else deps0)
                qd, _sg = fquant(stg_p[:, 0:wd_], qnt_p[:, 0:wd_], wd_, [l],
                                 si=p)
                nblk = wd_ // 128
                sub = []
                for bi in range(nblk):
                    sub.append(sc.op("T", lambda e, bi=bi, qnt_p=qnt_p:
                                     e.transpose(
                        ps_t[:, 128 * bi:128 * (bi + 1)],
                        qnt_p[:, 128 * bi:128 * (bi + 1)], ident[:]),
                        after=[qd, prev_ps_t_f]))
                ev = sc.op("V", lambda e, r=r, jg0=jg0, nblk=nblk, dstT=dstT,
                           jdim=jdim: e.tensor_copy(
                    out=dstT[:].rearrange("p (j n) -> p j n", j=jdim)
                    [:, jg0:jg0 + nblk, 128 * r:128 * (r + 1)],
                    in_=ps_t[:, 0:128 * nblk].rearrange(
                        "p (b n) -> p b n", b=nblk)), after=sub)
                wchain[p] = qd
                prev_ps_t_f = ev
                toks.append(ev)
        return toks

    if phase == "full":
        wg_ready += prep_w(wg_d, C22, wgT, C8, [(0, 1024, 0)], attn_exit)
        wg_ready += prep_w(wu_d, C22, wuT, C8, [(0, 1024, 0)], [])
        wg_ready += prep_w(wd_d, C8, wdT, C22,
                           [(0, 1024, 0), (1024, 2048, 8), (2048, 2816, 16)], [])

    out_stores = []
    f2_hist = {}
    f1_hist = {}
    ph3_hist = {}
    prev_h3 = None
    for t in range(T if phase == "full" else 0):
        _CACHE["marks"].append(("ffn", t, sc.gidx))
        ga_bank = ps_a if t % 2 == 0 else ps_b
        up_bank = ps_b if t % 2 == 0 else ps_a
        gdep = [hq_done[t], wg_ready[-1]]
        if t - 1 in ph3_hist:
            gdep.append(ph3_hist[t - 1])
        if t - 2 in f1_hist:
            gdep.append(f1_hist[t - 2])
        udep = [hq_done[t]]
        if t - 1 in f1_hist:
            udep.append(f1_hist[t - 1])
        xslot = xpt[:]
        xl = sc.op("DL", lambda e, t=t, xslot=xslot: e.dma_start(
            out=xslot, in_=xprime_d[128 * t:128 * (t + 1), :]), dma=True,
            after=[xp_stores[t]] + ([f2_hist[t - 1]] if t >= 1
                                    else [wg_ready[-1]]))
        hid_ready = []
        for half in range(2):
            mmg = []
            for j in range(C8):
                st = (j == 0)
                sp = (j == C8 - 1)
                for (c0, c1) in ((0, 512), (512, 1024), (1024, 1408)):
                    off = 1408 * half
                    mmg.append(sc.op("T", lambda e, j=j, st=st, sp=sp, c0=c0,
                                     c1=c1, off=off, t=t, bk=ga_bank: e.matmul(
                        bk[:, c0:c1], hqT_v(j, t),
                        wgT_v(j)[:, off + c0:off + c1], start=st, stop=sp),
                        after=gdep if half == 0 else [prev_h3]))
                    mmg.append(sc.op("T", lambda e, j=j, st=st, sp=sp, c0=c0,
                                     c1=c1, off=off, t=t, bk=up_bank: e.matmul(
                        bk[:, c0:c1], hqT_v(j, t),
                        wuT_v(j)[:, off + c0:off + c1], start=st, stop=sp),
                        after=udep if half == 0 else [prev_h3]))
            gu_done = mmg[-2:]
            for sl_i, (c0, c1) in enumerate(((0, 512), (512, 1024),
                                             (1024, 1408))):
                wd_ = c1 - c0
                slv = sc.op("S", lambda e, c0=c0, c1=c1, bk=ga_bank:
                            e.activation(
                    out=fstg[:, 0:c1 - c0], in_=bk[:, c0:c1], func=ACT.Silu),
                    after=gu_done + ([prev_h3] if prev_h3 else []))
                h3v = sc.op("V", lambda e, c0=c0, c1=c1, wd_=wd_, bk=up_bank:
                            e.tensor_tensor(
                    out=fh3[:, 0:wd_], in0=fstg[:, 0:wd_], in1=bk[:, c0:c1],
                    op=ALU.mult), after=[slv])
                hqd, _sg2 = fquant(fh3[:, 0:wd_], wqnt[:, 0:wd_], wd_, [h3v])
                prev_h3 = hqd
                nblk = wd_ // 128
                sub = []
                for bi in range(nblk):
                    sub.append(sc.op("T", lambda e, bi=bi: e.transpose(
                        ps_t[:, 128 * bi:128 * (bi + 1)],
                        wqnt[:, 128 * bi:128 * (bi + 1)], ident[:]),
                        after=[hqd, prev_ps_t_f]))
                base_r = half * 11 + (0, 4, 8)[sl_i]
                hv = sc.op("V", lambda e, base_r=base_r, nblk=nblk: e.tensor_copy(
                    out=hidqT[:].rearrange("p (r n) -> p r n", r=C22)
                    [:, base_r:base_r + nblk, :],
                    in_=ps_t[:, 0:128 * nblk].rearrange("p (b n) -> p b n",
                                                        b=nblk)), after=sub)
                prev_ps_t_f = hv
                hid_ready.append(hv)
        mmd = []
        for r in range(C22):
            st = (r == 0)
            sp = (r == C22 - 1)
            mmd.append(sc.op("T", lambda e, r=r, st=st, sp=sp, bk=ga_bank:
                             e.matmul(
                bk[:, 0:512], hidqT_v(r), wdT_v(r)[:, 0:512], start=st,
                stop=sp), after=hid_ready))
            mmd.append(sc.op("T", lambda e, r=r, st=st, sp=sp, bk=ga_bank:
                             e.matmul(
                bk[:, 512:1024], hidqT_v(r), wdT_v(r)[:, 512:1024],
                start=st, stop=sp), after=hid_ready))
        dn_done = mmd[-2:]
        ph3_hist[t] = prev_h3
        f1 = sc.op("V", lambda e, bk=ga_bank: e.tensor_tensor(
            out=fout[:], in0=bk[:, 0:1024], in1=gbb[:], op=ALU.mult),
            after=dn_done + bc_gb + out_stores[-1:])
        f2 = sc.op("V", lambda e, xslot=xslot: e.tensor_tensor(
            out=fout[:], in0=fout[:], in1=xslot, op=ALU.add),
            after=[f1, xl])
        ost = sc.op("DS", lambda e, t=t: e.dma_start(
            out=out_d[128 * t:128 * (t + 1), :], in_=fout[:]), dma=True,
            after=[f2])
        out_stores.append(ost)
        f2_hist[t] = f2
        f1_hist[t] = f1

    ffn.close()

    if KCUT:
        sc.cut(KCUT)
    sc.finalize()
    sems = {}
    for e in ENGS:
        sems[e] = stack.enter_context(nc.semaphore("sem_" + e))
    n_dma = {c: sum(16 for r in sc.ops[c] if r["inc"] and r["dma"])
             for c in DMA_CLASSES}
    with nc.Block() as block:
        @block.tensor
        def _(eng):
            sc.replay(eng, "T", sems)

        @block.vector
        def _(eng):
            sc.replay(eng, "V", sems)

        @block.scalar
        def _(eng):
            sc.replay(eng, "S", sems)

        @block.gpsimd
        def _(eng):
            sc.replay(eng, "G", sems)

        @block.sync
        def _(eng):
            sc.replay_merged(eng, list(DMA_CLASSES), sems)
            for c in DMA_CLASSES:
                if n_dma[c]:
                    eng.wait_ge(sems[c], n_dma[c])

    stack.close()
    _CACHE[key] = nc
    return nc


def kernel(L0_attn_x, L0_attn_Wq, L0_attn_Wk, L0_attn_Wv, L0_attn_Wo,
           L0_attn_q_gain, L0_attn_pair_mix, L0_ffn_w_gate, L0_ffn_w_up,
           L0_ffn_w_down, L0_c, L0_wmod):
    f = np.float32
    x = np.ascontiguousarray(np.asarray(L0_attn_x, f))
    wqT = np.ascontiguousarray(np.asarray(L0_attn_Wq, f).T)
    wkT = np.ascontiguousarray(np.asarray(L0_attn_Wk, f).T)
    wvT = np.ascontiguousarray(np.asarray(L0_attn_Wv, f).T)
    woT = np.ascontiguousarray(np.asarray(L0_attn_Wo, f).T)
    perm = np.arange(D).reshape(C8, 2, 64)[:, ::-1, :].reshape(D)
    woTsw = np.ascontiguousarray(woT[perm])
    wmod = np.ascontiguousarray(np.asarray(L0_wmod, f))
    wg = np.ascontiguousarray(np.asarray(L0_ffn_w_gate, f))
    wu = np.ascontiguousarray(np.asarray(L0_ffn_w_up, f))
    wd = np.ascontiguousarray(np.asarray(L0_ffn_w_down, f))
    cfull = np.asarray(L0_c, f)
    gain = np.ascontiguousarray(np.asarray(L0_attn_q_gain, f).reshape(1, H))
    pm = np.ascontiguousarray(np.asarray(L0_attn_pair_mix, f).reshape(1, 32))
    maskb = _mask_host()
    ctbl, stbl = _rope_tables_host()
    ident = np.eye(128, dtype=ml_dtypes.bfloat16)

    nc = build()
    shared = {"wqT": wqT, "wkT": wkT, "wvT": wvT, "woT": woT,
              "woTsw": woTsw, "wmod": wmod, "wg": wg, "wu": wu, "wd": wd,
              "gain": gain, "pm": pm, "maskb": maskb, "ident": ident,
              "ctbl": ctbl, "stbl": stbl}
    in_maps = []
    for b in range(NCORE):
        m = dict(shared)
        m["x"] = np.ascontiguousarray(x[b])
        m["crow"] = np.ascontiguousarray(cfull[b:b + 1])
        in_maps.append(m)
    res = run_bass_kernel_spmd(nc, in_maps, core_ids=list(range(NCORE)))
    out = np.stack([np.asarray(r["out"]) for r in res.results], axis=0)
    return out.astype(np.float32)
